# version 04
# baseline (speedup 1.0000x reference)
import numpy as np
import ml_dtypes
import concourse.bass as bass
import concourse.mybir as mybir
from concourse.bass import AP
from concourse.bass_utils import run_bass_kernel_spmd

F32 = mybir.dt.float32
BF16 = mybir.dt.bfloat16
I32 = mybir.dt.int32
AF = mybir.ActivationFunctionType
ALU = mybir.AluOpType
AX = mybir.AxisListType
BFNP = ml_dtypes.bfloat16

NCORES = 8
B_ALL, T, D = 16, 2048, 1024
SEQ_PER_CORE = 2
RMAIN = SEQ_PER_CORE * T
PT_ = 16
RPREF = B_ALL * PT_
NEG = -1e9
RND_C = np.float32(1.5 * 2.0 ** 23)

HID = 2730
GLU_BLOCKS = []
for j in range(0, HID, 128):
    w = min(128, HID - j)
    GLU_BLOCKS.append(("a", j, j + w))
    GLU_BLOCKS.append(("g", j, j + w))
H_CHUNKS = [(j, min(128, HID - j)) for j in range(0, HID, 128)]


class Buf:
    __slots__ = ("writers", "readers", "name", "serial")

    def __init__(self, name="", serial=False):
        self.writers = []
        self.readers = []
        self.name = name
        self.serial = serial


class Prog:
    COMPUTE = {"tensor": "cT", "vector": "cV", "scalar": "cS", "gpsimd": "cG"}
    QUEUE = {"sync": "dS", "gpsimd": "dG", "scalar": "dA", "vector": "dV"}
    NQSEM = 8

    def __init__(self, nc):
        self.nc = nc
        self.items = {e: [] for e in ("sync", "tensor", "vector", "scalar", "gpsimd")}
        self.sem = {}
        self.count = {}
        for s in ("cT", "cV", "cS", "cG"):
            self.sem[s] = nc.alloc_semaphore(s)
            self.count[s] = 0
        for q in ("dS", "dG"):
            for k in range(self.NQSEM):
                s = f"{q}{k}"
                self.sem[s] = nc.alloc_semaphore(s)
                self.count[s] = 0
        self.qidx = {"dS": 0, "dG": 0}
        self.last_wait = {}
        self.opord = {e: 0 for e in self.COMPUTE}
        self.n_selfwaits = 0

    def _wait(self, eng, tok, raw=False):
        if tok is None:
            return
        s, v = tok[0], tok[1]
        if self.COMPUTE.get(eng) == s:
            if raw and len(tok) > 2 and tok[2] >= self.opord[eng] - 1:
                key = (eng, s)
                if self.last_wait.get(key, 0) >= v:
                    return
                self.last_wait[key] = v
                self.items[eng].append(("w", s, v))
                self.n_selfwaits += 1
            return
        key = (eng, s)
        if self.last_wait.get(key, 0) >= v:
            return
        self.last_wait[key] = v
        self.items[eng].append(("w", s, v))

    def _deps(self, eng, reads, writes, excl):
        for b in reads:
            for t in b.writers:
                self._wait(eng, t, raw=True)
            if b.serial:
                for t in b.readers:
                    self._wait(eng, t)
        for b in writes:
            for t in b.writers:
                self._wait(eng, t)
            for t in b.readers:
                self._wait(eng, t)

    def _record(self, tok, reads, writes, excl):
        for b in reads:
            b.readers.append(tok)
        for b in writes:
            if excl:
                b.writers = [tok]
                b.readers = []
            else:
                b.writers.append(tok)

    def op(self, eng, fn, reads=(), writes=(), excl=True):
        self._deps(eng, reads, writes, excl)
        s = self.COMPUTE[eng]
        self.count[s] += 1
        tok = (s, self.count[s], self.opord[eng])
        self.opord[eng] += 1
        self.items[eng].append(("c", fn))
        self._record(tok, reads, writes, excl)
        return tok

    def dma(self, qeng, out_ap, in_ap, reads=(), writes=(), excl=True):
        self._deps(qeng, reads, writes, excl)
        q = self.QUEUE[qeng]
        s = f"{q}{self.qidx[q] % self.NQSEM}"
        self.qidx[q] += 1
        self._wait(qeng, (s, self.count[s]))
        self.count[s] += 16
        tok = (s, self.count[s])
        self.items[qeng].append(("d", out_ap, in_ap, s))
        self._record(tok, reads, writes, excl)
        return tok

    def barrier_token(self, eng):
        s = self.COMPUTE.get(eng) or self.QUEUE[eng]
        return (s, self.count[s])

    def wait_all(self, eng, excludes=()):
        for s, v in list(self.count.items()):
            if v and s not in excludes and (self.COMPUTE.get(eng) != s):
                self._wait(eng, (s, v))

    def finalize(self):
        nc = self.nc
        sem = self.sem

        def runner(items):
            def f(eng):
                for it in items:
                    if it[0] == "w":
                        eng.wait_ge(sem[it[1]], it[2])
                    elif it[0] == "c":
                        ins = it[1](eng)
                        ins.then_inc(sem[self.COMPUTE[engname(eng)]], 1)
                    else:
                        ins = eng.dma_start(out=it[1], in_=it[2])
                        ins.then_inc(sem[it[3]], 16)
            return f

        def engname(eng):
            et = eng.engine
            return {
                mybir.EngineType.PE: "tensor",
                mybir.EngineType.DVE: "vector",
                mybir.EngineType.Activation: "scalar",
                mybir.EngineType.Pool: "gpsimd",
                mybir.EngineType.SP: "sync",
            }[et]

        with nc.Block() as block:
            block.sync(runner(self.items["sync"]))
            block.tensor(runner(self.items["tensor"]))
            block.vector(runner(self.items["vector"]))
            block.scalar(runner(self.items["scalar"]))
            block.gpsimd(runner(self.items["gpsimd"]))


class TV:

    __slots__ = ("t", "c0", "cols", "buf", "_rsv")

    def __init__(self, t, c0, cols, name):
        self.t = t
        self.c0 = c0
        self.cols = cols
        self.buf = Buf(name)

    def v(self, rs=None, cs=None):
        r = rs if rs is not None else slice(0, 128)
        if cs is None:
            c = slice(self.c0, self.c0 + self.cols)
        else:
            c = slice(self.c0 + cs.start, self.c0 + cs.stop)
        return self.t[r, c]

    def bcast(self, groups, rep, rows=128):
        pitch = self.t.shape[1]
        return AP(self.t[:].tensor, self.c0, [[pitch, rows], [1, groups], [0, rep]])


class Arena:

    def __init__(self, nc, name, cols, dtype):
        self.t = nc.alloc_sbuf_tensor(name, [128, cols], dtype)
        self.cols = cols
        self.free = [(0, cols)]
        self.peak = 0
        self.name = name
        self.ghosts = []

    def alloc(self, cols, name=""):
        req = cols
        cols = (cols + 15) & ~15
        for i, (o, w) in enumerate(self.free):
            if w >= cols:
                tv = TV(self.t, o, req, name)
                tv._rsv = cols
                if w == cols:
                    self.free.pop(i)
                else:
                    self.free[i] = (o + cols, w - cols)
                self.peak = max(self.peak, self.cols - sum(x[1] for x in self.free))
                keep = []
                toks = {}
                for go, gw, gt in self.ghosts:
                    if go < o + cols and o < go + gw:
                        for t in gt:
                            k = t[0]
                            if k not in toks or toks[k][1] < t[1]:
                                toks[k] = t
                        if go < o:
                            keep.append((go, o - go, gt))
                        if go + gw > o + cols:
                            keep.append((o + cols, go + gw - (o + cols), gt))
                    else:
                        keep.append((go, gw, gt))
                self.ghosts = keep
                tv.buf.writers = list(toks.values())
                return tv
        raise AssertionError(
            f"{self.name} arena OOM: want {cols} cols for {name!r}; free={self.free}")

    def alloc_list(self, n, cols, name=""):
        return [self.alloc(cols, f"{name}{i}") for i in range(n)]

    def free_tv(self, *tvs):
        for tv in tvs:
            if tv is None:
                continue
            if isinstance(tv, (list, tuple)):
                self.free_tv(*tv)
                continue
            w_ = getattr(tv, "_rsv", (tv.cols + 15) & ~15)
            self.free.append((tv.c0, w_))
            toks = list(tv.buf.writers) + list(tv.buf.readers)
            if toks:
                self.ghosts.append((tv.c0, w_, toks))
        self.free.sort()
        merged = []
        for o, w in self.free:
            if merged and merged[-1][0] + merged[-1][1] == o:
                merged[-1] = (merged[-1][0], merged[-1][1] + w)
            else:
                merged.append((o, w))
        self.free = merged


def _host_consts():
    c = {}
    ii, jj = np.arange(128)[:, None], np.arange(128)[None, :]
    c["ident"] = np.eye(128, dtype=BFNP)
    c["maskLT"] = np.where(jj <= ii, 0.0, NEG).astype(np.float32)
    c["maskUT"] = np.where(jj >= ii, 0.0, NEG).astype(np.float32)
    same16 = (ii // PT_) == (jj // PT_)
    c["maskPD"] = np.where(same16 & (jj <= ii), 0.0, NEG).astype(np.float32)
    c["maskPF"] = np.where(same16, 0.0, NEG).astype(np.float32)
    c["ones_k"] = np.ones((128, 1), dtype=BFNP)
    c["ones_m"] = np.ones((1, 128), dtype=BFNP)
    c["ones_mf"] = np.ones((1, 128), dtype=np.float32)
    c["identf"] = np.eye(128, dtype=np.float32)

    RD_ = 32
    inv_freq = 1.0 / 10000.0 ** (np.arange(0, RD_, 2, dtype=np.float32) / RD_)
    t_ = np.arange(T, dtype=np.float32)
    emb = np.concatenate([np.outer(t_, inv_freq)] * 2, -1)
    cosT, sinT = np.cos(emb).astype(np.float32), np.sin(emb).astype(np.float32)
    c["ropeCos"] = np.tile(cosT.T, (4, 1)).astype(BFNP)
    c["ropeSin"] = np.tile(sinT.T, (4, 1)).astype(BFNP)
    ppos = np.tile(np.arange(PT_), B_ALL)
    c["ropeCosP"] = np.tile(cosT[ppos].T, (4, 1)).astype(BFNP)
    c["ropeSinP"] = np.tile(sinT[ppos].T, (4, 1)).astype(BFNP)
    P32 = np.zeros((32, 32), np.float32)
    for i in range(16):
        P32[i, 16 + i] = -1.0
        P32[16 + i, i] = 1.0
    R = np.zeros((128, 128), np.float32)
    for b in range(4):
        R[b * 32:(b + 1) * 32, b * 32:(b + 1) * 32] = P32
    c["Rrot"] = R.T.astype(BFNP)

    DH, WIN = 64, 512
    half = np.arange(0, DH, 2, dtype=np.float32)
    invf = 1.0 / 10000.0 ** (half / DH)
    fr = np.concatenate([t_[:, None] * invf[None, :]] * 2, -1)
    base = (half + 0.4 * DH) / (1.4 * DH)
    power = (t_ - T // 2) / float(WIN // 2)
    scl = np.concatenate([base[None, :] ** power[:, None]] * 2, -1).astype(np.float32)
    xc, xs = np.cos(fr).astype(np.float32), np.sin(fr).astype(np.float32)
    sgn = np.concatenate([-np.ones(32, np.float32), np.ones(32, np.float32)])
    tabs = dict(xcs=xc * scl, xss=xs * scl * sgn, xci=xc / scl, xsi=xs / scl * sgn)
    for k, v in tabs.items():
        c[k] = np.tile(v, (1, 8)).astype(BFNP)
        c[k + "p"] = np.tile(v[ppos], (1, 8)).astype(BFNP)
    return c


def _colpack(vec, blocks=None):
    if blocks is None:
        assert len(vec) % 128 == 0
        blocks = [(j, 128) for j in range(0, len(vec), 128)]
    out = np.zeros((128, len(blocks)), np.float32)
    for j, (lo, w) in enumerate(blocks):
        out[:w, j] = vec[lo:lo + w]
    return out


def _rb(vec):
    return np.ascontiguousarray(np.broadcast_to(np.asarray(vec, np.float32), (128, len(vec))))


def _stage_inputs(inputs):
    sh = {}
    f = lambda k: np.asarray(inputs[k], np.float32)
    for k in ("W_kv", "W_q", "W_krope", "W_kdec", "W_vdec", "W_qdec", "W_qrope", "W_o"):
        sh[k] = np.ascontiguousarray(f("L0_attn_" + k))
    sh["g_kv_rb"] = _rb(f("L0_attn_g_kv"))
    w1 = f("L0_ffn_w1")
    b1 = f("L0_ffn_b1")
    colperm = np.concatenate(
        [np.r_[lo:hi] + (HID if kind == "g" else 0) for kind, lo, hi in GLU_BLOCKS])
    for l in range(2):
        sh[f"wqkv{l}"] = np.ascontiguousarray(f("L0_ffn_wqkv")[l])
        sh[f"wo{l}"] = np.ascontiguousarray(f("L0_ffn_wo")[l])
        sh[f"w1p{l}"] = np.ascontiguousarray(w1[l][:, colperm])
        blocks1 = [(j * 128, GLU_BLOCKS[j][2] - GLU_BLOCKS[j][1]) for j in range(len(GLU_BLOCKS))]
        b1perm = b1[l][colperm]
        sh[f"b1cp{l}"] = _colpack(b1perm, [(sum(w for _, w in blocks1[:j]), blocks1[j][1])
                                           for j in range(len(blocks1))])
        sh[f"w2_{l}"] = np.ascontiguousarray(f("L0_ffn_w2")[l])
        sh[f"b2cp{l}"] = _colpack(f("L0_ffn_b2")[l])
        sh[f"lngcp{l}"] = _colpack(f("L0_ffn_ln_g")[l])
        sh[f"lnbcp{l}"] = _colpack(f("L0_ffn_ln_b")[l])
        sh[f"qs_rb{l}"] = _rb(np.tile(f("L0_ffn_q_scale")[l], 8))
        sh[f"ks_rb{l}"] = _rb(np.tile(f("L0_ffn_k_scale")[l], 8))
    for k in ("qw", "kw", "vw", "ow", "lqw", "lkw", "lvw"):
        sh[k + "T"] = np.ascontiguousarray(f("L1_attn_" + k).T)
    sh["qb_rb"] = _rb(f("L1_attn_qb"))
    sh["vb_rb"] = _rb(f("L1_attn_vb"))
    sh["ob_rb"] = _rb(f("L1_attn_ob"))
    sh["lna_rb"] = _rb(f("L1_attn_lna_w"))
    sh["lnb_rb"] = _rb(f("L1_attn_lnb_w"))
    sh["lnc_rb"] = _rb(f("L1_attn_lnc_w"))
    sh["lnd_rb"] = _rb(f("L1_attn_lnd_w"))
    sh["lqb_rb"] = _rb(f("L1_attn_lqb"))
    sh["lkb_rb"] = _rb(f("L1_attn_lkb"))
    sh["lvb_rb"] = _rb(f("L1_attn_lvb"))
    sh["w1fT"] = np.ascontiguousarray(f("L1_ffn_w1").T)
    sh["w2fT"] = np.ascontiguousarray(f("L1_ffn_w2").T)
    sh["ln1g_rb"] = _rb(f("L1_ffn_ln1_g"))
    sh["ln1b_rb"] = _rb(f("L1_ffn_ln1_b"))
    sh["ln2g_rb"] = _rb(f("L1_ffn_ln2_g"))
    sh["ln2b_rb"] = _rb(f("L1_ffn_ln2_b"))
    sh["b1f_rb"] = _rb(f("L1_ffn_b1"))
    sh["b2f_rb"] = _rb(f("L1_ffn_b2"))
    return sh


def kernel(**inputs):
    x = np.asarray(inputs["L0_attn_x"], np.float32)
    shared = _stage_inputs(inputs)
    consts = _host_consts()

    nc, input_names = build_program()

    in_maps = []
    for core in range(NCORES):
        own = [2 * core, 2 * core + 1]
        others = [b for b in range(B_ALL) if b not in own]
        perm = [own[0], others[0], own[1]] + others[1:]
        m = dict(shared)
        m.update(consts)
        m["x"] = np.ascontiguousarray(x[own].reshape(RMAIN, D))
        m["px"] = np.ascontiguousarray(x[perm, :PT_, :].reshape(RPREF, D))
        in_maps.append({k: m[k] for k in input_names})

    cores = DBG_CORES if DBG_CORES else list(range(NCORES))
    res = run_bass_kernel_spmd(nc, [in_maps[c] for c in cores], core_ids=cores)
    global LAST_RES
    if DBG:
        LAST_RES = res
    out = np.zeros((B_ALL, T, D), np.float32)
    for i, core in enumerate(cores):
        out[2 * core:2 * core + 2] = res.results[i]["out"].reshape(SEQ_PER_CORE, T, D)
    return out


DBG = False


def _dbg_on(name):
    return DBG is True or (isinstance(DBG, (list, tuple)) and name in DBG)
ATTN_STAGE = 9
DBG_CORES = None
MAX_PHASE = 99
LAST_RES = None
SCALE_MLA = np.float32(1.0 / np.sqrt(np.float32(96.0)))
SCALE_LOC = np.float32(8.0)
SCALE_L1 = np.float32(1.0 / np.sqrt(np.float32(64.0)))


class Ctx:
    pass


def _mk_dram(C):
    nc = C.nc
    names = {}
    def di(name, shape, dt=F32):
        names[name] = True
        C.dram[name] = nc.dram_tensor(name, list(shape), dt, kind="ExternalInput")
    di("x", (RMAIN, D)); di("px", (RPREF, D))
    for k in ("W_kv",): di(k, (1024, 256))
    di("W_q", (1024, 512)); di("W_krope", (1024, 512))
    di("W_kdec", (256, 1024)); di("W_vdec", (256, 1024))
    di("W_qdec", (512, 1024)); di("W_qrope", (512, 512)); di("W_o", (1024, 1024))
    di("g_kv_rb", (128, 256))
    for l in range(2):
        di(f"wqkv{l}", (1024, 1536)); di(f"wo{l}", (512, 1024))
        di(f"w1p{l}", (1024, 2 * HID)); di(f"b1cp{l}", (128, len(GLU_BLOCKS)))
        di(f"w2_{l}", (HID, 1024)); di(f"b2cp{l}", (128, 8))
        di(f"lngcp{l}", (128, 8)); di(f"lnbcp{l}", (128, 8))
        di(f"qs_rb{l}", (128, 512)); di(f"ks_rb{l}", (128, 512))
    for k in ("qwT", "kwT", "vwT", "owT"): di(k, (1024, 1024))
    for k in ("lqwT", "lkwT", "lvwT"): di(k, (64, 64))
    for k in ("qb_rb", "vb_rb", "ob_rb", "lna_rb", "lnb_rb",
              "ln1g_rb", "ln1b_rb", "b2f_rb"): di(k, (128, 1024))
    for k in ("lnc_rb", "lnd_rb", "lqb_rb", "lkb_rb", "lvb_rb"): di(k, (128, 64))
    di("w1fT", (1024, 4096)); di("w2fT", (4096, 1024))
    for k in ("ln2g_rb", "ln2b_rb", "b1f_rb"): di(k, (128, 4096))
    di("ident", (128, 128), BF16)
    for k in ("maskLT", "maskUT", "maskPD", "maskPF"): di(k, (128, 128))
    di("ones_k", (128, 1), BF16); di("ones_m", (1, 128), BF16)
    di("ones_mf", (1, 128), F32); di("identf", (128, 128), F32)
    di("ropeCos", (128, T), BF16); di("ropeSin", (128, T), BF16)
    di("ropeCosP", (128, RPREF), BF16); di("ropeSinP", (128, RPREF), BF16)
    di("Rrot", (128, 128), BF16)
    for k in ("xcs", "xss", "xci", "xsi"):
        di(k, (T, 512), BF16); di(k + "p", (RPREF, 512), BF16)
    C.input_names = list(names)

    def ds(name, shape, dt=F32, kind=None):
        if kind is None:
            expose = (DBG is True) or (DBG and name in DBG)
            kind = "ExternalOutput" if expose else "Internal"
        C.dram[name] = nc.dram_tensor(name, list(shape), dt, kind=kind)
    ds("XN", (RMAIN // 128, 128, 8, 128), BF16)
    ds("XNT", (RMAIN // 128, 128, 1024), BF16)
    ds("STA", (2, 8, 16, 128, 16), F32)
    ds("QCD", (8, 128, 2048), BF16)
    ds("PSD", (128, 512), F32)
    ds("WQD", (4, 128, 512), BF16)
    ds("QCD2", (128, 512), BF16)
    ds("PBD", (128, 2048), BF16)
    ds("OTMD", (16, 128, 1024), BF16)
    ds("STT", (RMAIN // 128, 128, 32), F32)
    ds("PXN", (RPREF // 128, 128, 8, 128), BF16)
    ds("Y", (SEQ_PER_CORE, 8, 128, T + 1))
    ds("PY", (1, 8, 128, RPREF + 1))
    ds("AO", (RMAIN, D)); ds("PAO", (RPREF, D))
    ds("X1", (RMAIN, D)); ds("PX1", (RPREF, D))
    ds("XQ2", (RMAIN // 128, 128, 4096), BF16)
    ds("W1C0", (len(GLU_BLOCKS), 128, 1024), BF16)
    ds("W1C1", (len(GLU_BLOCKS), 128, 1024), BF16)
    ds("out", (RMAIN, D), F32, "ExternalOutput")


def _db(C, name):
    if name not in C.dbuf:
        C.dbuf[name] = Buf(name)
    return C.dbuf[name]


def loadw(C, dname, K, M, name=None, cs=None, dt_cols=None):
    d = C.dram[dname]
    nk = (K + 127) // 128
    tvs = []
    for k in range(nk):
        r = min(128, K - k * 128)
        w = M if cs is None else (cs.stop - cs.start)
        tv = C.ba.alloc(w, f"{name or dname}k{k}")
        src = d.ap()[k * 128:k * 128 + r, :] if cs is None else \
            d.ap()[k * 128:k * 128 + r, cs.start:cs.stop]
        C.P.dma("gpsimd", tv.v(rs=slice(0, r)), src,
                reads=[_db(C, dname)], writes=[tv.buf])
        tvs.append(tv)
    return tvs


def load_const(C, dname, cols, dtype_arena="ba", rows=128):
    a = getattr(C, dtype_arena)
    tv = a.alloc(cols, dname)
    C.P.dma("gpsimd", tv.v(rs=slice(0, rows)), C.dram[dname].ap()[0:rows, :],
            reads=[_db(C, dname)], writes=[tv.buf])
    return tv


def ln_stats(C, st, c_s1, c_s2, n, eps, c_m, c_rstd):
    P = C.P
    sc = lambda c: st.v(cs=slice(c, c + 1))
    inv_n = float(1.0 / n)
    e2, mm, var, sd = 8, 9, 10, 11
    P.op("vector", lambda v: v.tensor_single_scalar(out=sc(c_m), in_=sc(c_s1),
         scalar=inv_n, op=ALU.mult), reads=[st.buf], writes=[st.buf], excl=False)
    P.op("vector", lambda v: v.tensor_single_scalar(out=sc(e2), in_=sc(c_s2),
         scalar=inv_n, op=ALU.mult), reads=[st.buf], writes=[st.buf], excl=False)
    P.op("vector", lambda v: v.tensor_tensor(out=sc(mm), in0=sc(c_m),
         in1=sc(c_m), op=ALU.mult), reads=[st.buf], writes=[st.buf], excl=False)
    P.op("vector", lambda v: v.tensor_tensor(out=sc(var), in0=sc(e2),
         in1=sc(mm), op=ALU.subtract), reads=[st.buf], writes=[st.buf], excl=False)
    P.op("vector", lambda v: v.tensor_single_scalar(out=sc(var), in_=sc(var),
         scalar=float(eps), op=ALU.add), reads=[st.buf], writes=[st.buf], excl=False)
    P.op("scalar", lambda s: s.activation(out=sc(sd), in_=sc(var),
         func=AF.Sqrt), reads=[st.buf], writes=[st.buf], excl=False)
    P.op("vector", lambda v: v.reciprocal(out=sc(c_rstd), in_=sc(sd)),
         reads=[st.buf], writes=[st.buf], excl=False)


def emit_ln0(C, src_name, R, xn_name):
    P, ba, fa = C.P, C.ba, C.fa
    nt = R // 128
    xin = [fa.alloc(1024, "ln0x") for _ in range(2)]
    junk = ba.alloc(1024, "ln0junk")
    xnt = ba.alloc(1024, "ln0xn")
    stage = [ba.alloc(1024, "ln0stage") for _ in range(2)]
    st = fa.alloc(32, "ln0st")
    src, xnd = C.dram[src_name], C.dram[xn_name]
    for i in range(nt):
        xi, sg = xin[i % 2], stage[i % 2]
        P.dma("gpsimd", xi.v(), src.ap()[i * 128:(i + 1) * 128, :],
              reads=[_db(C, src_name)], writes=[xi.buf])
        P.op("vector", lambda v, xi=xi: v.reduce_sum(
            out=st.v(cs=slice(0, 1)), in_=xi.v(), axis=AX.X),
            reads=[xi.buf], writes=[st.buf], excl=False)
        P.op("scalar", lambda s, xi=xi: s.activation(
            out=junk.v(), in_=xi.v(), func=AF.Square,
            accum_out=st.v(cs=slice(1, 2))),
            reads=[xi.buf], writes=[st.buf, junk.buf], excl=False)
        ln_stats(C, st, 0, 1, 1024, 1e-5, 2, 3)
        P.op("vector", lambda v, xi=xi: v.tensor_scalar(
            out=xnt.v(), in0=xi.v(), scalar1=st.v(cs=slice(2, 3)),
            scalar2=st.v(cs=slice(3, 4)), op0=ALU.subtract, op1=ALU.mult),
            reads=[xi.buf, st.buf], writes=[xnt.buf])
        if _dbg_on("XNT"):
            P.dma("sync", C.dram["XNT"].ap()[i], xnt.v(),
                  reads=[xnt.buf], writes=[_db(C, "XNT")], excl=False)
        if _dbg_on("STT"):
            P.dma("sync", C.dram["STT"].ap()[i], st.v(),
                  reads=[st.buf], writes=[_db(C, "STT")], excl=False)
        for c in range(8):
            P.op("tensor", lambda t, c=c: t.transpose(
                C.psT.v(cs=slice(c * 128, (c + 1) * 128)),
                xnt.v(cs=slice(c * 128, (c + 1) * 128)), C.ident.v()),
                reads=[xnt.buf, C.ident.buf], writes=[C.psT.buf], excl=(c == 0))
        P.op("scalar", lambda s, sg=sg: s.copy(out=sg.v(), in_=C.psT.v()),
             reads=[C.psT.buf], writes=[sg.buf])
        P.dma("sync", xnd.ap()[i],
              sg.v().rearrange("p (c j) -> p c j", c=8),
              reads=[sg.buf], writes=[_db(C, xn_name)], excl=False)
    ba.free_tv(junk, xnt, stage)
    fa.free_tv(xin, st)


class SeqCtx:

    def __init__(self, **kw):
        self.__dict__.update(kw)
        self.nt = self.Tn // 128
        self.nch = min(512, self.Tn)


def _xn_sweep(C, sq, body):
    P, ba = C.P, C.ba
    nch = sq.nch
    nstg = 2
    stages = [[ba.alloc(nch, f"xns{b}f{f}") for f in range(8)] for b in range(nstg)]
    xnd = C.dram[sq.xn_name]
    for n in range(sq.Tn // nch):
        stg = stages[n % nstg]
        tc0 = sq.tc0 + n * (nch // 128)
        for f in range(8):
            src = xnd.ap()[tc0:tc0 + nch // 128, :, f, :].transpose([1, 0, 2])
            P.dma("gpsimd", stg[f].v().rearrange("p (a j) -> p a j", j=128), src,
                  reads=[_db(C, sq.xn_name)], writes=[stg[f].buf])
        body(n, n * nch, stg)
    for b in stages:
        ba.free_tv(b)


def emit_l0_attn(C, sq):
    P, ba, fa = C.P, C.ba, C.fa
    nt, Tn, nch = sq.nt, sq.Tn, sq.nch
    psS, psA, psB, psT = C.psS, C.psA, C.psB, C.psT

    kvn = ba.alloc_list(2, Tn, "kvn")
    qc = ba.alloc_list(4, Tn, "qc")
    kraw = ba.alloc_list(4, Tn, "kraw")
    kvc = fa.alloc(256, "kvc")
    st = fa.alloc(32, "attnst")
    junk = ba.alloc(256, "rmsjunk")

    wkv = loadw(C, "W_kv", 1024, 256)
    wq = loadw(C, "W_q", 1024, 512)
    wkr = loadw(C, "W_krope", 1024, 512)

    def proj_sweep(n, t0, stg):
        nsub = nch // 128
        if ATTN_STAGE <= 0.2:
            return
        for ts_ in range(nsub):
            tg = t0 + ts_ * 128
            for f in range(8):
                P.op("tensor", lambda t, f=f, ts_=ts_: t.matmul(
                    psB.v(cs=slice(0, 256)),
                    stg[f].v(cs=slice(ts_ * 128, ts_ * 128 + 128)),
                    wkv[f].v(), start=(f == 0), stop=(f == 7)),
                    reads=[stg[f].buf, wkv[f].buf], writes=[psB.buf], excl=(f == 0))
            if ATTN_STAGE <= 0.25:
                continue
            P.op("vector", lambda v: v.tensor_copy(out=kvc.v(), in_=psB.v(cs=slice(0, 256))),
                 reads=[psB.buf], writes=[kvc.buf])
            if ATTN_STAGE <= 0.3:
                continue
            P.op("scalar", lambda s: s.activation(
                out=junk.v(), in_=psB.v(cs=slice(0, 256)), func=AF.Square,
                accum_out=st.v(cs=slice(1, 2))),
                reads=[psB.buf], writes=[st.buf, junk.buf], excl=False)
            P.op("vector", lambda v: v.tensor_single_scalar(
                out=st.v(cs=slice(2, 3)), in_=st.v(cs=slice(1, 2)),
                scalar=1.0 / 256.0, op=ALU.mult),
                reads=[st.buf], writes=[st.buf], excl=False)
            P.op("vector", lambda v: v.tensor_single_scalar(
                out=st.v(cs=slice(2, 3)), in_=st.v(cs=slice(2, 3)),
                scalar=1e-6, op=ALU.add), reads=[st.buf], writes=[st.buf], excl=False)
            P.op("scalar", lambda s: s.activation(
                out=st.v(cs=slice(3, 4)), in_=st.v(cs=slice(2, 3)),
                func=AF.Sqrt), reads=[st.buf], writes=[st.buf], excl=False)
            P.op("vector", lambda v: v.reciprocal(
                out=st.v(cs=slice(4, 5)), in_=st.v(cs=slice(3, 4))),
                reads=[st.buf], writes=[st.buf], excl=False)
            if ATTN_STAGE <= 0.35:
                continue
            P.op("vector", lambda v: v.scalar_tensor_tensor(
                out=junk.v(), in0=kvc.v(), scalar=st.v(cs=slice(4, 5)),
                in1=C.gkv.v(rs=slice(0, 128), cs=slice(0, 256)),
                op0=ALU.mult, op1=ALU.mult),
                reads=[kvc.buf, st.buf, C.gkv.buf], writes=[junk.buf])
            for r in range(2):
                P.op("tensor", lambda t, r=r: t.transpose(
                    psT.v(cs=slice(r * 128, r * 128 + 128)),
                    junk.v(cs=slice(r * 128, r * 128 + 128)), C.ident.v()),
                    reads=[junk.buf, C.ident.buf], writes=[psT.buf], excl=(r == 0))
            for r in range(2):
                P.op("scalar", lambda s, r=r, tg=tg: s.copy(
                    out=kvn[r].v(cs=slice(tg, tg + 128)),
                    in_=psT.v(cs=slice(r * 128, r * 128 + 128))),
                    reads=[psT.buf], writes=[kvn[r].buf], excl=False)
        if ATTN_STAGE <= 0.4:
            return
        for m in range(4):
            for f in range(8):
                P.op("tensor", lambda t, m=m, f=f: t.matmul(
                    psS.v(cs=slice(m * 512, m * 512 + nch)),
                    wq[f].v(cs=slice(m * 128, m * 128 + 128)), stg[f].v(),
                    start=(f == 0), stop=(f == 7)),
                    reads=[wq[f].buf, stg[f].buf], writes=[C.psSb[m]], excl=(f == 0))
            P.op("scalar", lambda s, m=m: s.copy(
                out=qc[m].v(cs=slice(t0, t0 + nch)),
                in_=psS.v(cs=slice(m * 512, m * 512 + nch))),
                reads=[C.psSb[m]], writes=[qc[m].buf], excl=False)
        if ATTN_STAGE <= 0.6:
            return
        for m in range(4):
            for f in range(8):
                P.op("tensor", lambda t, m=m, f=f: t.matmul(
                    psS.v(cs=slice(m * 512, m * 512 + nch)),
                    wkr[f].v(cs=slice(m * 128, m * 128 + 128)), stg[f].v(),
                    start=(f == 0), stop=(f == 7)),
                    reads=[wkr[f].buf, stg[f].buf], writes=[C.psSb[m]], excl=(f == 0))
            P.op("scalar", lambda s, m=m: s.copy(
                out=kraw[m].v(cs=slice(t0, t0 + nch)),
                in_=psS.v(cs=slice(m * 512, m * 512 + nch))),
                reads=[C.psSb[m]], writes=[kraw[m].buf], excl=False)

    _xn_sweep(C, sq, proj_sweep)
    ba.free_tv(wkv, wq, wkr)
    if ATTN_STAGE < 2:
        return

    def rope(raw_tvs, out_tvs):
        tmp1 = ba.alloc(512, "ropet1")
        tmp2 = ba.alloc(512, "ropet2")
        for m in range(len(raw_tvs)):
            for w0 in range(0, Tn, 512):
                w = min(512, Tn - w0)
                cs = slice(w0, w0 + w)
                P.op("tensor", lambda t, m=m, cs=cs, w=w: t.matmul(
                    psB.v(cs=slice(0, w)), C.rrot.v(), raw_tvs[m].v(cs=cs),
                    start=True, stop=True),
                    reads=[C.rrot.buf, raw_tvs[m].buf], writes=[psB.buf])
                P.op("vector", lambda v, cs=cs, w=w: v.tensor_tensor(
                    out=tmp1.v(cs=slice(0, w)), in0=psB.v(cs=slice(0, w)),
                    in1=sq.ropeS.v(cs=cs), op=ALU.mult),
                    reads=[psB.buf, sq.ropeS.buf], writes=[tmp1.buf])
                P.op("vector", lambda v, m=m, cs=cs, w=w: v.tensor_tensor(
                    out=tmp2.v(cs=slice(0, w)), in0=raw_tvs[m].v(cs=cs),
                    in1=sq.ropeC.v(cs=cs), op=ALU.mult),
                    reads=[raw_tvs[m].buf, sq.ropeC.buf], writes=[tmp2.buf])
                P.op("vector", lambda v, m=m, cs=cs, w=w: v.tensor_tensor(
                    out=out_tvs[m].v(cs=cs), in0=tmp1.v(cs=slice(0, w)),
                    in1=tmp2.v(cs=slice(0, w)), op=ALU.add),
                    reads=[tmp1.buf, tmp2.buf], writes=[out_tvs[m].buf], excl=False)
        ba.free_tv(tmp1, tmp2)

    otm = ba.alloc(nt * 1024, "otm")
    Kh = ba.alloc(Tn, "Kh")
    Qh = ba.alloc(Tn, "Qh")
    Pb = ba.alloc(Tn, "Pb")
    PTs = ba.alloc(512, "PTs")
    for g in range(2):
        gcs = slice(g * 512, g * 512 + 512)
        kr = ba.alloc_list(2, Tn, "kr")
        rope(kraw[2 * g:2 * g + 2], kr)
        wqr = loadw(C, "W_qrope", 512, 512, name="wqr", cs=slice(g * 256, g * 256 + 256))
        qraw = ba.alloc_list(2, Tn, "qraw")
        for m in range(2):
            for w0 in range(0, Tn, 512):
                w = min(512, Tn - w0)
                for f in range(4):
                    P.op("tensor", lambda t, m=m, f=f, w0=w0, w=w, wqr=wqr: t.matmul(
                        psS.v(cs=slice(0, w)),
                        wqr[f].v(cs=slice(m * 128, m * 128 + 128)),
                        qc[f].v(cs=slice(w0, w0 + w)),
                        start=(f == 0), stop=(f == 3)),
                        reads=[wqr[f].buf, qc[f].buf], writes=[C.psSb[0]], excl=(f == 0))
                P.op("scalar", lambda s, m=m, w0=w0, w=w, qraw=qraw: s.copy(
                    out=qraw[m].v(cs=slice(w0, w0 + w)), in_=psS.v(cs=slice(0, w))),
                    reads=[C.psSb[0]], writes=[qraw[m].buf], excl=False)
        ba.free_tv(wqr)
        qr = ba.alloc_list(2, Tn, "qr")
        rope(qraw, qr)
        ba.free_tv(qraw)
        wqd = loadw(C, "W_qdec", 512, 1024, name="wqd", cs=gcs)
        if _dbg_on("WQD") and sq.name == "s0" and g == 1:
            for ii in range(4):
                P.dma("sync", C.dram["WQD"].ap()[ii], wqd[ii].v(),
                      reads=[wqd[ii].buf], writes=[_db(C, "WQD")], excl=False)
        qcont = ba.alloc_list(4, Tn, "qcont")
        for m in range(4):
            for w0 in range(0, Tn, 512):
                w = min(512, Tn - w0)
                for f in range(4):
                    P.op("tensor", lambda t, m=m, f=f, w0=w0, w=w, wqd=wqd: t.matmul(
                        psS.v(cs=slice(0, w)),
                        wqd[f].v(cs=slice(m * 128, m * 128 + 128)),
                        qc[f].v(cs=slice(w0, w0 + w)),
                        start=(f == 0), stop=(f == 3)),
                        reads=[wqd[f].buf, qc[f].buf], writes=[C.psSb[0]], excl=(f == 0))
                if _dbg_on("PSD") and sq.name == "s0" and g == 1 and m == 0 and w0 == 0:
                    tapb = ba.alloc(1024, "tapb")
                    P.op("vector", lambda v, tapb=tapb: v.tensor_copy(
                        out=tapb.v(cs=slice(0, 512)), in_=psS.v(cs=slice(0, 512))),
                        reads=[C.psSb[0]], writes=[tapb.buf])
                    P.dma("gpsimd", C.dram["PSD"].ap()[:, 0:512],
                          tapb.v(cs=slice(0, 512)), reads=[tapb.buf],
                          writes=[_db(C, "PSD")], excl=False)
                P.op("scalar", lambda s, m=m, w0=w0, w=w, qcont=qcont: s.copy(
                    out=qcont[m].v(cs=slice(w0, w0 + w)), in_=psS.v(cs=slice(0, w))),
                    reads=[C.psSb[0]], writes=[qcont[m].buf], excl=False)
        ba.free_tv(wqd)
        wkd = loadw(C, "W_kdec", 256, 1024, name="wkd", cs=gcs)
        kc = ba.alloc_list(4, Tn, "kc")
        for m in range(4):
            for w0 in range(0, Tn, 512):
                w = min(512, Tn - w0)
                for r in range(2):
                    P.op("tensor", lambda t, m=m, r=r, w0=w0, w=w, wkd=wkd: t.matmul(
                        psS.v(cs=slice(0, w)),
                        wkd[r].v(cs=slice(m * 128, m * 128 + 128)),
                        kvn[r].v(cs=slice(w0, w0 + w)),
                        start=(r == 0), stop=(r == 1)),
                        reads=[wkd[r].buf, kvn[r].buf], writes=[C.psSb[0]], excl=(r == 0))
                P.op("scalar", lambda s, m=m, w0=w0, w=w, kc=kc: s.copy(
                    out=kc[m].v(cs=slice(w0, w0 + w)), in_=psS.v(cs=slice(0, w))),
                    reads=[C.psSb[0]], writes=[kc[m].buf], excl=False)
        ba.free_tv(wkd)
        wvd = loadw(C, "W_vdec", 256, 1024, name="wvd", cs=gcs)
        vtm = ba.alloc(nt * 512, "vtm")
        for tc in range(nt):
            for r in range(2):
                P.op("tensor", lambda t, tc=tc, r=r, wvd=wvd: t.matmul(
                    psA.v(cs=slice(0, 512)),
                    kvn[r].v(cs=slice(tc * 128, tc * 128 + 128)),
                    wvd[r].v(), start=(r == 0), stop=(r == 1)),
                    reads=[kvn[r].buf, wvd[r].buf], writes=[psA.buf], excl=(r == 0))
            P.op("scalar", lambda s, tc=tc, vtm=vtm: s.copy(
                out=vtm.v(cs=slice(tc * 512, tc * 512 + 512)), in_=psA.v(cs=slice(0, 512))),
                reads=[psA.buf], writes=[vtm.buf], excl=False)
        ba.free_tv(wvd)

        if _dbg_on("QCD") and sq.name == "s0" and g == 1:
            for ii, tvv in enumerate(qcont + qr + kc[:1] + kr[:1]):
                P.dma("sync", C.dram["QCD"].ap()[ii], tvv.v(),
                      reads=[tvv.buf], writes=[_db(C, "QCD")], excl=False)
        for hl in range(8 if ATTN_STAGE >= 3 else 0):
            h = g * 8 + hl
            for dst, cont, rope_ in ((Kh, kc, kr), (Qh, qcont, qr)):
                P.op("vector", lambda v, dst=dst, cont=cont, hl=hl: v.tensor_copy(
                    out=dst.v(rs=slice(0, 64)),
                    in_=cont[hl // 2].v(rs=slice((hl % 2) * 64, (hl % 2) * 64 + 64))),
                    reads=[cont[hl // 2].buf], writes=[dst.buf])
                P.op("vector", lambda v, dst=dst, rope_=rope_, hl=hl: v.tensor_copy(
                    out=dst.v(rs=slice(64, 96)),
                    in_=rope_[hl // 4].v(rs=slice((hl % 4) * 32, (hl % 4) * 32 + 32))),
                    reads=[rope_[hl // 4].buf], writes=[dst.buf], excl=False)
            for qi in range(nt):
                klist = sq.kmap(qi)
                ktot = len(klist) * 128
                j0 = klist[0][0]
                for a0 in range(0, ktot, 512):
                    w = min(512, ktot - a0)
                    P.op("tensor", lambda t, a0=a0, w=w, qi=qi, j0=j0: t.matmul(
                        psS.v(cs=slice(a0, a0 + w)),
                        Qh.v(rs=slice(0, 96), cs=slice(qi * 128, qi * 128 + 128)),
                        Kh.v(rs=slice(0, 96), cs=slice(j0 * 128 + a0, j0 * 128 + a0 + w)),
                        start=True, stop=True),
                        reads=[Qh.buf, Kh.buf], writes=[C.psSb[a0 // 512]],
                        excl=True)
                for idx, (j, mtv) in enumerate(klist):
                    if mtv is not None:
                        P.op("vector", lambda v, idx=idx, mtv=mtv: v.tensor_tensor(
                            out=psS.v(cs=slice(idx * 128, idx * 128 + 128)),
                            in0=psS.v(cs=slice(idx * 128, idx * 128 + 128)),
                            in1=mtv.v(), op=ALU.add),
                            reads=[C.psSb[idx * 128 // 512], mtv.buf],
                            writes=[C.psSb[idx * 128 // 512]], excl=False)
                sbufs = [C.psSb[a // 512] for a in range(0, ktot, 512)]
                P.op("vector", lambda v, ktot=ktot: v.reduce_max(
                    out=st.v(cs=slice(8, 9)), in_=psS.v(cs=slice(0, ktot)), axis=AX.X),
                    reads=sbufs, writes=[st.buf], excl=False)
                P.op("vector", lambda v: v.tensor_single_scalar(
                    out=st.v(cs=slice(9, 10)), in_=st.v(cs=slice(8, 9)),
                    scalar=-float(SCALE_MLA), op=ALU.mult),
                    reads=[st.buf], writes=[st.buf], excl=False)
                P.op("scalar", lambda s, ktot=ktot: s.activation(
                    out=Pb.v(cs=slice(0, ktot)), in_=psS.v(cs=slice(0, ktot)),
                    func=AF.Exp, bias=st.v(cs=slice(9, 10)),
                    scale=float(SCALE_MLA), accum_out=st.v(cs=slice(10, 11))),
                    reads=sbufs + [st.buf], writes=[Pb.buf, st.buf])
                P.op("vector", lambda v: v.reciprocal(
                    out=st.v(cs=slice(11, 12)), in_=st.v(cs=slice(10, 11))),
                    reads=[st.buf], writes=[st.buf], excl=False)
                nk = len(klist)
                for b0 in range(0, nk, 4):
                    bw = min(4, nk - b0)
                    for u in range(bw):
                        P.op("tensor", lambda t, b0=b0, u=u: t.transpose(
                            psT.v(cs=slice(u * 128, u * 128 + 128)),
                            Pb.v(cs=slice((b0 + u) * 128, (b0 + u) * 128 + 128)),
                            C.ident.v()),
                            reads=[Pb.buf, C.ident.buf], writes=[psT.buf], excl=(u == 0))
                    eng = "vector" if (b0 // 4) % 2 == 0 else "scalar"
                    if eng == "vector":
                        fn = lambda e, bw=bw: e.tensor_copy(
                            out=PTs.v(cs=slice(0, bw * 128)),
                            in_=psT.v(cs=slice(0, bw * 128)))
                    else:
                        fn = lambda e, bw=bw: e.copy(
                            out=PTs.v(cs=slice(0, bw * 128)),
                            in_=psT.v(cs=slice(0, bw * 128)))
                    P.op(eng, fn, reads=[psT.buf], writes=[PTs.buf])
                    for u in range(bw):
                        j = klist[b0 + u][0]
                        P.op("tensor", lambda t, u=u, j=j, hl=hl, vtm=vtm,
                             first=(b0 == 0 and u == 0), last=(b0 + u == nk - 1): t.matmul(
                            psA.v(cs=slice(512, 576)),
                            PTs.v(cs=slice(u * 128, u * 128 + 128)),
                            vtm.v(cs=slice(j * 512 + hl * 64, j * 512 + hl * 64 + 64)),
                            start=first, stop=last),
                            reads=[PTs.buf, vtm.buf], writes=[psA.buf],
                            excl=(b0 == 0 and u == 0))
                P.op("vector", lambda v, qi=qi, h=h: v.tensor_single_scalar(
                    out=otm.v(cs=slice(qi * 1024 + h * 64, qi * 1024 + h * 64 + 64)),
                    in_=psA.v(cs=slice(512, 576)), scalar=st.v(cs=slice(11, 12)),
                    op=ALU.mult),
                    reads=[psA.buf, st.buf], writes=[otm.buf], excl=False)
                if _dbg_on("STA") and sq.name == "s0":
                    P.dma("sync", C.dram["STA"].ap()[g, hl, qi],
                          st.v(cs=slice(0, 16)), reads=[st.buf],
                          writes=[_db(C, "STA")], excl=False)
                    if g == 0 and hl == 0 and qi == 1:
                        P.dma("sync", C.dram["PBD"].ap()[:, 0:ktot],
                              Pb.v(cs=slice(0, ktot)), reads=[Pb.buf],
                              writes=[_db(C, "PBD")], excl=False)
        if _dbg_on("QCD2") and sq.name == "s0" and g == 1:
            P.dma("sync", C.dram["QCD2"].ap(), qcont[0].v(cs=slice(0, 512)),
                  reads=[qcont[0].buf], writes=[_db(C, "QCD2")], excl=False)
        ba.free_tv(kr, qr, qcont, kc, vtm)
    ba.free_tv(Kh, Qh, Pb, PTs, qc, kvn, kraw)
    if ATTN_STAGE < 4:
        return

    if _dbg_on("OTMD") and sq.name == "s0":
        for tc in range(nt):
            P.dma("sync", C.dram["OTMD"].ap()[tc],
                  otm.v(cs=slice(tc * 1024, tc * 1024 + 1024)),
                  reads=[otm.buf], writes=[_db(C, "OTMD")], excl=False)
    wo = loadw(C, "W_o", 1024, 1024)
    ofst = ba.alloc(1024, "ofstage")
    aost = [fa.alloc(1024, "aostage") for _ in range(2)]
    aod = C.dram[sq.ao_name]
    for tc in range(nt):
        for c in range(8):
            P.op("tensor", lambda t, tc=tc, c=c: t.transpose(
                psT.v(cs=slice(c * 128, c * 128 + 128)),
                otm.v(cs=slice(tc * 1024 + c * 128, tc * 1024 + c * 128 + 128)),
                C.ident.v()),
                reads=[otm.buf, C.ident.buf], writes=[psT.buf], excl=(c == 0))
        P.op("scalar", lambda s: s.copy(out=ofst.v(), in_=psT.v()),
             reads=[psT.buf], writes=[ofst.buf])
        for c in range(8):
            for half in range(2):
                P.op("tensor", lambda t, c=c, half=half: t.matmul(
                    psA.v(cs=slice(half * 512, half * 512 + 512)),
                    ofst.v(cs=slice(c * 128, c * 128 + 128)),
                    wo[c].v(cs=slice(half * 512, half * 512 + 512)),
                    start=(c == 0), stop=(c == 7)),
                    reads=[ofst.buf, wo[c].buf], writes=[psA.buf],
                    excl=(c == 0 and half == 0))
        ast = aost[tc % 2]
        P.op("vector", lambda v, ast=ast: v.tensor_copy(out=ast.v(), in_=psA.v()),
             reads=[psA.buf], writes=[ast.buf])
        P.dma("sync", aod.ap()[sq.ao_row0 + tc * 128: sq.ao_row0 + (tc + 1) * 128, :],
              ast.v(), reads=[ast.buf], writes=[_db(C, sq.ao_name)], excl=False)
    ba.free_tv(wo, ofst, otm, junk)
    fa.free_tv(aost, kvc, st)


def emit_l0_ffn(C, sq, l):
    P, ba, fa = C.P, C.ba, C.fa
    nt, Tn = sq.nt, sq.Tn
    W5 = min(512, Tn)
    psS, psA, psB, psT = C.psS, C.psA, C.psB, C.psT
    yd = C.dram[sq.y_name]
    ydb = _db(C, sq.y_name)

    yb = ba.alloc_list(8, Tn + 16, "yb")
    if l == 0:
        xnd = C.dram[sq.xn_name]
        for f in range(8):
            P.op("gpsimd", lambda g, f=f: g.memset(yb[f].v(cs=slice(0, 1)), 0.0),
                 writes=[yb[f].buf])
            src = xnd.ap()[sq.tc0:sq.tc0 + nt, :, f, :].transpose([1, 0, 2])
            P.dma("gpsimd", yb[f].v(cs=slice(1, Tn + 1)).rearrange(
                "p (a j) -> p a j", j=128), src,
                reads=[_db(C, sq.xn_name)], writes=[yb[f].buf], excl=False)
    else:
        for f in range(8):
            P.op("gpsimd", lambda g, f=f: g.memset(yb[f].v(cs=slice(0, 1)), 0.0),
                 writes=[yb[f].buf])
            P.dma("gpsimd", yb[f].v(cs=slice(1, Tn + 1)),
                  yd.ap()[sq.y_idx, f, :, 1:Tn + 1],
                  reads=[ydb], writes=[yb[f].buf], excl=False)

    if sq.shift_blocks is None:
        ysh = None
        def sh_ap(f, t0, w):
            base = yb[f] if f < 4 else yb[f]
            off = 1 if f < 4 else 0
            return base.v(cs=slice(t0 + off, t0 + off + w)), base.buf
    else:
        ysh = ba.alloc_list(4, Tn + 16, "ysh")
        nb = Tn // PT_
        for f in range(4, 8):
            sv = ysh[f - 4]
            P.op("gpsimd", lambda g, sv=sv: g.memset(sv.v(cs=slice(0, Tn + 1)), 0.0),
                 writes=[sv.buf])
            pitchy = yb[f].t.shape[1]
            src = AP(yb[f].t[:].tensor, yb[f].c0 + 1, [[pitchy, 128], [PT_, nb], [1, PT_ - 1]])
            dst = AP(sv.t[:].tensor, sv.c0 + 2, [[sv.t.shape[1], 128], [PT_, nb], [1, PT_ - 1]])
            P.op("vector", lambda v, dst=dst, src=src: v.tensor_copy(out=dst, in_=src),
                 reads=[yb[f].buf], writes=[sv.buf], excl=False)
        def sh_ap(f, t0, w):
            base = yb[f] if f < 4 else ysh[f - 4]
            return base.v(cs=slice(t0 + 1, t0 + 1 + w)), base.buf

    wqkv = loadw(C, f"wqkv{l}", 1024, 1536, name="wqkv")
    qfm = ba.alloc_list(4, Tn, "qfm")
    kfm = ba.alloc_list(4, Tn, "kfm")
    vloc = ba.alloc(nt * 512, "vloc")
    qkf = fa.alloc(1024, "qkf")
    qksq = fa.alloc(1024, "qksq")
    st16 = fa.alloc(64, "ffst")
    qkn = ba.alloc(1024, "qkn")
    q1 = ba.alloc(1024, "q1")
    qsw = ba.alloc(512, "qsw")
    q2 = ba.alloc(512, "q2")
    k2t = ba.alloc(512, "k2t")
    xt = {k: ba.alloc(512, "xt" + k) for k in ("cs", "ss", "ci", "si")}
    qsrb = load_const(C, f"qs_rb{l}", 512, "fa")
    ksrb = load_const(C, f"ks_rb{l}", 512, "fa")
    for tc in range(nt):
        t0 = tc * 128
        for k in ("cs", "ss", "ci", "si"):
            nmk = ("x" + k) + ("p" if sq.prefix else "")
            P.dma("sync", xt[k].v(), C.dram[nmk].ap()[t0:t0 + 128, :],
                  reads=[_db(C, nmk)], writes=[xt[k].buf])
        for n3 in range(3):
            for f in range(8):
                ap_, bf_ = sh_ap(f, t0, 128)
                P.op("tensor", lambda t, n3=n3, f=f, ap_=ap_: t.matmul(
                    psS.v(cs=slice(n3 * 512, n3 * 512 + 512)), ap_,
                    wqkv[f].v(cs=slice(n3 * 512, n3 * 512 + 512)),
                    start=(f == 0), stop=(f == 7)),
                    reads=[bf_, wqkv[f].buf], writes=[C.psSb[n3]], excl=(f == 0))
        for half in range(2):
            P.op("vector", lambda v, half=half: v.tensor_copy(
                out=qkf.v(cs=slice(half * 512, half * 512 + 512)),
                in_=psS.v(cs=slice(half * 512, half * 512 + 512))),
                reads=[C.psSb[half]], writes=[qkf.buf], excl=(half == 0))
        P.op("scalar", lambda s, tc=tc: s.copy(
            out=vloc.v(cs=slice(tc * 512, tc * 512 + 512)),
            in_=psS.v(cs=slice(1024, 1536))),
            reads=[C.psSb[2]], writes=[vloc.buf], excl=False)
        P.op("scalar", lambda s: s.square(out=qksq.v(), in_=qkf.v()),
             reads=[qkf.buf], writes=[qksq.buf])
        for hseg in range(16):
            P.op("vector", lambda v, hseg=hseg: v.reduce_sum(
                out=st16.v(cs=slice(hseg, hseg + 1)),
                in_=qksq.v(cs=slice(hseg * 64, hseg * 64 + 64)), axis=AX.X),
                reads=[qksq.buf], writes=[st16.buf], excl=False)
        P.op("scalar", lambda s: s.activation(
            out=st16.v(cs=slice(16, 32)), in_=st16.v(cs=slice(0, 16)),
            func=AF.Sqrt), reads=[st16.buf], writes=[st16.buf], excl=False)
        P.op("vector", lambda v: v.tensor_single_scalar(
            out=st16.v(cs=slice(32, 48)), in_=st16.v(cs=slice(16, 32)),
            scalar=1e-12, op=ALU.max), reads=[st16.buf], writes=[st16.buf], excl=False)
        P.op("vector", lambda v: v.reciprocal(
            out=st16.v(cs=slice(48, 64)), in_=st16.v(cs=slice(32, 48))),
            reads=[st16.buf], writes=[st16.buf], excl=False)
        P.op("vector", lambda v: v.tensor_tensor(
            out=qkn.v().rearrange("p (a b) -> p a b", a=16),
            in0=qkf.v().rearrange("p (a b) -> p a b", a=16),
            in1=AP(st16.t[:].tensor, st16.c0 + 48,
                   [[st16.t.shape[1], 128], [1, 16], [0, 64]]),
            op=ALU.mult),
            reads=[qkf.buf, st16.buf], writes=[qkn.buf])
        P.op("vector", lambda v: v.tensor_tensor(
            out=q1.v(cs=slice(0, 512)), in0=qkn.v(cs=slice(0, 512)),
            in1=qsrb.v(), op=ALU.mult),
            reads=[qkn.buf, qsrb.buf], writes=[q1.buf])
        P.op("vector", lambda v: v.tensor_tensor(
            out=q1.v(cs=slice(512, 1024)), in0=qkn.v(cs=slice(512, 1024)),
            in1=ksrb.v(), op=ALU.mult),
            reads=[qkn.buf, ksrb.buf], writes=[q1.buf], excl=False)
        for side, (tc_a, tc_b), dst in (("q", ("cs", "ss"), q2), ("k", ("ci", "si"), k2t)):
            off = 0 if side == "q" else 512
            src_sw = AP(q1.t[:].tensor, q1.c0 + off + 32,
                        [[q1.t.shape[1], 128], [64, 8], [1, 32]])
            dst_sw = AP(qsw.t[:].tensor, qsw.c0,
                        [[qsw.t.shape[1], 128], [64, 8], [1, 32]])
            P.op("vector", lambda v, src_sw=src_sw, dst_sw=dst_sw: v.tensor_copy(
                out=dst_sw, in_=src_sw), reads=[q1.buf], writes=[qsw.buf])
            src_sw2 = AP(q1.t[:].tensor, q1.c0 + off,
                         [[q1.t.shape[1], 128], [64, 8], [1, 32]])
            dst_sw2 = AP(qsw.t[:].tensor, qsw.c0 + 32,
                         [[qsw.t.shape[1], 128], [64, 8], [1, 32]])
            P.op("vector", lambda v, src_sw2=src_sw2, dst_sw2=dst_sw2: v.tensor_copy(
                out=dst_sw2, in_=src_sw2), reads=[q1.buf], writes=[qsw.buf], excl=False)
            P.op("vector", lambda v, tcb=tc_b: v.tensor_tensor(
                out=qsw.v(), in0=qsw.v(), in1=xt[tcb].v(),
                op=ALU.mult), reads=[qsw.buf, xt[tc_b].buf], writes=[qsw.buf])
            P.op("vector", lambda v, off=off, dst=dst, tc_a=tc_a: v.scalar_tensor_tensor(
                out=dst.v(), in0=q1.v(cs=slice(off, off + 512)), scalar=1.0,
                in1=xt[tc_a].v(), op0=ALU.mult, op1=ALU.mult),
                reads=[q1.buf, xt[tc_a].buf], writes=[dst.buf])
            P.op("vector", lambda v, dst=dst: v.tensor_tensor(
                out=dst.v(), in0=dst.v(), in1=qsw.v(), op=ALU.add),
                reads=[dst.buf, qsw.buf], writes=[dst.buf], excl=False)
        for src, fml in ((q2, qfm), (k2t, kfm)):
            for c in range(4):
                P.op("tensor", lambda t, src=src, c=c: t.transpose(
                    psT.v(cs=slice(c * 128, c * 128 + 128)),
                    src.v(cs=slice(c * 128, c * 128 + 128)), C.ident.v()),
                    reads=[src.buf, C.ident.buf], writes=[psT.buf], excl=(c == 0))
            for c in range(4):
                P.op("scalar", lambda s, fml=fml, c=c, t0=t0: s.copy(
                    out=fml[c].v(cs=slice(t0, t0 + 128)),
                    in_=psT.v(cs=slice(c * 128, c * 128 + 128))),
                    reads=[psT.buf], writes=[fml[c].buf], excl=False)
    ba.free_tv(wqkv, qkn, q1, qsw, q2, k2t, list(xt.values()))
    fa.free_tv(qkf, qksq, qsrb, ksrb)

    Pb = ba.alloc(768, "ffPb")
    PTs = ba.alloc(512, "ffPTs")
    otm = ba.alloc(nt * 512, "ffotm")
    for h in range(8):
        ch, rs_ = h // 2, slice((h % 2) * 64, (h % 2) * 64 + 64)
        for qi in range(nt):
            klist = sq.ff_kmap(qi)
            ktot = len(klist) * 128
            j0 = klist[0][0]
            for a0 in range(0, ktot, 512):
                w = min(512, ktot - a0)
                P.op("tensor", lambda t, a0=a0, w=w, qi=qi, j0=j0, ch=ch, rs_=rs_: t.matmul(
                    psS.v(cs=slice(a0, a0 + w)),
                    qfm[ch].v(rs=rs_, cs=slice(qi * 128, qi * 128 + 128)),
                    kfm[ch].v(rs=rs_, cs=slice(j0 * 128 + a0, j0 * 128 + a0 + w)),
                    start=True, stop=True),
                    reads=[qfm[ch].buf, kfm[ch].buf], writes=[C.psSb[a0 // 512]])
            for idx, (j, mtv) in enumerate(klist):
                if mtv is not None:
                    P.op("vector", lambda v, idx=idx, mtv=mtv: v.tensor_tensor(
                        out=psS.v(cs=slice(idx * 128, idx * 128 + 128)),
                        in0=psS.v(cs=slice(idx * 128, idx * 128 + 128)),
                        in1=mtv.v(), op=ALU.add),
                        reads=[C.psSb[idx * 128 // 512], mtv.buf],
                        writes=[C.psSb[idx * 128 // 512]], excl=False)
            sbufs = [C.psSb[a // 512] for a in range(0, ktot, 512)]
            P.op("vector", lambda v, ktot=ktot: v.reduce_max(
                out=st16.v(cs=slice(8, 9)), in_=psS.v(cs=slice(0, ktot)), axis=AX.X),
                reads=sbufs, writes=[st16.buf], excl=False)
            P.op("vector", lambda v: v.tensor_single_scalar(
                out=st16.v(cs=slice(9, 10)), in_=st16.v(cs=slice(8, 9)),
                scalar=-float(SCALE_LOC), op=ALU.mult),
                reads=[st16.buf], writes=[st16.buf], excl=False)
            P.op("scalar", lambda s, ktot=ktot: s.activation(
                out=Pb.v(cs=slice(0, ktot)), in_=psS.v(cs=slice(0, ktot)),
                func=AF.Exp, bias=st16.v(cs=slice(9, 10)),
                scale=float(SCALE_LOC), accum_out=st16.v(cs=slice(10, 11))),
                reads=sbufs + [st16.buf], writes=[Pb.buf, st16.buf])
            P.op("vector", lambda v: v.reciprocal(
                out=st16.v(cs=slice(11, 12)), in_=st16.v(cs=slice(10, 11))),
                reads=[st16.buf], writes=[st16.buf], excl=False)
            nk = len(klist)
            for b0 in range(0, nk, 4):
                bw = min(4, nk - b0)
                for u in range(bw):
                    P.op("tensor", lambda t, b0=b0, u=u: t.transpose(
                        psT.v(cs=slice(u * 128, u * 128 + 128)),
                        Pb.v(cs=slice((b0 + u) * 128, (b0 + u) * 128 + 128)),
                        C.ident.v()),
                        reads=[Pb.buf, C.ident.buf], writes=[psT.buf], excl=(u == 0))
                eng = "vector" if (b0 // 4) % 2 == 0 else "scalar"
                if eng == "vector":
                    fn = lambda e, bw=bw: e.tensor_copy(
                        out=PTs.v(cs=slice(0, bw * 128)), in_=psT.v(cs=slice(0, bw * 128)))
                else:
                    fn = lambda e, bw=bw: e.copy(
                        out=PTs.v(cs=slice(0, bw * 128)), in_=psT.v(cs=slice(0, bw * 128)))
                P.op(eng, fn, reads=[psT.buf], writes=[PTs.buf])
                for u in range(bw):
                    j = klist[b0 + u][0]
                    P.op("tensor", lambda t, u=u, j=j, h=h,
                         first=(b0 == 0 and u == 0), last=(b0 + u == nk - 1): t.matmul(
                        psA.v(cs=slice(0, 64)),
                        PTs.v(cs=slice(u * 128, u * 128 + 128)),
                        vloc.v(cs=slice(j * 512 + h * 64, j * 512 + h * 64 + 64)),
                        start=first, stop=last),
                        reads=[PTs.buf, vloc.buf], writes=[psA.buf],
                        excl=(b0 == 0 and u == 0))
            P.op("vector", lambda v, qi=qi, h=h: v.tensor_single_scalar(
                out=otm.v(cs=slice(qi * 512 + h * 64, qi * 512 + h * 64 + 64)),
                in_=psA.v(cs=slice(0, 64)), scalar=st16.v(cs=slice(11, 12)),
                op=ALU.mult),
                reads=[psA.buf, st16.buf], writes=[otm.buf], excl=False)
    ba.free_tv(Pb, PTs, qfm, kfm, vloc)

    ofm = ba.alloc_list(4, Tn, "ofm")
    for tc in range(nt):
        for c in range(4):
            P.op("tensor", lambda t, tc=tc, c=c: t.transpose(
                psT.v(cs=slice(c * 128, c * 128 + 128)),
                otm.v(cs=slice(tc * 512 + c * 128, tc * 512 + c * 128 + 128)),
                C.ident.v()),
                reads=[otm.buf, C.ident.buf], writes=[psT.buf], excl=(c == 0))
        for c in range(4):
            P.op("scalar", lambda s, c=c, tc=tc: s.copy(
                out=ofm[c].v(cs=slice(tc * 128, tc * 128 + 128)),
                in_=psT.v(cs=slice(c * 128, c * 128 + 128))),
                reads=[psT.buf], writes=[ofm[c].buf], excl=False)
    ba.free_tv(otm)
    wo = loadw(C, f"wo{l}", 512, 1024, name="wo")
    ybm = ba.alloc_list(8, Tn + 16, "ybm")
    yst = [fa.alloc(512, "yst") for _ in range(2)]
    yin = [fa.alloc(512, "yin") for _ in range(2)]
    for m in range(8):
        P.op("gpsimd", lambda g, m=m: g.memset(ybm[m].v(cs=slice(0, 1)), 0.0),
             writes=[ybm[m].buf], excl=False)
        for n in range(Tn // W5):
            n0 = n * W5
            for c in range(4):
                P.op("tensor", lambda t, m=m, c=c, n0=n0: t.matmul(
                    psB.v(cs=slice(0, W5)),
                    wo[c].v(cs=slice(m * 128, m * 128 + 128)),
                    ofm[c].v(cs=slice(n0, n0 + W5)),
                    start=(c == 0), stop=(c == 3)),
                    reads=[wo[c].buf, ofm[c].buf], writes=[psB.buf], excl=(c == 0))
            ys = yst[n % 2]
            if l == 0:
                P.op("vector", lambda v, m=m, n0=n0, ys=ys: v.tensor_tensor(
                    out=ys.v(cs=slice(0, W5)), in0=psB.v(cs=slice(0, W5)),
                    in1=yb[m].v(cs=slice(n0 + 1, n0 + 1 + W5)), op=ALU.add),
                    reads=[psB.buf, yb[m].buf], writes=[ys.buf])
            else:
                yi = yin[n % 2]
                P.dma("gpsimd", yi.v(cs=slice(0, W5)),
                      yd.ap()[sq.y_idx, m, :, n0 + 1:n0 + 1 + W5],
                      reads=[ydb], writes=[yi.buf])
                P.op("vector", lambda v, yi=yi, ys=ys: v.tensor_tensor(
                    out=ys.v(cs=slice(0, W5)), in0=psB.v(cs=slice(0, W5)),
                    in1=yi.v(cs=slice(0, W5)), op=ALU.add),
                    reads=[psB.buf, yi.buf], writes=[ys.buf])
            P.dma("sync", yd.ap()[sq.y_idx, m, :, n0 + 1:n0 + 1 + W5],
                  ys.v(cs=slice(0, W5)), reads=[ys.buf], writes=[ydb], excl=False)
            P.op("vector", lambda v, m=m, n0=n0, ys=ys: v.tensor_copy(
                out=ybm[m].v(cs=slice(n0 + 1, n0 + 1 + W5)), in_=ys.v(cs=slice(0, W5))),
                reads=[ys.buf], writes=[ybm[m].buf], excl=False)
    ba.free_tv(wo, ofm, yb)
    if ysh is not None:
        ba.free_tv(ysh)
    fa.free_tv(yst, yin)

    if sq.shift_blocks is None:
        ysh2 = None
        def sh2_ap(f, t0, w):
            off = 1 if f < 4 else 0
            return ybm[f].v(cs=slice(t0 + off, t0 + off + w)), ybm[f].buf
    else:
        ysh2 = ba.alloc_list(4, Tn + 16, "ysh2")
        nb = Tn // PT_
        for f in range(4, 8):
            sv = ysh2[f - 4]
            P.op("gpsimd", lambda g, sv=sv: g.memset(sv.v(cs=slice(0, Tn + 1)), 0.0),
                 writes=[sv.buf])
            src = AP(ybm[f].t[:].tensor, ybm[f].c0 + 1,
                     [[ybm[f].t.shape[1], 128], [PT_, nb], [1, PT_ - 1]])
            dst = AP(sv.t[:].tensor, sv.c0 + 2,
                     [[sv.t.shape[1], 128], [PT_, nb], [1, PT_ - 1]])
            P.op("vector", lambda v, dst=dst, src=src, f=f: v.tensor_copy(
                out=dst, in_=src),
                reads=[ybm[f].buf], writes=[sv.buf], excl=False)
        def sh2_ap(f, t0, w):
            base = ybm[f] if f < 4 else ysh2[f - 4]
            return base.v(cs=slice(t0 + 1, t0 + 1 + w)), base.buf

    w2res = loadw(C, f"w2_{l}", HID, 1024, name="w2res")
    b1cp = load_const(C, f"b1cp{l}", len(GLU_BLOCKS), "fa")
    b2cp = load_const(C, f"b2cp{l}", 8, "fa")
    lgcp = load_const(C, f"lngcp{l}", 8, "fa")
    lbcp = load_const(C, f"lnbcp{l}", 8, "fa")
    nchunks = Tn // W5
    xw1d = [ba.alloc_list(8, W5, "xw1n") for _ in range(min(2, nchunks))]
    hn = [ba.alloc(W5, f"hn{j}") for j in range(len(H_CHUNKS))]
    sqt = [ba.alloc(W5, "sqt") for _ in range(2)]
    srow = fa.alloc(W5, "srow")
    sqrow = fa.alloc(W5, "sqrow")
    t1f = fa.alloc(W5, "t1f")
    w2out = fa.alloc(W5, "w2out")
    yin2 = fa.alloc(W5, "yin2")
    w1t = [ba.alloc(1024, "w1t") for _ in range(2)]
    gl = ba.alloc(W5, "gelu")
    w1c = C.dram[f"W1C{l}"]
    w1p = C.dram[f"w1p{l}"]

    def stats_ln_thunks(n):
        n0 = n * W5
        xw1n = xw1d[n % 2]
        th = []
        for f in range(8):
            ap_, bf_ = sh2_ap(f, n0, W5)
            th.append(lambda ap_=ap_, bf_=bf_, f=f: P.op(
                "tensor", lambda t: t.matmul(
                    psB.v(rs=slice(0, 1), cs=slice(0, W5)), C.onesk.v(), ap_,
                    start=(f == 0), stop=(f == 7)),
                reads=[bf_, C.onesk.buf], writes=[psB.buf], excl=(f == 0)))
        for f in range(8):
            ap_, bf_ = sh2_ap(f, n0, W5)
            sq_ = sqt[f % 2]
            th.append(lambda ap_=ap_, bf_=bf_, sq_=sq_: P.op(
                "scalar", lambda s: s.square(out=sq_.v(cs=slice(0, W5)), in_=ap_),
                reads=[bf_], writes=[sq_.buf]))
            th.append(lambda sq_=sq_, f=f: P.op(
                "tensor", lambda t: t.matmul(
                    psB.v(rs=slice(32, 33), cs=slice(0, W5)), C.onesk.v(),
                    sq_.v(cs=slice(0, W5)), start=(f == 0), stop=(f == 7)),
                reads=[sq_.buf, C.onesk.buf], writes=[psB.buf], excl=False))
        th.append(lambda: P.op("vector", lambda v: v.tensor_single_scalar(
            out=srow.v(rs=slice(0, 1)), in_=psB.v(rs=slice(0, 1), cs=slice(0, W5)),
            scalar=1.0 / 1024.0, op=ALU.mult),
            reads=[psB.buf], writes=[srow.buf]))
        th.append(lambda: P.op("vector", lambda v: v.tensor_single_scalar(
            out=sqrow.v(rs=slice(0, 1)), in_=psB.v(rs=slice(32, 33), cs=slice(0, W5)),
            scalar=1.0 / 1024.0, op=ALU.mult),
            reads=[psB.buf], writes=[sqrow.buf]))
        th.append(lambda: P.op("vector", lambda v: v.tensor_tensor(
            out=t1f.v(rs=slice(0, 1)), in0=srow.v(rs=slice(0, 1)),
            in1=srow.v(rs=slice(0, 1)), op=ALU.mult),
            reads=[srow.buf], writes=[t1f.buf]))
        th.append(lambda: P.op("vector", lambda v: v.tensor_tensor(
            out=sqrow.v(rs=slice(0, 1)), in0=sqrow.v(rs=slice(0, 1)),
            in1=t1f.v(rs=slice(0, 1)), op=ALU.subtract),
            reads=[sqrow.buf, t1f.buf], writes=[sqrow.buf]))
        th.append(lambda: P.op("vector", lambda v: v.tensor_single_scalar(
            out=sqrow.v(rs=slice(0, 1)), in_=sqrow.v(rs=slice(0, 1)),
            scalar=1e-5, op=ALU.add), reads=[sqrow.buf], writes=[sqrow.buf]))
        th.append(lambda: P.op("scalar", lambda s: s.activation(
            out=sqrow.v(rs=slice(0, 1)), in_=sqrow.v(rs=slice(0, 1)),
            func=AF.Sqrt),
            reads=[sqrow.buf], writes=[sqrow.buf]))
        th.append(lambda: P.op("vector", lambda v: v.reciprocal(
            out=sqrow.v(rs=slice(0, 1)), in_=sqrow.v(rs=slice(0, 1))),
            reads=[sqrow.buf], writes=[sqrow.buf]))
        th.append(lambda: P.op("tensor", lambda t: t.matmul(
            psA.v(cs=slice(0, W5)), C.onesmf.v(rs=slice(0, 1)),
            srow.v(rs=slice(0, 1)), start=True, stop=True),
            reads=[C.onesmf.buf, srow.buf], writes=[psA.buf]))
        th.append(lambda: P.op("tensor", lambda t: t.matmul(
            psA.v(cs=slice(512, 512 + W5)), C.onesmf.v(rs=slice(0, 1)),
            sqrow.v(rs=slice(0, 1)), start=True, stop=True),
            reads=[C.onesmf.buf, sqrow.buf], writes=[psA.buf], excl=False))
        for f in range(8):
            ap_, bf_ = sh2_ap(f, n0, W5)
            th.append(lambda ap_=ap_, bf_=bf_: P.op(
                "vector", lambda v: v.tensor_tensor(
                    out=t1f.v(cs=slice(0, W5)), in0=ap_,
                    in1=psA.v(cs=slice(0, W5)), op=ALU.subtract),
                reads=[bf_, psA.buf], writes=[t1f.buf]))
            th.append(lambda: P.op("vector", lambda v: v.tensor_tensor(
                out=t1f.v(cs=slice(0, W5)), in0=t1f.v(cs=slice(0, W5)),
                in1=psA.v(cs=slice(512, 512 + W5)), op=ALU.mult),
                reads=[t1f.buf, psA.buf], writes=[t1f.buf]))
            th.append(lambda f=f, xw1n=xw1n: P.op(
                "vector", lambda v: v.tensor_scalar(
                    out=xw1n[f].v(cs=slice(0, W5)), in0=t1f.v(cs=slice(0, W5)),
                    scalar1=lgcp.v(cs=slice(f, f + 1)), scalar2=lbcp.v(cs=slice(f, f + 1)),
                    op0=ALU.mult, op1=ALU.add),
                reads=[t1f.buf, lgcp.buf, lbcp.buf], writes=[xw1n[f].buf]))
        return th

    for t_ in stats_ln_thunks(0):
        t_()
    for n in range(nchunks):
        n0 = n * W5
        xw1n = xw1d[n % 2]
        pending = stats_ln_thunks(n + 1) if n + 1 < nchunks else []
        for jb, (kind, lo, hi) in enumerate(GLU_BLOCKS):
            blk = hi - lo
            wt = w1t[jb % 2]
            if C.w1_cached[l]:
                P.dma("gpsimd", wt.v(), w1c.ap()[jb], reads=[_db(C, f"W1C{l}")],
                      writes=[wt.buf])
            else:
                joff = sum(b2 - b1 for _, b1, b2 in GLU_BLOCKS[:jb])
                src = w1p.ap().rearrange("(k p) m -> p k m", p=128)[:, :, joff:joff + blk]
                P.dma("gpsimd", wt.v().rearrange("p (k m) -> p k m", m=128)[:, :, 0:blk],
                      src, reads=[_db(C, f"w1p{l}")], writes=[wt.buf])
                P.dma("sync", w1c.ap()[jb], wt.v(), reads=[wt.buf],
                      writes=[_db(C, f"W1C{l}")], excl=False)
            bank = 2 * ((jb // 2) % 2) + (0 if kind == "a" else 1)
            for f in range(8):
                P.op("tensor", lambda t, wt=wt, f=f, blk=blk, bank=bank, xw1n=xw1n: t.matmul(
                    psS.v(rs=slice(0, blk), cs=slice(bank * 512, bank * 512 + W5)),
                    wt.v(cs=slice(f * 128, f * 128 + blk)),
                    xw1n[f].v(cs=slice(0, W5)),
                    start=(f == 0), stop=(f == 7)),
                    reads=[wt.buf, xw1n[f].buf], writes=[C.psSb[bank]], excl=(f == 0))
            for _ in range(2):
                if pending:
                    pending.pop(0)()
            if kind == "g":
                jj = jb // 2
                banka, bankg = 2 * (jj % 2), 2 * (jj % 2) + 1
                P.op("scalar", lambda s, blk=blk, bankg=bankg, jb=jb: s.activation(
                    out=gl.v(rs=slice(0, blk), cs=slice(0, W5)),
                    in_=psS.v(rs=slice(0, blk), cs=slice(bankg * 512, bankg * 512 + W5)),
                    func=AF.Gelu, bias=b1cp.v(rs=slice(0, blk), cs=slice(jb, jb + 1))),
                    reads=[C.psSb[bankg], b1cp.buf], writes=[gl.buf])
                P.op("vector", lambda v, blk=blk, banka=banka, jj=jj, jb=jb: v.scalar_tensor_tensor(
                    out=hn[jj].v(rs=slice(0, blk), cs=slice(0, W5)),
                    in0=psS.v(rs=slice(0, blk), cs=slice(banka * 512, banka * 512 + W5)),
                    scalar=b1cp.v(rs=slice(0, blk), cs=slice(jb - 1, jb)),
                    in1=gl.v(rs=slice(0, blk), cs=slice(0, W5)),
                    op0=ALU.add, op1=ALU.mult),
                    reads=[C.psSb[banka], b1cp.buf, gl.buf], writes=[hn[jj].buf])
        for m in range(8):
            wb = m % 4
            for k2, (r0, rw) in enumerate(H_CHUNKS):
                P.op("tensor", lambda t, m=m, k2=k2, rw=rw, wb=wb: t.matmul(
                    psS.v(cs=slice(wb * 512, wb * 512 + W5)),
                    w2res[k2].v(rs=slice(0, rw), cs=slice(m * 128, m * 128 + 128)),
                    hn[k2].v(rs=slice(0, rw), cs=slice(0, W5)),
                    start=(k2 == 0), stop=(k2 == len(H_CHUNKS) - 1)),
                    reads=[w2res[k2].buf, hn[k2].buf], writes=[C.psSb[wb]],
                    excl=(k2 == 0))
            P.dma("gpsimd", yin2.v(cs=slice(0, W5)),
                  yd.ap()[sq.y_idx, m, :, n0 + 1:n0 + 1 + W5],
                  reads=[ydb], writes=[yin2.buf])
            P.op("vector", lambda v, m=m, wb=wb: v.scalar_tensor_tensor(
                out=w2out.v(cs=slice(0, W5)), in0=psS.v(cs=slice(wb * 512, wb * 512 + W5)),
                scalar=b2cp.v(cs=slice(m, m + 1)), in1=yin2.v(cs=slice(0, W5)),
                op0=ALU.add, op1=ALU.add),
                reads=[C.psSb[wb], b2cp.buf, yin2.buf], writes=[w2out.buf])
            P.dma("sync", yd.ap()[sq.y_idx, m, :, n0 + 1:n0 + 1 + W5],
                  w2out.v(cs=slice(0, W5)), reads=[w2out.buf], writes=[ydb], excl=False)
            for _ in range(2):
                if pending:
                    pending.pop(0)()
        while pending:
            pending.pop(0)()
    C.w1_cached[l] = True
    ba.free_tv(ybm, xw1d, hn, sqt, w1t, gl, w2res)
    if ysh2 is not None:
        ba.free_tv(ysh2)
    fa.free_tv(srow, sqrow, t1f, w2out, yin2, b1cp, b2cp, lgcp, lbcp)
    fa.free_tv(st16)


def emit_x1(C, sq):
    P, ba, fa = C.P, C.ba, C.fa
    yd = C.dram[sq.y_name]
    ytmp = fa.alloc(1024, "x1y")
    xin = fa.alloc(1024, "x1x")
    aoin = fa.alloc(1024, "x1ao")
    x1st = fa.alloc(1024, "x1st")
    for tc in range(sq.nt):
        t0 = tc * 128
        for f in range(8):
            P.dma("gpsimd", ytmp.v(cs=slice(f * 128, f * 128 + 128)),
                  yd.ap()[sq.y_idx, f, :, t0 + 1:t0 + 129],
                  reads=[_db(C, sq.y_name)], writes=[ytmp.buf], excl=(f == 0))
        P.dma("sync", xin.v(), C.dram[sq.x_src].ap()[sq.x_row0 + t0:sq.x_row0 + t0 + 128, :],
              reads=[_db(C, sq.x_src)], writes=[xin.buf])
        P.dma("sync", aoin.v(), C.dram[sq.ao_name].ap()[sq.ao_row0 + t0:sq.ao_row0 + t0 + 128, :],
              reads=[_db(C, sq.ao_name)], writes=[aoin.buf])
        for f in range(8):
            P.op("tensor", lambda t, f=f: t.transpose(
                C.psA.v(cs=slice(f * 128, f * 128 + 128)),
                ytmp.v(cs=slice(f * 128, f * 128 + 128)), C.identf.v()),
                reads=[ytmp.buf, C.identf.buf], writes=[C.psA.buf], excl=(f == 0))
        P.op("vector", lambda v: v.tensor_tensor(
            out=x1st.v(), in0=C.psA.v(), in1=xin.v(), op=ALU.add),
            reads=[C.psA.buf, xin.buf], writes=[x1st.buf])
        P.op("vector", lambda v: v.tensor_tensor(
            out=x1st.v(), in0=x1st.v(), in1=aoin.v(), op=ALU.add),
            reads=[x1st.buf, aoin.buf], writes=[x1st.buf], excl=False)
        P.dma("sync", C.dram[sq.x1_name].ap()[sq.ao_row0 + t0:sq.ao_row0 + t0 + 128, :],
              x1st.v(), reads=[x1st.buf], writes=[_db(C, sq.x1_name)], excl=False)
    fa.free_tv(ytmp, xin, aoin, x1st)


def _tm_ln(C, st, src_v, src_bufs, width, eps, junk, out_v, out_bufs,
           out_excl=True):
    P = C.P
    P.op("vector", lambda v: v.reduce_sum(out=st.v(cs=slice(0, 1)), in_=src_v,
         axis=AX.X), reads=src_bufs, writes=[st.buf], excl=False)
    P.op("scalar", lambda s: s.activation(out=junk.v(cs=slice(0, width)), in_=src_v,
         func=AF.Square, accum_out=st.v(cs=slice(1, 2))),
         reads=src_bufs, writes=[st.buf, junk.buf], excl=False)
    ln_stats(C, st, 0, 1, width, eps, 2, 3)
    P.op("vector", lambda v: v.tensor_scalar(
        out=out_v, in0=src_v, scalar1=st.v(cs=slice(2, 3)),
        scalar2=st.v(cs=slice(3, 4)), op0=ALU.subtract, op1=ALU.mult),
        reads=src_bufs + [st.buf], writes=out_bufs, excl=out_excl)


def emit_prefix_l1(C):
    P, ba, fa = C.P, C.ba, C.fa
    psA, psB, psT = C.psA, C.psB, C.psT
    st = fa.alloc(32, "p1st")
    junk = ba.alloc(1024, "p1junk")
    x1t = fa.alloc(1024, "p1x")
    xn1 = [fa.alloc(1024, "p1xn") for _ in range(2)]
    za = ba.alloc_list(2, 1024, "p1za")
    zb = ba.alloc_list(2, 1024, "p1zb")
    for tt in range(2):
        P.dma("gpsimd", x1t.v(), C.dram["PX1"].ap()[tt * 128:tt * 128 + 128, :],
              reads=[_db(C, "PX1")], writes=[x1t.buf])
        _tm_ln(C, st, x1t.v(), [x1t.buf], 1024, 1e-5, junk,
               xn1[tt].v(), [xn1[tt].buf])
    lna = load_const(C, "lna_rb", 1024, "fa")
    lnb = load_const(C, "lnb_rb", 1024, "fa")
    tmpf = fa.alloc(1024, "p1tmp")
    for tt in range(2):
        _tm_ln(C, st, xn1[tt].v(), [xn1[tt].buf], 1024, 1e-5, junk,
               tmpf.v(), [tmpf.buf])
        P.op("vector", lambda v, tt=tt: v.tensor_tensor(
            out=za[tt].v(), in0=tmpf.v(), in1=lna.v(), op=ALU.mult),
            reads=[tmpf.buf, lna.buf], writes=[za[tt].buf])
        P.op("vector", lambda v, tt=tt: v.tensor_tensor(
            out=zb[tt].v(), in0=tmpf.v(), in1=lnb.v(), op=ALU.mult),
            reads=[tmpf.buf, lnb.buf], writes=[zb[tt].buf])
    fa.free_tv(lna, lnb, x1t)
    fa.free_tv(xn1)

    zaf = ba.alloc_list(8, 256, "p1zaf")
    zbf = ba.alloc_list(8, 256, "p1zbf")
    for src, fm in ((za, zaf), (zb, zbf)):
        for tt in range(2):
            for c in range(8):
                P.op("tensor", lambda t, src=src, tt=tt, c=c: t.transpose(
                    psT.v(cs=slice(c * 128, c * 128 + 128)),
                    src[tt].v(cs=slice(c * 128, c * 128 + 128)), C.ident.v()),
                    reads=[src[tt].buf, C.ident.buf], writes=[psT.buf], excl=(c == 0))
            for c in range(8):
                P.op("scalar", lambda s, fm=fm, tt=tt, c=c: s.copy(
                    out=fm[c].v(cs=slice(tt * 128, tt * 128 + 128)),
                    in_=psT.v(cs=slice(c * 128, c * 128 + 128))),
                    reads=[psT.buf], writes=[fm[c].buf], excl=False)
    ba.free_tv(za, zb)

    query = [fa.alloc(1024, "p1q") for _ in range(2)]
    kb16 = ba.alloc_list(2, 1024, "p1k")
    vb16 = ba.alloc_list(2, 1024, "p1v")
    for wname, brch, out_tiles, bias in (
            ("qwT", zaf, query, "qb_rb"), ("kwT", zbf, kb16, None),
            ("vwT", zbf, vb16, "vb_rb")):
        wt = loadw(C, wname, 1024, 1024, name=wname)
        btv = load_const(C, bias, 1024, "fa") if bias else None
        for tt in range(2):
            for half in range(2):
                for f in range(8):
                    P.op("tensor", lambda t, brch=brch, f=f, tt=tt, half=half, wt=wt: t.matmul(
                        psA.v(cs=slice(half * 512, half * 512 + 512)),
                        brch[f].v(cs=slice(tt * 128, tt * 128 + 128)),
                        wt[f].v(cs=slice(half * 512, half * 512 + 512)),
                        start=(f == 0), stop=(f == 7)),
                        reads=[brch[f].buf, wt[f].buf], writes=[psA.buf],
                        excl=(f == 0 and half == 0))
            ot = out_tiles[tt]
            if btv is not None:
                P.op("vector", lambda v, ot=ot, btv=btv: v.scalar_tensor_tensor(
                    out=ot.v(), in0=psA.v(), scalar=1.0, in1=btv.v(),
                    op0=ALU.mult, op1=ALU.add),
                    reads=[psA.buf, btv.buf], writes=[ot.buf])
            else:
                P.op("vector", lambda v, ot=ot: v.tensor_copy(out=ot.v(), in_=psA.v()),
                     reads=[psA.buf], writes=[ot.buf])
        ba.free_tv(wt)
        if btv is not None:
            fa.free_tv(btv)
    ba.free_tv(zaf, zbf)

    lkw = load_const(C, "lkwT", 64, rows=64)
    lvw = load_const(C, "lvwT", 64, rows=64)
    lqw = load_const(C, "lqwT", 64, rows=64)
    lkb = load_const(C, "lkb_rb", 64, "fa")
    lvb = load_const(C, "lvb_rb", 64, "fa")
    lqb = load_const(C, "lqb_rb", 64, "fa")
    lnc = load_const(C, "lnc_rb", 64, "fa")
    lnd = load_const(C, "lnd_rb", 64, "fa")
    kn = ba.alloc_list(2, 1024, "p1kn")
    vi = ba.alloc_list(2, 1024, "p1vi")
    kpf = fa.alloc(64, "p1kpf")
    htf = ba.alloc(128, "p1htf")

    def head_proj(src_tile, h, wtv, btv, out_ap, out_bufs, ln_gamma):
        P.op("tensor", lambda t, src_tile=src_tile, h=h: t.transpose(
            psT.v(rs=slice(0, 64), cs=slice(0, 128)),
            src_tile.v(cs=slice(h * 64, h * 64 + 64)), C.ident.v()),
            reads=[src_tile.buf, C.ident.buf], writes=[psT.buf])
        P.op("scalar", lambda s: s.copy(out=htf.v(rs=slice(0, 64), cs=slice(0, 128)),
             in_=psT.v(rs=slice(0, 64), cs=slice(0, 128))),
             reads=[psT.buf], writes=[htf.buf])
        P.op("tensor", lambda t, wtv=wtv: t.matmul(
            psB.v(cs=slice(0, 64)), htf.v(rs=slice(0, 64), cs=slice(0, 128)),
            wtv.v(rs=slice(0, 64)), start=True, stop=True),
            reads=[htf.buf, wtv.buf], writes=[psB.buf])
        P.op("vector", lambda v, btv=btv: v.scalar_tensor_tensor(
            out=kpf.v(cs=slice(0, 64)), in0=psB.v(cs=slice(0, 64)), scalar=1.0,
            in1=btv.v(), op0=ALU.mult, op1=ALU.add),
            reads=[psB.buf, btv.buf], writes=[kpf.buf])
        if ln_gamma is None:
            P.op("vector", lambda v: v.tensor_copy(out=out_ap, in_=kpf.v(cs=slice(0, 64))),
                 reads=[kpf.buf], writes=out_bufs, excl=False)
        else:
            _tm_ln(C, st, kpf.v(cs=slice(0, 64)), [kpf.buf], 64, 1e-5, junk,
                   tmpf.v(cs=slice(0, 64)), [tmpf.buf])
            P.op("vector", lambda v, ln_gamma=ln_gamma: v.tensor_tensor(
                out=out_ap, in0=tmpf.v(cs=slice(0, 64)), in1=ln_gamma.v(),
                op=ALU.mult), reads=[tmpf.buf, ln_gamma.buf], writes=out_bufs,
                excl=False)

    for tt in range(2):
        for h in range(16):
            head_proj(kb16[tt], h, lkw, lkb,
                      kn[tt].v(cs=slice(h * 64, h * 64 + 64)), [kn[tt].buf], lnd)
            head_proj(vb16[tt], h, lvw, lvb,
                      vi[tt].v(cs=slice(h * 64, h * 64 + 64)), [vi[tt].buf], None)
    ba.free_tv(kb16, vb16)
    knf = ba.alloc_list(16, 256, "p1knf")
    for h in range(16):
        for tt in range(2):
            P.op("tensor", lambda t, h=h, tt=tt: t.transpose(
                psT.v(rs=slice(0, 64), cs=slice(tt * 128, tt * 128 + 128)),
                kn[tt].v(cs=slice(h * 64, h * 64 + 64)), C.ident.v()),
                reads=[kn[tt].buf, C.ident.buf], writes=[psT.buf], excl=(tt == 0))
        P.op("scalar", lambda s, h=h: s.copy(
            out=knf[h].v(rs=slice(0, 64)), in_=psT.v(rs=slice(0, 64), cs=slice(0, 256))),
            reads=[psT.buf], writes=[knf[h].buf])
    ba.free_tv(kn)

    qn = ba.alloc(1024, "p1qn")
    qnf = ba.alloc(128, "p1qnf")
    Pp = ba.alloc(256, "p1Pp")
    PTs = ba.alloc(256, "p1PTs")
    itcur = [fa.alloc(1024, "p1it") for _ in range(2)]
    itsv = [fa.alloc(1024, "p1isv") for _ in range(2)]
    cb = fa.alloc(16, "p1cb")
    dcol = fa.alloc(16, "p1dcol")
    dbf = ba.alloc(16, "p1dbf")
    mpf = C.mPF
    qbb = ba.alloc_list(2, 1024, "p1qbb")
    for it in range(3):
        for tt in range(2):
            P.op("vector", lambda v, tt=tt: v.tensor_copy(
                out=qbb[tt].v(), in_=query[tt].v()),
                reads=[query[tt].buf], writes=[qbb[tt].buf])
            for h in range(16):
                head_proj(qbb[tt], h, lqw, lqb,
                          qn.v(cs=slice(h * 64, h * 64 + 64)), [qn.buf], lnc)
            for h in range(16):
                P.op("tensor", lambda t, h=h: t.transpose(
                    psT.v(rs=slice(0, 64), cs=slice(0, 128)),
                    qn.v(cs=slice(h * 64, h * 64 + 64)),
                    C.ident.v()),
                    reads=[qn.buf, C.ident.buf], writes=[psT.buf])
                P.op("scalar", lambda s: s.copy(
                    out=qnf.v(rs=slice(0, 64)), in_=psT.v(rs=slice(0, 64), cs=slice(0, 128))),
                    reads=[psT.buf], writes=[qnf.buf])
                P.op("tensor", lambda t, h=h, tt=tt: t.matmul(
                    psB.v(cs=slice(0, 128)), qnf.v(rs=slice(0, 64), cs=slice(0, 128)),
                    knf[h].v(rs=slice(0, 64), cs=slice(tt * 128, tt * 128 + 128)),
                    start=True, stop=True),
                    reads=[qnf.buf, knf[h].buf], writes=[psB.buf])
                P.op("vector", lambda v: v.tensor_tensor(
                    out=psB.v(cs=slice(0, 128)),
                    in0=psB.v(cs=slice(0, 128)),
                    in1=mpf.v(), op=ALU.add),
                    reads=[psB.buf, mpf.buf], writes=[psB.buf], excl=False)
                P.op("vector", lambda v: v.reduce_max(
                    out=st.v(cs=slice(8, 9)), in_=psB.v(cs=slice(0, 128)), axis=AX.X),
                    reads=[psB.buf], writes=[st.buf], excl=False)
                P.op("vector", lambda v: v.tensor_single_scalar(
                    out=st.v(cs=slice(9, 10)), in_=st.v(cs=slice(8, 9)),
                    scalar=-float(SCALE_L1), op=ALU.mult),
                    reads=[st.buf], writes=[st.buf], excl=False)
                P.op("scalar", lambda s: s.activation(
                    out=Pp.v(cs=slice(0, 128)), in_=psB.v(cs=slice(0, 128)),
                    func=AF.Exp, bias=st.v(cs=slice(9, 10)), scale=float(SCALE_L1),
                    accum_out=st.v(cs=slice(10, 11))),
                    reads=[psB.buf, st.buf], writes=[Pp.buf, st.buf])
                P.op("vector", lambda v: v.reciprocal(
                    out=st.v(cs=slice(11, 12)), in_=st.v(cs=slice(10, 11))),
                    reads=[st.buf], writes=[st.buf], excl=False)
                P.op("tensor", lambda t: t.transpose(
                    psT.v(cs=slice(0, 128)),
                    Pp.v(cs=slice(0, 128)), C.ident.v()),
                    reads=[Pp.buf, C.ident.buf], writes=[psT.buf])
                P.op("scalar", lambda s: s.copy(
                    out=PTs.v(cs=slice(0, 128)), in_=psT.v(cs=slice(0, 128))),
                    reads=[psT.buf], writes=[PTs.buf])
                P.op("tensor", lambda t, tt=tt, h=h: t.matmul(
                    psB.v(cs=slice(256, 320)),
                    PTs.v(cs=slice(0, 128)),
                    vi[tt].v(cs=slice(h * 64, h * 64 + 64)),
                    start=True, stop=True),
                    reads=[PTs.buf, vi[tt].buf], writes=[psB.buf])
                P.op("vector", lambda v, tt=tt, h=h: v.tensor_single_scalar(
                    out=itcur[tt].v(cs=slice(h * 64, h * 64 + 64)),
                    in_=psB.v(cs=slice(256, 320)), scalar=st.v(cs=slice(11, 12)),
                    op=ALU.mult),
                    reads=[psB.buf, st.buf], writes=[itcur[tt].buf], excl=False)
        if it == 0:
            for tt in range(2):
                P.op("vector", lambda v, tt=tt: v.tensor_copy(
                    out=itsv[tt].v(), in_=itcur[tt].v()),
                    reads=[itcur[tt].buf], writes=[itsv[tt].buf])
                P.op("vector", lambda v, tt=tt: v.tensor_tensor(
                    out=query[tt].v(), in0=query[tt].v(), in1=itcur[tt].v(),
                    op=ALU.add),
                    reads=[query[tt].buf, itcur[tt].buf], writes=[query[tt].buf],
                    excl=False)
        elif it == 1:
            for tt in range(2):
                P.op("vector", lambda v, tt=tt: v.tensor_tensor(
                    out=itsv[tt].v(), in0=itcur[tt].v(), in1=itsv[tt].v(),
                    op=ALU.subtract),
                    reads=[itcur[tt].buf, itsv[tt].buf], writes=[itsv[tt].buf])
                P.op("vector", lambda v, tt=tt: v.tensor_reduce(
                    out=dcol.v(cs=slice(tt, tt + 1)), in_=itsv[tt].v(),
                    axis=AX.X, op=ALU.add, apply_absolute_value=True),
                    reads=[itsv[tt].buf], writes=[dcol.buf], excl=False)
            P.op("vector", lambda v: v.tensor_tensor(
                out=dcol.v(cs=slice(2, 3)), in0=dcol.v(cs=slice(0, 1)),
                in1=dcol.v(cs=slice(1, 2)), op=ALU.add),
                reads=[dcol.buf], writes=[dcol.buf], excl=False)
            P.op("vector", lambda v: v.tensor_copy(
                out=dbf.v(cs=slice(0, 1)), in_=dcol.v(cs=slice(2, 3))),
                reads=[dcol.buf], writes=[dbf.buf])
            P.op("tensor", lambda t: t.matmul(
                psB.v(rs=slice(0, 1), cs=slice(0, 1)), C.onesk.v(),
                dbf.v(cs=slice(0, 1)), start=True, stop=True),
                reads=[dbf.buf, C.onesk.buf], writes=[psB.buf])
            NTOT = float(B_ALL * 16 * T * 64)
            P.op("vector", lambda v: v.tensor_single_scalar(
                out=dcol.v(rs=slice(0, 1), cs=slice(4, 5)),
                in_=psB.v(rs=slice(0, 1), cs=slice(0, 1)),
                scalar=0.9 / NTOT, op=ALU.mult),
                reads=[psB.buf], writes=[dcol.buf])
            P.op("vector", lambda v: v.tensor_single_scalar(
                out=dcol.v(rs=slice(0, 1), cs=slice(5, 6)),
                in_=dcol.v(rs=slice(0, 1), cs=slice(4, 5)),
                scalar=0.01, op=ALU.is_lt),
                reads=[dcol.buf], writes=[dcol.buf], excl=False)
            P.op("tensor", lambda t: t.matmul(
                psB.v(cs=slice(16, 17)), C.onesmf.v(rs=slice(0, 1)),
                dcol.v(rs=slice(0, 1), cs=slice(5, 6)), start=True, stop=True),
                reads=[dcol.buf, C.onesmf.buf], writes=[psB.buf])
            P.op("vector", lambda v: v.tensor_copy(
                out=cb.v(cs=slice(0, 1)), in_=psB.v(cs=slice(16, 17))),
                reads=[psB.buf], writes=[cb.buf])
            P.op("vector", lambda v: v.tensor_scalar(
                out=cb.v(cs=slice(1, 2)), in0=cb.v(cs=slice(0, 1)),
                scalar1=-1.0, scalar2=1.0, op0=ALU.mult, op1=ALU.add),
                reads=[cb.buf], writes=[cb.buf], excl=False)
            for tt in range(2):
                P.op("vector", lambda v, tt=tt: v.tensor_copy(
                    out=itsv[tt].v(), in_=itcur[tt].v()),
                    reads=[itcur[tt].buf], writes=[itsv[tt].buf])
                P.op("vector", lambda v, tt=tt: v.scalar_tensor_tensor(
                    out=query[tt].v(), in0=itcur[tt].v(), scalar=cb.v(cs=slice(1, 2)),
                    in1=query[tt].v(), op0=ALU.mult, op1=ALU.add),
                    reads=[itcur[tt].buf, cb.buf, query[tt].buf],
                    writes=[query[tt].buf])
    fa.free_tv(query)
    ba.free_tv(qbb)
    fin = [fa.alloc(1024, "p1fin") for _ in range(2)]
    for tt in range(2):
        P.op("vector", lambda v, tt=tt: v.tensor_single_scalar(
            out=fin[tt].v(), in_=itcur[tt].v(), scalar=cb.v(cs=slice(1, 2)),
            op=ALU.mult), reads=[itcur[tt].buf, cb.buf], writes=[fin[tt].buf])
        P.op("vector", lambda v, tt=tt: v.scalar_tensor_tensor(
            out=fin[tt].v(), in0=itsv[tt].v(), scalar=cb.v(cs=slice(0, 1)),
            in1=fin[tt].v(), op0=ALU.mult, op1=ALU.add),
            reads=[itsv[tt].buf, cb.buf, fin[tt].buf], writes=[fin[tt].buf])
    fa.free_tv(itcur, itsv)
    ba.free_tv(qn, qnf, Pp, PTs, knf, vi)
    finb = ba.alloc(1024, "p1finb")
    P.op("vector", lambda v: v.tensor_copy(out=finb.v(), in_=fin[0].v()),
         reads=[fin[0].buf], writes=[finb.buf])
    ff = ba.alloc_list(8, 128, "p1ff")
    for c in range(8):
        P.op("tensor", lambda t, c=c: t.transpose(
            psT.v(cs=slice(c * 128, c * 128 + 128)),
            finb.v(cs=slice(c * 128, c * 128 + 128)), C.ident.v()),
            reads=[finb.buf, C.ident.buf], writes=[psT.buf], excl=(c == 0))
    for c in range(8):
        P.op("scalar", lambda s, c=c: s.copy(
            out=ff[c].v(), in_=psT.v(cs=slice(c * 128, c * 128 + 128))),
            reads=[psT.buf], writes=[ff[c].buf], excl=False)
    owt = loadw(C, "owT", 1024, 1024, name="owT")
    o16 = fa.alloc(1024, "o16")
    for half in range(2):
        for c in range(8):
            P.op("tensor", lambda t, c=c, half=half: t.matmul(
                psA.v(cs=slice(half * 512, half * 512 + 512)),
                ff[c].v(), owt[c].v(cs=slice(half * 512, half * 512 + 512)),
                start=(c == 0), stop=(c == 7)),
                reads=[ff[c].buf, owt[c].buf], writes=[psA.buf],
                excl=(c == 0 and half == 0))
    P.op("vector", lambda v: v.tensor_copy(out=o16.v(), in_=psA.v()),
         reads=[psA.buf], writes=[o16.buf])
    o16b = fa.alloc(1024, "o16b")
    P.op("vector", lambda v: v.tensor_copy(
        out=o16b.v(rs=slice(0, 16)), in_=o16.v(rs=slice(32, 48))),
        reads=[o16.buf], writes=[o16b.buf])
    C.o16b = o16b
    ba.free_tv(owt, ff, finb, junk, dbf, htf, lkw, lvw, lqw)
    fa.free_tv(fin, cb, dcol, kpf, tmpf, st, lkb, lvb, lqb, lnc, lnd)
    C.o16 = o16


def emit_wquant(C, src_name, K, M):
    P, ba, fa = C.P, C.ba, C.fa
    nk = (K + 127) // 128
    wq = ba.alloc_list(nk, M, "wq_" + src_name)
    st = fa.alloc(16, "wqst")
    acc = fa.alloc(16, "wqacc")
    tmps = [fa.alloc(1024, "wqtmp") for _ in range(2)]
    d = C.dram[src_name]
    P.op("vector", lambda v: v.memset(acc.v(cs=slice(0, 1)), 0.0), writes=[acc.buf])
    si = 0
    for k in range(nk):
        for m0 in range(0, M, 1024):
            w = min(1024, M - m0)
            tmp = tmps[si % 2]
            sc_ = 4 + si % 2
            si += 1
            P.dma("sync", tmp.v(cs=slice(0, w)), d.ap()[k * 128:(k + 1) * 128, m0:m0 + w],
                  reads=[_db(C, src_name)], writes=[tmp.buf])
            P.op("vector", lambda v, w=w, tmp=tmp, sc_=sc_: v.tensor_reduce(
                out=st.v(cs=slice(sc_, sc_ + 1)), in_=tmp.v(cs=slice(0, w)), axis=AX.X,
                op=ALU.add, apply_absolute_value=True),
                reads=[tmp.buf], writes=[st.buf])
            P.op("vector", lambda v, sc_=sc_: v.tensor_tensor(
                out=acc.v(cs=slice(0, 1)), in0=acc.v(cs=slice(0, 1)),
                in1=st.v(cs=slice(sc_, sc_ + 1)), op=ALU.add),
                reads=[st.buf, acc.buf], writes=[acc.buf])
    P.op("tensor", lambda t: t.transpose(
        C.psA.v(rs=slice(0, 1), cs=slice(0, 128)), acc.v(cs=slice(0, 1)),
        C.identf.v()),
        reads=[acc.buf, C.identf.buf], writes=[C.psA.buf])
    for i in range(4):
        P.op("vector", lambda v, i=i: v.reduce_sum(
            out=st.v(rs=slice(0, 1), cs=slice(4 + i, 5 + i)),
            in_=C.psA.v(rs=slice(0, 1), cs=slice(32 * i, 32 * i + 32)), axis=AX.X),
            reads=[C.psA.buf], writes=[st.buf], excl=(i == 0))
    P.op("vector", lambda v: v.reduce_sum(
        out=st.v(rs=slice(0, 1), cs=slice(0, 1)),
        in_=st.v(rs=slice(0, 1), cs=slice(4, 8)), axis=AX.X),
        reads=[st.buf], writes=[st.buf], excl=False)
    P.op("vector", lambda v: v.tensor_single_scalar(
        out=st.v(rs=slice(0, 1), cs=slice(1, 2)),
        in_=st.v(rs=slice(0, 1), cs=slice(0, 1)),
        scalar=1.0 / float(K * M), op=ALU.mult),
        reads=[st.buf], writes=[st.buf], excl=False)
    P.op("vector", lambda v: v.tensor_single_scalar(
        out=st.v(rs=slice(0, 1), cs=slice(2, 3)), in_=st.v(rs=slice(0, 1), cs=slice(1, 2)),
        scalar=1e-5, op=ALU.max), reads=[st.buf], writes=[st.buf], excl=False)
    P.op("vector", lambda v: v.reciprocal(
        out=st.v(rs=slice(0, 1), cs=slice(3, 4)), in_=st.v(rs=slice(0, 1), cs=slice(2, 3))),
        reads=[st.buf], writes=[st.buf], excl=False)
    swb = fa.alloc(16, "swb")
    for i, srcc in enumerate((3, 2)):
        P.op("tensor", lambda t, i=i, srcc=srcc: t.matmul(
            C.psB.v(cs=slice(16 + i, 17 + i)), C.onesmf.v(rs=slice(0, 1)),
            st.v(rs=slice(0, 1), cs=slice(srcc, srcc + 1)), start=True, stop=True),
            reads=[st.buf, C.onesmf.buf], writes=[C.psB.buf], excl=(i == 0))
    P.op("vector", lambda v: v.tensor_copy(
        out=swb.v(cs=slice(0, 2)), in_=C.psB.v(cs=slice(16, 18))),
        reads=[C.psB.buf], writes=[swb.buf])
    si = 0
    for k in range(nk):
        r = min(128, K - k * 128)
        for m0 in range(0, M, 1024):
            w = min(1024, M - m0)
            tmp = tmps[si % 2]
            si += 1
            P.dma("sync", tmp.v(rs=slice(0, r), cs=slice(0, w)),
                  d.ap()[k * 128:k * 128 + r, m0:m0 + w],
                  reads=[_db(C, src_name)], writes=[tmp.buf])
            P.op("vector", lambda v, w=w, tmp=tmp: v.tensor_scalar(
                out=tmp.v(cs=slice(0, w)), in0=tmp.v(cs=slice(0, w)),
                scalar1=swb.v(cs=slice(0, 1)), scalar2=float(RND_C),
                op0=ALU.mult, op1=ALU.add),
                reads=[tmp.buf, swb.buf], writes=[tmp.buf], excl=False)
            P.op("vector", lambda v, w=w, tmp=tmp: v.tensor_scalar(
                out=tmp.v(cs=slice(0, w)), in0=tmp.v(cs=slice(0, w)),
                scalar1=float(RND_C), scalar2=-1.0,
                op0=ALU.subtract, op1=ALU.max),
                reads=[tmp.buf], writes=[tmp.buf], excl=False)
            P.op("vector", lambda v, k=k, m0=m0, w=w, tmp=tmp: v.tensor_scalar(
                out=wq[k].v(cs=slice(m0, m0 + w)), in0=tmp.v(cs=slice(0, w)),
                scalar1=1.0, scalar2=None, op0=ALU.min),
                reads=[tmp.buf], writes=[wq[k].buf])
    fa.free_tv(st, acc, tmps)
    return wq, swb


def emit_l1_main(C):
    P, ba, fa = C.P, C.ba, C.fa
    psS, psA, psB, psT = C.psS, C.psA, C.psB, C.psT
    nt = RMAIN // 128

    w1q, sw1b = emit_wquant(C, "w1fT", 1024, 4096)
    ln1g = load_const(C, "ln1g_rb", 1024)
    ln1b = load_const(C, "ln1b_rb", 1024)
    b1f = load_const(C, "b1f_rb", 4096)
    ln2g = load_const(C, "ln2g_rb", 4096)
    ln2b = load_const(C, "ln2b_rb", 4096)
    junk = ba.alloc(4096, "l1junk")
    h1 = fa.alloc(4096, "l1h1")
    x1c = fa.alloc(1024, "l1x")
    xn1c = fa.alloc(1024, "l1xn")
    hc = fa.alloc(1024, "l1h")
    st = fa.alloc(32, "l1st")
    st2 = fa.alloc(32, "l1st2")
    xqb = ba.alloc(1024, "l1xq")
    xqfd = [ba.alloc(1024, "l1xqf") for _ in range(2)]
    xq2b = ba.alloc(4096, "l1xq2")
    scales2 = fa.alloc(48, "l1sc2")
    gacc = fa.alloc(16, "l1gacc")
    xq2d = C.dram["XQ2"]

    def frontend(tc):
        xqf = xqfd[tc % 2]
        P.dma("gpsimd", x1c.v(), C.dram["X1"].ap()[tc * 128:(tc + 1) * 128, :],
              reads=[_db(C, "X1")], writes=[x1c.buf])
        _tm_ln(C, st2, x1c.v(), [x1c.buf], 1024, 1e-5, junk, xn1c.v(), [xn1c.buf])
        _tm_ln(C, st2, xn1c.v(), [xn1c.buf], 1024, 1e-5, junk, hc.v(), [hc.buf])
        P.op("vector", lambda v: v.tensor_tensor(
            out=hc.v(), in0=hc.v(), in1=ln1g.v(), op=ALU.mult),
            reads=[hc.buf, ln1g.buf], writes=[hc.buf], excl=False)
        P.op("vector", lambda v: v.tensor_tensor(
            out=hc.v(), in0=hc.v(), in1=ln1b.v(), op=ALU.add),
            reads=[hc.buf, ln1b.buf], writes=[hc.buf], excl=False)
        P.op("vector", lambda v: v.tensor_reduce(
            out=st2.v(cs=slice(16, 17)), in_=hc.v(), axis=AX.X, op=ALU.max,
            apply_absolute_value=True), reads=[hc.buf], writes=[st2.buf], excl=False)
        P.op("vector", lambda v: v.tensor_single_scalar(
            out=st2.v(cs=slice(17, 18)), in_=st2.v(cs=slice(16, 17)),
            scalar=1e-5, op=ALU.max), reads=[st2.buf], writes=[st2.buf], excl=False)
        P.op("vector", lambda v: v.reciprocal(
            out=st2.v(cs=slice(18, 19)), in_=st2.v(cs=slice(17, 18))),
            reads=[st2.buf], writes=[st2.buf], excl=False)
        P.op("vector", lambda v: v.tensor_single_scalar(
            out=st2.v(cs=slice(19, 20)), in_=st2.v(cs=slice(18, 19)),
            scalar=127.0, op=ALU.mult), reads=[st2.buf], writes=[st2.buf], excl=False)
        P.op("vector", lambda v: v.tensor_single_scalar(
            out=st2.v(cs=slice(20, 21)), in_=st2.v(cs=slice(17, 18)),
            scalar=1.0 / 127.0, op=ALU.mult), reads=[st2.buf], writes=[st2.buf], excl=False)
        P.op("vector", lambda v: v.tensor_tensor(
            out=st2.v(cs=slice(21, 22)), in0=st2.v(cs=slice(20, 21)),
            in1=sw1b.v(cs=slice(1, 2)), op=ALU.mult),
            reads=[st2.buf, sw1b.buf], writes=[st2.buf], excl=False)
        P.op("vector", lambda v: v.tensor_scalar(
            out=hc.v(), in0=hc.v(), scalar1=st2.v(cs=slice(19, 20)),
            scalar2=float(RND_C), op0=ALU.mult, op1=ALU.add),
            reads=[hc.buf, st2.buf], writes=[hc.buf], excl=False)
        P.op("vector", lambda v: v.tensor_single_scalar(
            out=xqb.v(), in_=hc.v(), scalar=float(RND_C), op=ALU.subtract),
            reads=[hc.buf], writes=[xqb.buf])
        for c in range(8):
            P.op("tensor", lambda t, c=c: t.transpose(
                psT.v(cs=slice(c * 128, c * 128 + 128)),
                xqb.v(cs=slice(c * 128, c * 128 + 128)), C.ident.v()),
                reads=[xqb.buf, C.ident.buf], writes=[psT.buf], excl=(c == 0))
        P.op("scalar", lambda s, xqf=xqf: s.copy(out=xqf.v(), in_=psT.v()),
             reads=[psT.buf], writes=[xqf.buf])

    frontend(0)
    for tc in range(nt):
        xqf = xqfd[tc % 2]
        P.op("vector", lambda v: v.memset(gacc.v(cs=slice(0, 8)), 0.0),
             writes=[gacc.buf])
        for nb in range(8):
            bank = nb % 4
            for f in range(8):
                P.op("tensor", lambda t, nb=nb, f=f, bank=bank, xqf=xqf: t.matmul(
                    psS.v(cs=slice(bank * 512, bank * 512 + 512)),
                    xqf.v(cs=slice(f * 128, f * 128 + 128)),
                    w1q[f].v(cs=slice(nb * 512, nb * 512 + 512)),
                    start=(f == 0), stop=(f == 7)),
                    reads=[xqf.buf, w1q[f].buf], writes=[C.psSb[bank]], excl=(f == 0))
            P.op("vector", lambda v, nb=nb, bank=bank: v.scalar_tensor_tensor(
                out=h1.v(cs=slice(nb * 512, nb * 512 + 512)),
                in0=psS.v(cs=slice(bank * 512, bank * 512 + 512)),
                scalar=st2.v(cs=slice(21, 22)),
                in1=b1f.v(cs=slice(nb * 512, nb * 512 + 512)),
                op0=ALU.mult, op1=ALU.add),
                reads=[C.psSb[bank], st2.buf, b1f.buf], writes=[h1.buf],
                excl=(nb == 0))
            P.op("scalar", lambda s, nb=nb: s.activation(
                out=h1.v(cs=slice(nb * 512, nb * 512 + 512)),
                in_=h1.v(cs=slice(nb * 512, nb * 512 + 512)),
                func=AF.Gelu, accum_out=gacc.v(cs=slice(nb, nb + 1))),
                reads=[h1.buf], writes=[h1.buf, gacc.buf], excl=False)
        if tc + 1 < nt:
            frontend(tc + 1)
        P.op("vector", lambda v: v.reduce_sum(
            out=st.v(cs=slice(0, 1)), in_=gacc.v(cs=slice(0, 8)), axis=AX.X),
            reads=[gacc.buf], writes=[st.buf], excl=False)
        P.op("scalar", lambda s: s.activation(
            out=junk.v(), in_=h1.v(), func=AF.Square, accum_out=st.v(cs=slice(1, 2))),
            reads=[h1.buf], writes=[st.buf, junk.buf], excl=False)
        ln_stats(C, st, 0, 1, 4096, 1e-5, 2, 3)
        P.op("vector", lambda v: v.tensor_scalar(
            out=h1.v(), in0=h1.v(), scalar1=st.v(cs=slice(2, 3)),
            scalar2=st.v(cs=slice(3, 4)), op0=ALU.subtract, op1=ALU.mult),
            reads=[h1.buf, st.buf], writes=[h1.buf], excl=False)
        P.op("vector", lambda v: v.tensor_tensor(
            out=h1.v(), in0=h1.v(), in1=ln2g.v(), op=ALU.mult),
            reads=[h1.buf, ln2g.buf], writes=[h1.buf], excl=False)
        P.op("vector", lambda v: v.tensor_tensor(
            out=h1.v(), in0=h1.v(), in1=ln2b.v(), op=ALU.add),
            reads=[h1.buf, ln2b.buf], writes=[h1.buf], excl=False)
        P.op("vector", lambda v: v.tensor_reduce(
            out=st.v(cs=slice(16, 17)), in_=h1.v(), axis=AX.X, op=ALU.max,
            apply_absolute_value=True), reads=[h1.buf], writes=[st.buf], excl=False)
        P.op("vector", lambda v: v.tensor_single_scalar(
            out=st.v(cs=slice(17, 18)), in_=st.v(cs=slice(16, 17)),
            scalar=1e-5, op=ALU.max), reads=[st.buf], writes=[st.buf], excl=False)
        P.op("vector", lambda v: v.reciprocal(
            out=st.v(cs=slice(18, 19)), in_=st.v(cs=slice(17, 18))),
            reads=[st.buf], writes=[st.buf], excl=False)
        P.op("vector", lambda v: v.tensor_single_scalar(
            out=st.v(cs=slice(19, 20)), in_=st.v(cs=slice(18, 19)),
            scalar=127.0, op=ALU.mult), reads=[st.buf], writes=[st.buf], excl=False)
        P.op("vector", lambda v, tc=tc: v.tensor_single_scalar(
            out=scales2.v(cs=slice(tc, tc + 1)), in_=st.v(cs=slice(17, 18)),
            scalar=1.0 / 127.0, op=ALU.mult),
            reads=[st.buf], writes=[scales2.buf], excl=False)
        P.op("vector", lambda v: v.tensor_scalar(
            out=h1.v(), in0=h1.v(), scalar1=st.v(cs=slice(19, 20)),
            scalar2=float(RND_C), op0=ALU.mult, op1=ALU.add),
            reads=[h1.buf, st.buf], writes=[h1.buf], excl=False)
        P.op("vector", lambda v: v.tensor_single_scalar(
            out=xq2b.v(), in_=h1.v(), scalar=float(RND_C), op=ALU.subtract),
            reads=[h1.buf], writes=[xq2b.buf])
        P.dma("sync", xq2d.ap()[tc], xq2b.v(), reads=[xq2b.buf],
              writes=[_db(C, "XQ2")], excl=False)
    ba.free_tv(w1q, b1f, ln2g, ln2b, xqb, xqfd, ln1g, ln1b)
    fa.free_tv(sw1b, x1c, xn1c, hc, h1, gacc, st2)

    w2q, sw2b = emit_wquant(C, "w2fT", 4096, 1024)
    b2f = load_const(C, "b2f_rb", 1024, "fa")
    obr = load_const(C, "ob_rb", 1024, "fa")
    combo = fa.alloc(1024, "l1combo")
    P.op("vector", lambda v: v.tensor_tensor(
        out=combo.v(), in0=b2f.v(), in1=obr.v(), op=ALU.add),
        reads=[b2f.buf, obr.buf], writes=[combo.buf])
    fa.free_tv(b2f, obr)
    x1c = fa.alloc(1024, "l2x")
    t1 = fa.alloc(1024, "l2t1")
    sc2c = fa.alloc(16, "l2sc")
    xq2bd = [xq2b, ba.alloc(4096, "l2xq2b")]
    xq2fd2 = [ba.alloc(4096, "l2xq2f") for _ in range(2)]
    outd = C.dram["out"]

    def frontend2(tc):
        xb, xf = xq2bd[tc % 2], xq2fd2[tc % 2]
        P.dma("gpsimd", xb.v(), xq2d.ap()[tc], reads=[_db(C, "XQ2")],
              writes=[xb.buf])
        for b4 in range(4):
            for c in range(8):
                P.op("tensor", lambda t, b4=b4, c=c, xb=xb: t.transpose(
                    psT.v(cs=slice(c * 128, c * 128 + 128)),
                    xb.v(cs=slice(b4 * 1024 + c * 128, b4 * 1024 + c * 128 + 128)),
                    C.ident.v()),
                    reads=[xb.buf, C.ident.buf], writes=[psT.buf], excl=(c == 0))
            P.op("scalar", lambda s, b4=b4, xf=xf: s.copy(
                out=xf.v(cs=slice(b4 * 1024, b4 * 1024 + 1024)), in_=psT.v()),
                reads=[psT.buf], writes=[xf.buf], excl=(b4 == 0))

    frontend2(0)
    for tc in range(nt):
        xq2f = xq2fd2[tc % 2]
        P.op("vector", lambda v, tc=tc: v.tensor_tensor(
            out=sc2c.v(cs=slice(0, 1)), in0=scales2.v(cs=slice(tc, tc + 1)),
            in1=sw2b.v(cs=slice(1, 2)), op=ALU.mult),
            reads=[scales2.buf, sw2b.buf], writes=[sc2c.buf])
        for half in range(2):
            for k2 in range(32):
                P.op("tensor", lambda t, half=half, k2=k2, xq2f=xq2f: t.matmul(
                    psA.v(cs=slice(half * 512, half * 512 + 512)),
                    xq2f.v(cs=slice(k2 * 128, k2 * 128 + 128)),
                    w2q[k2].v(cs=slice(half * 512, half * 512 + 512)),
                    start=(k2 == 0), stop=(k2 == 31)),
                    reads=[xq2f.buf, w2q[k2].buf], writes=[psA.buf],
                    excl=(half == 0 and k2 == 0))
        P.dma("sync", x1c.v(), C.dram["X1"].ap()[tc * 128:(tc + 1) * 128, :],
              reads=[_db(C, "X1")], writes=[x1c.buf])
        if tc + 1 < nt:
            frontend2(tc + 1)
        P.op("vector", lambda v: v.scalar_tensor_tensor(
            out=t1.v(), in0=psA.v(), scalar=sc2c.v(cs=slice(0, 1)),
            in1=combo.v(), op0=ALU.mult, op1=ALU.add),
            reads=[psA.buf, sc2c.buf, combo.buf], writes=[t1.buf])
        P.op("vector", lambda v: v.tensor_tensor(
            out=t1.v(), in0=t1.v(), in1=x1c.v(), op=ALU.add),
            reads=[t1.buf, x1c.buf], writes=[t1.buf], excl=False)
        if tc == 0:
            P.op("vector", lambda v: v.tensor_tensor(
                out=t1.v(rs=slice(0, 16)), in0=t1.v(rs=slice(0, 16)),
                in1=C.o16.v(rs=slice(0, 16)), op=ALU.add),
                reads=[t1.buf, C.o16.buf], writes=[t1.buf], excl=False)
        if tc == 16:
            P.op("vector", lambda v: v.tensor_tensor(
                out=t1.v(rs=slice(0, 16)), in0=t1.v(rs=slice(0, 16)),
                in1=C.o16b.v(rs=slice(0, 16)), op=ALU.add),
                reads=[t1.buf, C.o16b.buf], writes=[t1.buf], excl=False)
        P.dma("sync", outd.ap()[tc * 128:(tc + 1) * 128, :], t1.v(),
              reads=[t1.buf], writes=[_db(C, "out")], excl=False)
    ba.free_tv(w2q, xq2bd, xq2fd2, junk)
    fa.free_tv(combo, t1, x1c, st, scales2, sw2b, sc2c)


def build_program():
    nc = bass.Bass("TRN2", target_bir_lowering=False, detect_race_conditions=False)
    C = Ctx()
    C.nc = nc
    C.dram = {}
    C.dbuf = {}
    _mk_dram(C)
    C.P = Prog(nc)
    C.ba = Arena(nc, "AB", 84992, BF16)
    C.fa = Arena(nc, "FA", 10240, F32)
    C.psS = TV(nc.alloc_psum_tensor("psS", [128, 2048], F32), 0, 2048, "psS")
    C.psSb = [Buf(f"psS{i}", serial=True) for i in range(4)]
    C.psS.buf.serial = True
    C.psA = TV(nc.alloc_psum_tensor("psA", [128, 1024], F32), 0, 1024, "psA")
    C.psB = TV(nc.alloc_psum_tensor("psB", [128, 512], F32), 0, 512, "psB")
    C.psT = TV(nc.alloc_psum_tensor("psT", [128, 1024], BF16), 0, 1024, "psT")
    for b_ in (C.psA.buf, C.psB.buf, C.psT.buf):
        b_.serial = True
    C.ident = load_const(C, "ident", 128)
    C.identf = load_const(C, "identf", 128, "fa")
    C.rrot = load_const(C, "Rrot", 128)
    C.onesk = load_const(C, "ones_k", 1)
    C.onesmf = load_const(C, "ones_mf", 128, "fa", rows=1)
    C.mLT = load_const(C, "maskLT", 128, "fa")
    C.mUT = load_const(C, "maskUT", 128, "fa")
    C.mPD = load_const(C, "maskPD", 128, "fa")
    C.mPF = load_const(C, "maskPF", 128, "fa")
    C.gkv = load_const(C, "g_kv_rb", 256, "fa")
    C.w1_cached = [False, False]

    ph = [0]

    def phase():
        ph[0] += 1
        return ph[0] <= MAX_PHASE

    if phase():
        emit_ln0(C, "x", RMAIN, "XN")
    if phase():
        emit_ln0(C, "px", RPREF, "PXN")

    def main_kmap(qi):
        return [(j, (C.mLT if j == qi else None)) for j in range(qi + 1)]

    def main_ffkmap(qi):
        lo = max(0, qi - 4)
        out = []
        for j in range(lo, qi + 1):
            m = None
            if qi >= 4 and j == qi - 4:
                m = C.mUT
            if j == qi:
                m = C.mLT
            out.append((j, m))
        return out

    def pref_kmap(qi):
        return [(qi, C.mPD)]

    seqs = []
    for s in range(SEQ_PER_CORE):
        seqs.append(SeqCtx(
            name=f"s{s}", Tn=T, tc0=s * (T // 128), prefix=False,
            xn_name="XN", ao_name="AO", ao_row0=s * T,
            ropeC_name="ropeCos", ropeS_name="ropeSin",
            kmap=main_kmap, ff_kmap=main_ffkmap,
            y_name="Y", y_idx=s, shift_blocks=None,
            x_src="x", x_row0=s * T, x1_name="X1"))
    seqs.append(SeqCtx(
        name="pref", Tn=RPREF, tc0=0, prefix=True,
        xn_name="PXN", ao_name="PAO", ao_row0=0,
        ropeC_name="ropeCosP", ropeS_name="ropeSinP",
        kmap=pref_kmap, ff_kmap=pref_kmap,
        y_name="PY", y_idx=0, shift_blocks=PT_,
        x_src="px", x_row0=0, x1_name="PX1"))

    for sq in seqs:
        if phase():
            sq.ropeC = load_const(C, sq.ropeC_name, sq.Tn)
            sq.ropeS = load_const(C, sq.ropeS_name, sq.Tn)
            emit_l0_attn(C, sq)
            C.ba.free_tv(sq.ropeC, sq.ropeS)
        for l in range(2):
            if phase():
                emit_l0_ffn(C, sq, l)
        if phase():
            emit_x1(C, sq)

    if phase():
        C.fa.free_tv(C.gkv, C.mLT, C.mUT, C.mPD)
        emit_prefix_l1(C)
        C.fa.free_tv(C.mPF)
    if phase():
        emit_l1_main(C)
    C.P.wait_all("sync")
    C.P.finalize()
    print(f"[build] n_inst={nc.n_instructions()} ba_peak={C.ba.peak} "
          f"fa_peak={C.fa.peak} selfwaits={C.P.n_selfwaits} sems={C.P.count}")
    return nc, C.input_names


# version 05 (final)
# speedup vs baseline: 1.0462x; 1.0462x over previous
import numpy as np
import ml_dtypes
import concourse.bass as bass
import concourse.mybir as mybir
from concourse.bass import AP
from concourse.bass_utils import run_bass_kernel_spmd

F32 = mybir.dt.float32
BF16 = mybir.dt.bfloat16
I32 = mybir.dt.int32
AF = mybir.ActivationFunctionType
ALU = mybir.AluOpType
AX = mybir.AxisListType
BFNP = ml_dtypes.bfloat16

NCORES = 8
B_ALL, T, D = 16, 2048, 1024
SEQ_PER_CORE = 2
RMAIN = SEQ_PER_CORE * T
PT_ = 16
RPREF = B_ALL * PT_
NEG = -1e9
RND_C = np.float32(1.5 * 2.0 ** 23)

HID = 2730
GLU_BLOCKS = []
for j in range(0, HID, 128):
    w = min(128, HID - j)
    GLU_BLOCKS.append(("a", j, j + w))
    GLU_BLOCKS.append(("g", j, j + w))
H_CHUNKS = [(j, min(128, HID - j)) for j in range(0, HID, 128)]


class Buf:
    __slots__ = ("writers", "readers", "name", "serial")

    def __init__(self, name="", serial=False):
        self.writers = []
        self.readers = []
        self.name = name
        self.serial = serial


class Prog:
    COMPUTE = {"tensor": "cT", "vector": "cV", "scalar": "cS", "gpsimd": "cG"}
    QUEUE = {"sync": "dS", "gpsimd": "dG", "scalar": "dA", "vector": "dV"}
    NQSEM = 8

    def __init__(self, nc):
        self.nc = nc
        self.items = {e: [] for e in ("sync", "tensor", "vector", "scalar", "gpsimd")}
        self.sem = {}
        self.count = {}
        for s in ("cT", "cV", "cS", "cG"):
            self.sem[s] = nc.alloc_semaphore(s)
            self.count[s] = 0
        for q in ("dS", "dG"):
            for k in range(self.NQSEM):
                s = f"{q}{k}"
                self.sem[s] = nc.alloc_semaphore(s)
                self.count[s] = 0
        self.qidx = {"dS": 0, "dG": 0}
        self.last_wait = {}
        self.opord = {e: 0 for e in self.COMPUTE}
        self.n_selfwaits = 0

    def _wait(self, eng, tok, raw=False):
        if tok is None:
            return
        s, v = tok[0], tok[1]
        if self.COMPUTE.get(eng) == s:
            if raw and len(tok) > 2 and tok[2] >= self.opord[eng] - 1:
                key = (eng, s)
                if self.last_wait.get(key, 0) >= v:
                    return
                self.last_wait[key] = v
                self.items[eng].append(("w", s, v))
                self.n_selfwaits += 1
            return
        key = (eng, s)
        if self.last_wait.get(key, 0) >= v:
            return
        self.last_wait[key] = v
        self.items[eng].append(("w", s, v))

    def _deps(self, eng, reads, writes, excl):
        for b in reads:
            for t in b.writers:
                self._wait(eng, t, raw=True)
            if b.serial:
                for t in b.readers:
                    self._wait(eng, t)
        for b in writes:
            for t in b.writers:
                self._wait(eng, t)
            for t in b.readers:
                self._wait(eng, t)

    def _record(self, tok, reads, writes, excl):
        for b in reads:
            b.readers.append(tok)
        for b in writes:
            if excl:
                b.writers = [tok]
                b.readers = []
            else:
                b.writers.append(tok)

    def op(self, eng, fn, reads=(), writes=(), excl=True):
        self._deps(eng, reads, writes, excl)
        s = self.COMPUTE[eng]
        self.count[s] += 1
        tok = (s, self.count[s], self.opord[eng])
        self.opord[eng] += 1
        self.items[eng].append(("c", fn))
        self._record(tok, reads, writes, excl)
        return tok

    def dma(self, qeng, out_ap, in_ap, reads=(), writes=(), excl=True):
        self._deps(qeng, reads, writes, excl)
        q = self.QUEUE[qeng]
        s = f"{q}{self.qidx[q] % self.NQSEM}"
        self.qidx[q] += 1
        self._wait(qeng, (s, self.count[s]))
        self.count[s] += 16
        tok = (s, self.count[s])
        self.items[qeng].append(("d", out_ap, in_ap, s))
        self._record(tok, reads, writes, excl)
        return tok

    def barrier_token(self, eng):
        s = self.COMPUTE.get(eng) or self.QUEUE[eng]
        return (s, self.count[s])

    def wait_all(self, eng, excludes=()):
        for s, v in list(self.count.items()):
            if v and s not in excludes and (self.COMPUTE.get(eng) != s):
                self._wait(eng, (s, v))

    def finalize(self):
        nc = self.nc
        sem = self.sem

        def runner(items):
            def f(eng):
                for it in items:
                    if it[0] == "w":
                        eng.wait_ge(sem[it[1]], it[2])
                    elif it[0] == "c":
                        ins = it[1](eng)
                        ins.then_inc(sem[self.COMPUTE[engname(eng)]], 1)
                    else:
                        ins = eng.dma_start(out=it[1], in_=it[2])
                        ins.then_inc(sem[it[3]], 16)
            return f

        def engname(eng):
            et = eng.engine
            return {
                mybir.EngineType.PE: "tensor",
                mybir.EngineType.DVE: "vector",
                mybir.EngineType.Activation: "scalar",
                mybir.EngineType.Pool: "gpsimd",
                mybir.EngineType.SP: "sync",
            }[et]

        with nc.Block() as block:
            block.sync(runner(self.items["sync"]))
            block.tensor(runner(self.items["tensor"]))
            block.vector(runner(self.items["vector"]))
            block.scalar(runner(self.items["scalar"]))
            block.gpsimd(runner(self.items["gpsimd"]))


class TV:

    __slots__ = ("t", "c0", "cols", "buf", "_rsv")

    def __init__(self, t, c0, cols, name):
        self.t = t
        self.c0 = c0
        self.cols = cols
        self.buf = Buf(name)

    def v(self, rs=None, cs=None):
        r = rs if rs is not None else slice(0, 128)
        if cs is None:
            c = slice(self.c0, self.c0 + self.cols)
        else:
            c = slice(self.c0 + cs.start, self.c0 + cs.stop)
        return self.t[r, c]

    def bcast(self, groups, rep, rows=128):
        pitch = self.t.shape[1]
        return AP(self.t[:].tensor, self.c0, [[pitch, rows], [1, groups], [0, rep]])


class Arena:

    def __init__(self, nc, name, cols, dtype):
        self.t = nc.alloc_sbuf_tensor(name, [128, cols], dtype)
        self.cols = cols
        self.free = [(0, cols)]
        self.peak = 0
        self.name = name
        self.ghosts = []

    def alloc(self, cols, name=""):
        req = cols
        cols = (cols + 15) & ~15
        for i, (o, w) in enumerate(self.free):
            if w >= cols:
                tv = TV(self.t, o, req, name)
                tv._rsv = cols
                if w == cols:
                    self.free.pop(i)
                else:
                    self.free[i] = (o + cols, w - cols)
                self.peak = max(self.peak, self.cols - sum(x[1] for x in self.free))
                keep = []
                toks = {}
                for go, gw, gt in self.ghosts:
                    if go < o + cols and o < go + gw:
                        for t in gt:
                            k = t[0]
                            if k not in toks or toks[k][1] < t[1]:
                                toks[k] = t
                        if go < o:
                            keep.append((go, o - go, gt))
                        if go + gw > o + cols:
                            keep.append((o + cols, go + gw - (o + cols), gt))
                    else:
                        keep.append((go, gw, gt))
                self.ghosts = keep
                tv.buf.writers = list(toks.values())
                return tv
        raise AssertionError(
            f"{self.name} arena OOM: want {cols} cols for {name!r}; free={self.free}")

    def alloc_list(self, n, cols, name=""):
        return [self.alloc(cols, f"{name}{i}") for i in range(n)]

    def free_tv(self, *tvs):
        for tv in tvs:
            if tv is None:
                continue
            if isinstance(tv, (list, tuple)):
                self.free_tv(*tv)
                continue
            w_ = getattr(tv, "_rsv", (tv.cols + 15) & ~15)
            self.free.append((tv.c0, w_))
            toks = list(tv.buf.writers) + list(tv.buf.readers)
            if toks:
                self.ghosts.append((tv.c0, w_, toks))
        self.free.sort()
        merged = []
        for o, w in self.free:
            if merged and merged[-1][0] + merged[-1][1] == o:
                merged[-1] = (merged[-1][0], merged[-1][1] + w)
            else:
                merged.append((o, w))
        self.free = merged


def _host_consts():
    c = {}
    ii, jj = np.arange(128)[:, None], np.arange(128)[None, :]
    c["ident"] = np.eye(128, dtype=BFNP)
    c["maskLT"] = np.where(jj <= ii, 0.0, NEG).astype(np.float32)
    c["maskUT"] = np.where(jj >= ii, 0.0, NEG).astype(np.float32)
    same16 = (ii // PT_) == (jj // PT_)
    c["maskPD"] = np.where(same16 & (jj <= ii), 0.0, NEG).astype(np.float32)
    c["maskPF"] = np.where(same16, 0.0, NEG).astype(np.float32)
    c["ones_k"] = np.ones((128, 1), dtype=BFNP)
    c["ones_m"] = np.ones((1, 128), dtype=BFNP)
    c["ones_mf"] = np.ones((1, 128), dtype=np.float32)
    c["identf"] = np.eye(128, dtype=np.float32)

    RD_ = 32
    inv_freq = 1.0 / 10000.0 ** (np.arange(0, RD_, 2, dtype=np.float32) / RD_)
    t_ = np.arange(T, dtype=np.float32)
    emb = np.concatenate([np.outer(t_, inv_freq)] * 2, -1)
    cosT, sinT = np.cos(emb).astype(np.float32), np.sin(emb).astype(np.float32)
    c["ropeCos"] = np.tile(cosT.T, (4, 1)).astype(BFNP)
    c["ropeSin"] = np.tile(sinT.T, (4, 1)).astype(BFNP)
    ppos = np.tile(np.arange(PT_), B_ALL)
    c["ropeCosP"] = np.tile(cosT[ppos].T, (4, 1)).astype(BFNP)
    c["ropeSinP"] = np.tile(sinT[ppos].T, (4, 1)).astype(BFNP)
    P32 = np.zeros((32, 32), np.float32)
    for i in range(16):
        P32[i, 16 + i] = -1.0
        P32[16 + i, i] = 1.0
    R = np.zeros((128, 128), np.float32)
    for b in range(4):
        R[b * 32:(b + 1) * 32, b * 32:(b + 1) * 32] = P32
    c["Rrot"] = R.T.astype(BFNP)

    DH, WIN = 64, 512
    half = np.arange(0, DH, 2, dtype=np.float32)
    invf = 1.0 / 10000.0 ** (half / DH)
    fr = np.concatenate([t_[:, None] * invf[None, :]] * 2, -1)
    base = (half + 0.4 * DH) / (1.4 * DH)
    power = (t_ - T // 2) / float(WIN // 2)
    scl = np.concatenate([base[None, :] ** power[:, None]] * 2, -1).astype(np.float32)
    xc, xs = np.cos(fr).astype(np.float32), np.sin(fr).astype(np.float32)
    sgn = np.concatenate([-np.ones(32, np.float32), np.ones(32, np.float32)])
    tabs = dict(xcs=xc * scl, xss=xs * scl * sgn, xci=xc / scl, xsi=xs / scl * sgn)
    for k, v in tabs.items():
        c[k] = np.tile(v, (1, 8)).astype(BFNP)
        c[k + "p"] = np.tile(v[ppos], (1, 8)).astype(BFNP)
    return c


def _colpack(vec, blocks=None):
    if blocks is None:
        assert len(vec) % 128 == 0
        blocks = [(j, 128) for j in range(0, len(vec), 128)]
    out = np.zeros((128, len(blocks)), np.float32)
    for j, (lo, w) in enumerate(blocks):
        out[:w, j] = vec[lo:lo + w]
    return out


def _rb(vec):
    return np.ascontiguousarray(np.broadcast_to(np.asarray(vec, np.float32), (128, len(vec))))


def _stage_inputs(inputs):
    sh = {}
    f = lambda k: np.asarray(inputs[k], np.float32)
    for k in ("W_kv", "W_q", "W_krope", "W_kdec", "W_vdec", "W_qdec", "W_qrope", "W_o"):
        sh[k] = np.ascontiguousarray(f("L0_attn_" + k))
    sh["g_kv_rb"] = _rb(f("L0_attn_g_kv"))
    w1 = f("L0_ffn_w1")
    b1 = f("L0_ffn_b1")
    colperm = np.concatenate(
        [np.r_[lo:hi] + (HID if kind == "g" else 0) for kind, lo, hi in GLU_BLOCKS])
    for l in range(2):
        sh[f"wqkv{l}"] = np.ascontiguousarray(f("L0_ffn_wqkv")[l])
        sh[f"wo{l}"] = np.ascontiguousarray(f("L0_ffn_wo")[l])
        w1pl = np.ascontiguousarray(w1[l][:, colperm])
        w1pt = np.zeros((len(GLU_BLOCKS), 128, 1024), np.float32)
        w1pr = w1pl.reshape(8, 128, 2 * HID)
        joff = 0
        for jb, (kind, lo, hi) in enumerate(GLU_BLOCKS):
            blk = hi - lo
            w1pt[jb, :, :].reshape(128, 8, 128)[:, :, 0:blk] = \
                w1pr[:, :, joff:joff + blk].transpose(1, 0, 2)
            joff += blk
        sh[f"w1pt{l}"] = w1pt
        blocks1 = [(j * 128, GLU_BLOCKS[j][2] - GLU_BLOCKS[j][1]) for j in range(len(GLU_BLOCKS))]
        b1perm = b1[l][colperm]
        sh[f"b1cp{l}"] = _colpack(b1perm, [(sum(w for _, w in blocks1[:j]), blocks1[j][1])
                                           for j in range(len(blocks1))])
        sh[f"w2_{l}"] = np.ascontiguousarray(f("L0_ffn_w2")[l])
        sh[f"b2cp{l}"] = _colpack(f("L0_ffn_b2")[l])
        sh[f"lngcp{l}"] = _colpack(f("L0_ffn_ln_g")[l])
        sh[f"lnbcp{l}"] = _colpack(f("L0_ffn_ln_b")[l])
        sh[f"qs_rb{l}"] = _rb(np.tile(f("L0_ffn_q_scale")[l], 8))
        sh[f"ks_rb{l}"] = _rb(np.tile(f("L0_ffn_k_scale")[l], 8))
    for k in ("qw", "kw", "vw", "ow", "lqw", "lkw", "lvw"):
        sh[k + "T"] = np.ascontiguousarray(f("L1_attn_" + k).T)
    sh["qb_rb"] = _rb(f("L1_attn_qb"))
    sh["vb_rb"] = _rb(f("L1_attn_vb"))
    sh["ob_rb"] = _rb(f("L1_attn_ob"))
    sh["lna_rb"] = _rb(f("L1_attn_lna_w"))
    sh["lnb_rb"] = _rb(f("L1_attn_lnb_w"))
    sh["lnc_rb"] = _rb(f("L1_attn_lnc_w"))
    sh["lnd_rb"] = _rb(f("L1_attn_lnd_w"))
    sh["lqb_rb"] = _rb(f("L1_attn_lqb"))
    sh["lkb_rb"] = _rb(f("L1_attn_lkb"))
    sh["lvb_rb"] = _rb(f("L1_attn_lvb"))
    sh["w1fT"] = np.ascontiguousarray(f("L1_ffn_w1").T)
    sh["w2fT"] = np.ascontiguousarray(f("L1_ffn_w2").T)
    sh["ln1g_rb"] = _rb(f("L1_ffn_ln1_g"))
    sh["ln1b_rb"] = _rb(f("L1_ffn_ln1_b"))
    sh["ln2g_rb"] = _rb(f("L1_ffn_ln2_g"))
    sh["ln2b_rb"] = _rb(f("L1_ffn_ln2_b"))
    sh["b1f_rb"] = _rb(f("L1_ffn_b1"))
    sh["b2f_rb"] = _rb(f("L1_ffn_b2"))
    return sh


def kernel(**inputs):
    x = np.asarray(inputs["L0_attn_x"], np.float32)
    shared = _stage_inputs(inputs)
    consts = _host_consts()

    nc, input_names = build_program()

    in_maps = []
    for core in range(NCORES):
        own = [2 * core, 2 * core + 1]
        others = [b for b in range(B_ALL) if b not in own]
        perm = [own[0], others[0], own[1]] + others[1:]
        m = dict(shared)
        m.update(consts)
        m["x"] = np.ascontiguousarray(x[own].reshape(RMAIN, D))
        m["px"] = np.ascontiguousarray(x[perm, :PT_, :].reshape(RPREF, D))
        in_maps.append({k: m[k] for k in input_names})

    cores = DBG_CORES if DBG_CORES else list(range(NCORES))
    res = run_bass_kernel_spmd(nc, [in_maps[c] for c in cores], core_ids=cores)
    global LAST_RES
    if DBG:
        LAST_RES = res
    out = np.zeros((B_ALL, T, D), np.float32)
    for i, core in enumerate(cores):
        out[2 * core:2 * core + 2] = res.results[i]["out"].reshape(SEQ_PER_CORE, T, D)
    return out


DBG = False


def _dbg_on(name):
    return DBG is True or (isinstance(DBG, (list, tuple)) and name in DBG)
ATTN_STAGE = 9
DBG_CORES = None
MAX_PHASE = 99
LAST_RES = None
SCALE_MLA = np.float32(1.0 / np.sqrt(np.float32(96.0)))
SCALE_LOC = np.float32(8.0)
SCALE_L1 = np.float32(1.0 / np.sqrt(np.float32(64.0)))


class Ctx:
    pass


def _mk_dram(C):
    nc = C.nc
    names = {}
    def di(name, shape, dt=F32):
        names[name] = True
        C.dram[name] = nc.dram_tensor(name, list(shape), dt, kind="ExternalInput")
    di("x", (RMAIN, D)); di("px", (RPREF, D))
    for k in ("W_kv",): di(k, (1024, 256))
    di("W_q", (1024, 512)); di("W_krope", (1024, 512))
    di("W_kdec", (256, 1024)); di("W_vdec", (256, 1024))
    di("W_qdec", (512, 1024)); di("W_qrope", (512, 512)); di("W_o", (1024, 1024))
    di("g_kv_rb", (128, 256))
    for l in range(2):
        di(f"wqkv{l}", (1024, 1536)); di(f"wo{l}", (512, 1024))
        di(f"w1pt{l}", (len(GLU_BLOCKS), 128, 1024)); di(f"b1cp{l}", (128, len(GLU_BLOCKS)))
        di(f"w2_{l}", (HID, 1024)); di(f"b2cp{l}", (128, 8))
        di(f"lngcp{l}", (128, 8)); di(f"lnbcp{l}", (128, 8))
        di(f"qs_rb{l}", (128, 512)); di(f"ks_rb{l}", (128, 512))
    for k in ("qwT", "kwT", "vwT", "owT"): di(k, (1024, 1024))
    for k in ("lqwT", "lkwT", "lvwT"): di(k, (64, 64))
    for k in ("qb_rb", "vb_rb", "ob_rb", "lna_rb", "lnb_rb",
              "ln1g_rb", "ln1b_rb", "b2f_rb"): di(k, (128, 1024))
    for k in ("lnc_rb", "lnd_rb", "lqb_rb", "lkb_rb", "lvb_rb"): di(k, (128, 64))
    di("w1fT", (1024, 4096)); di("w2fT", (4096, 1024))
    for k in ("ln2g_rb", "ln2b_rb", "b1f_rb"): di(k, (128, 4096))
    di("ident", (128, 128), BF16)
    for k in ("maskLT", "maskUT", "maskPD", "maskPF"): di(k, (128, 128))
    di("ones_k", (128, 1), BF16); di("ones_m", (1, 128), BF16)
    di("ones_mf", (1, 128), F32); di("identf", (128, 128), F32)
    di("ropeCos", (128, T), BF16); di("ropeSin", (128, T), BF16)
    di("ropeCosP", (128, RPREF), BF16); di("ropeSinP", (128, RPREF), BF16)
    di("Rrot", (128, 128), BF16)
    for k in ("xcs", "xss", "xci", "xsi"):
        di(k, (T, 512), BF16); di(k + "p", (RPREF, 512), BF16)
    C.input_names = list(names)

    def ds(name, shape, dt=F32, kind=None):
        if kind is None:
            expose = (DBG is True) or (DBG and name in DBG)
            kind = "ExternalOutput" if expose else "Internal"
        C.dram[name] = nc.dram_tensor(name, list(shape), dt, kind=kind)
    ds("XN", (RMAIN // 128, 128, 8, 128), BF16)
    ds("XNT", (RMAIN // 128, 128, 1024), BF16)
    ds("STA", (2, 8, 16, 128, 16), F32)
    ds("QCD", (8, 128, 2048), BF16)
    ds("PSD", (128, 512), F32)
    ds("WQD", (4, 128, 512), BF16)
    ds("QCD2", (128, 512), BF16)
    ds("PBD", (128, 2048), BF16)
    ds("OTMD", (16, 128, 1024), BF16)
    ds("STT", (RMAIN // 128, 128, 32), F32)
    ds("PXN", (RPREF // 128, 128, 8, 128), BF16)
    ds("Y", (SEQ_PER_CORE, 8, 128, T + 1))
    ds("PY", (1, 8, 128, RPREF + 1))
    ds("AO", (RMAIN, D)); ds("PAO", (RPREF, D))
    ds("X1", (RMAIN, D)); ds("PX1", (RPREF, D))
    ds("XQ2", (RMAIN // 128, 128, 4096), BF16)
    ds("W1C0", (len(GLU_BLOCKS), 128, 1024), BF16)
    ds("W1C1", (len(GLU_BLOCKS), 128, 1024), BF16)
    ds("out", (RMAIN, D), F32, "ExternalOutput")


def _db(C, name):
    if name not in C.dbuf:
        C.dbuf[name] = Buf(name)
    return C.dbuf[name]


def loadw(C, dname, K, M, name=None, cs=None, dt_cols=None):
    d = C.dram[dname]
    nk = (K + 127) // 128
    tvs = []
    for k in range(nk):
        r = min(128, K - k * 128)
        w = M if cs is None else (cs.stop - cs.start)
        tv = C.ba.alloc(w, f"{name or dname}k{k}")
        src = d.ap()[k * 128:k * 128 + r, :] if cs is None else \
            d.ap()[k * 128:k * 128 + r, cs.start:cs.stop]
        C.P.dma("gpsimd", tv.v(rs=slice(0, r)), src,
                reads=[_db(C, dname)], writes=[tv.buf])
        tvs.append(tv)
    return tvs


def load_const(C, dname, cols, dtype_arena="ba", rows=128):
    a = getattr(C, dtype_arena)
    tv = a.alloc(cols, dname)
    C.P.dma("gpsimd", tv.v(rs=slice(0, rows)), C.dram[dname].ap()[0:rows, :],
            reads=[_db(C, dname)], writes=[tv.buf])
    return tv


def ln_stats(C, st, c_s1, c_s2, n, eps, c_m, c_rstd):
    P = C.P
    sc = lambda c: st.v(cs=slice(c, c + 1))
    inv_n = float(1.0 / n)
    e2, mm, var, sd = 8, 9, 10, 11
    P.op("vector", lambda v: v.tensor_single_scalar(out=sc(c_m), in_=sc(c_s1),
         scalar=inv_n, op=ALU.mult), reads=[st.buf], writes=[st.buf], excl=False)
    P.op("vector", lambda v: v.tensor_single_scalar(out=sc(e2), in_=sc(c_s2),
         scalar=inv_n, op=ALU.mult), reads=[st.buf], writes=[st.buf], excl=False)
    P.op("vector", lambda v: v.tensor_tensor(out=sc(mm), in0=sc(c_m),
         in1=sc(c_m), op=ALU.mult), reads=[st.buf], writes=[st.buf], excl=False)
    P.op("vector", lambda v: v.tensor_tensor(out=sc(var), in0=sc(e2),
         in1=sc(mm), op=ALU.subtract), reads=[st.buf], writes=[st.buf], excl=False)
    P.op("vector", lambda v: v.tensor_single_scalar(out=sc(var), in_=sc(var),
         scalar=float(eps), op=ALU.add), reads=[st.buf], writes=[st.buf], excl=False)
    P.op("scalar", lambda s: s.activation(out=sc(sd), in_=sc(var),
         func=AF.Sqrt), reads=[st.buf], writes=[st.buf], excl=False)
    P.op("vector", lambda v: v.reciprocal(out=sc(c_rstd), in_=sc(sd)),
         reads=[st.buf], writes=[st.buf], excl=False)


def emit_ln0(C, src_name, R, xn_name):
    P, ba, fa = C.P, C.ba, C.fa
    nt = R // 128
    xin = [fa.alloc(1024, "ln0x") for _ in range(2)]
    junk = ba.alloc(1024, "ln0junk")
    xnt = ba.alloc(1024, "ln0xn")
    stage = [ba.alloc(1024, "ln0stage") for _ in range(2)]
    st = fa.alloc(32, "ln0st")
    src, xnd = C.dram[src_name], C.dram[xn_name]
    for i in range(nt):
        xi, sg = xin[i % 2], stage[i % 2]
        P.dma("gpsimd", xi.v(), src.ap()[i * 128:(i + 1) * 128, :],
              reads=[_db(C, src_name)], writes=[xi.buf])
        P.op("vector", lambda v, xi=xi: v.reduce_sum(
            out=st.v(cs=slice(0, 1)), in_=xi.v(), axis=AX.X),
            reads=[xi.buf], writes=[st.buf], excl=False)
        P.op("scalar", lambda s, xi=xi: s.activation(
            out=junk.v(), in_=xi.v(), func=AF.Square,
            accum_out=st.v(cs=slice(1, 2))),
            reads=[xi.buf], writes=[st.buf, junk.buf], excl=False)
        ln_stats(C, st, 0, 1, 1024, 1e-5, 2, 3)
        P.op("vector", lambda v, xi=xi: v.tensor_scalar(
            out=xnt.v(), in0=xi.v(), scalar1=st.v(cs=slice(2, 3)),
            scalar2=st.v(cs=slice(3, 4)), op0=ALU.subtract, op1=ALU.mult),
            reads=[xi.buf, st.buf], writes=[xnt.buf])
        if _dbg_on("XNT"):
            P.dma("sync", C.dram["XNT"].ap()[i], xnt.v(),
                  reads=[xnt.buf], writes=[_db(C, "XNT")], excl=False)
        if _dbg_on("STT"):
            P.dma("sync", C.dram["STT"].ap()[i], st.v(),
                  reads=[st.buf], writes=[_db(C, "STT")], excl=False)
        for c in range(8):
            P.op("tensor", lambda t, c=c: t.transpose(
                C.psT.v(cs=slice(c * 128, (c + 1) * 128)),
                xnt.v(cs=slice(c * 128, (c + 1) * 128)), C.ident.v()),
                reads=[xnt.buf, C.ident.buf], writes=[C.psT.buf], excl=(c == 0))
        P.op("scalar", lambda s, sg=sg: s.copy(out=sg.v(), in_=C.psT.v()),
             reads=[C.psT.buf], writes=[sg.buf])
        P.dma("sync", xnd.ap()[i],
              sg.v().rearrange("p (c j) -> p c j", c=8),
              reads=[sg.buf], writes=[_db(C, xn_name)], excl=False)
    ba.free_tv(junk, xnt, stage)
    fa.free_tv(xin, st)


class SeqCtx:

    def __init__(self, **kw):
        self.__dict__.update(kw)
        self.nt = self.Tn // 128
        self.nch = min(512, self.Tn)


def _xn_sweep(C, sq, body):
    P, ba = C.P, C.ba
    nch = sq.nch
    nstg = 2
    stages = [[ba.alloc(nch, f"xns{b}f{f}") for f in range(8)] for b in range(nstg)]
    xnd = C.dram[sq.xn_name]
    for n in range(sq.Tn // nch):
        stg = stages[n % nstg]
        tc0 = sq.tc0 + n * (nch // 128)
        for f in range(8):
            src = xnd.ap()[tc0:tc0 + nch // 128, :, f, :].transpose([1, 0, 2])
            P.dma("gpsimd", stg[f].v().rearrange("p (a j) -> p a j", j=128), src,
                  reads=[_db(C, sq.xn_name)], writes=[stg[f].buf])
        body(n, n * nch, stg)
    for b in stages:
        ba.free_tv(b)


def emit_l0_attn(C, sq):
    P, ba, fa = C.P, C.ba, C.fa
    nt, Tn, nch = sq.nt, sq.Tn, sq.nch
    psS, psA, psB, psT = C.psS, C.psA, C.psB, C.psT

    kvn = ba.alloc_list(2, Tn, "kvn")
    qc = ba.alloc_list(4, Tn, "qc")
    kraw = ba.alloc_list(4, Tn, "kraw")
    kvc = fa.alloc(256, "kvc")
    st = fa.alloc(32, "attnst")
    junk = ba.alloc(256, "rmsjunk")

    wkv = loadw(C, "W_kv", 1024, 256)
    wq = loadw(C, "W_q", 1024, 512)
    wkr = loadw(C, "W_krope", 1024, 512)

    def proj_sweep(n, t0, stg):
        nsub = nch // 128
        if ATTN_STAGE <= 0.2:
            return
        for ts_ in range(nsub):
            tg = t0 + ts_ * 128
            for f in range(8):
                P.op("tensor", lambda t, f=f, ts_=ts_: t.matmul(
                    psB.v(cs=slice(0, 256)),
                    stg[f].v(cs=slice(ts_ * 128, ts_ * 128 + 128)),
                    wkv[f].v(), start=(f == 0), stop=(f == 7)),
                    reads=[stg[f].buf, wkv[f].buf], writes=[psB.buf], excl=(f == 0))
            if ATTN_STAGE <= 0.25:
                continue
            P.op("vector", lambda v: v.tensor_copy(out=kvc.v(), in_=psB.v(cs=slice(0, 256))),
                 reads=[psB.buf], writes=[kvc.buf])
            if ATTN_STAGE <= 0.3:
                continue
            P.op("scalar", lambda s: s.activation(
                out=junk.v(), in_=psB.v(cs=slice(0, 256)), func=AF.Square,
                accum_out=st.v(cs=slice(1, 2))),
                reads=[psB.buf], writes=[st.buf, junk.buf], excl=False)
            P.op("vector", lambda v: v.tensor_single_scalar(
                out=st.v(cs=slice(2, 3)), in_=st.v(cs=slice(1, 2)),
                scalar=1.0 / 256.0, op=ALU.mult),
                reads=[st.buf], writes=[st.buf], excl=False)
            P.op("vector", lambda v: v.tensor_single_scalar(
                out=st.v(cs=slice(2, 3)), in_=st.v(cs=slice(2, 3)),
                scalar=1e-6, op=ALU.add), reads=[st.buf], writes=[st.buf], excl=False)
            P.op("scalar", lambda s: s.activation(
                out=st.v(cs=slice(3, 4)), in_=st.v(cs=slice(2, 3)),
                func=AF.Sqrt), reads=[st.buf], writes=[st.buf], excl=False)
            P.op("vector", lambda v: v.reciprocal(
                out=st.v(cs=slice(4, 5)), in_=st.v(cs=slice(3, 4))),
                reads=[st.buf], writes=[st.buf], excl=False)
            if ATTN_STAGE <= 0.35:
                continue
            P.op("vector", lambda v: v.scalar_tensor_tensor(
                out=junk.v(), in0=kvc.v(), scalar=st.v(cs=slice(4, 5)),
                in1=C.gkv.v(rs=slice(0, 128), cs=slice(0, 256)),
                op0=ALU.mult, op1=ALU.mult),
                reads=[kvc.buf, st.buf, C.gkv.buf], writes=[junk.buf])
            for r in range(2):
                P.op("tensor", lambda t, r=r: t.transpose(
                    psT.v(cs=slice(r * 128, r * 128 + 128)),
                    junk.v(cs=slice(r * 128, r * 128 + 128)), C.ident.v()),
                    reads=[junk.buf, C.ident.buf], writes=[psT.buf], excl=(r == 0))
            for r in range(2):
                P.op("scalar", lambda s, r=r, tg=tg: s.copy(
                    out=kvn[r].v(cs=slice(tg, tg + 128)),
                    in_=psT.v(cs=slice(r * 128, r * 128 + 128))),
                    reads=[psT.buf], writes=[kvn[r].buf], excl=False)
        if ATTN_STAGE <= 0.4:
            return
        for m in range(4):
            for f in range(8):
                P.op("tensor", lambda t, m=m, f=f: t.matmul(
                    psS.v(cs=slice(m * 512, m * 512 + nch)),
                    wq[f].v(cs=slice(m * 128, m * 128 + 128)), stg[f].v(),
                    start=(f == 0), stop=(f == 7)),
                    reads=[wq[f].buf, stg[f].buf], writes=[C.psSb[m]], excl=(f == 0))
            P.op("scalar", lambda s, m=m: s.copy(
                out=qc[m].v(cs=slice(t0, t0 + nch)),
                in_=psS.v(cs=slice(m * 512, m * 512 + nch))),
                reads=[C.psSb[m]], writes=[qc[m].buf], excl=False)
        if ATTN_STAGE <= 0.6:
            return
        for m in range(4):
            for f in range(8):
                P.op("tensor", lambda t, m=m, f=f: t.matmul(
                    psS.v(cs=slice(m * 512, m * 512 + nch)),
                    wkr[f].v(cs=slice(m * 128, m * 128 + 128)), stg[f].v(),
                    start=(f == 0), stop=(f == 7)),
                    reads=[wkr[f].buf, stg[f].buf], writes=[C.psSb[m]], excl=(f == 0))
            P.op("scalar", lambda s, m=m: s.copy(
                out=kraw[m].v(cs=slice(t0, t0 + nch)),
                in_=psS.v(cs=slice(m * 512, m * 512 + nch))),
                reads=[C.psSb[m]], writes=[kraw[m].buf], excl=False)

    _xn_sweep(C, sq, proj_sweep)
    ba.free_tv(wkv, wq, wkr)
    if ATTN_STAGE < 2:
        return

    def rope(raw_tvs, out_tvs):
        tmp1 = ba.alloc(512, "ropet1")
        tmp2 = ba.alloc(512, "ropet2")
        for m in range(len(raw_tvs)):
            for w0 in range(0, Tn, 512):
                w = min(512, Tn - w0)
                cs = slice(w0, w0 + w)
                P.op("tensor", lambda t, m=m, cs=cs, w=w: t.matmul(
                    psB.v(cs=slice(0, w)), C.rrot.v(), raw_tvs[m].v(cs=cs),
                    start=True, stop=True),
                    reads=[C.rrot.buf, raw_tvs[m].buf], writes=[psB.buf])
                P.op("vector", lambda v, cs=cs, w=w: v.tensor_tensor(
                    out=tmp1.v(cs=slice(0, w)), in0=psB.v(cs=slice(0, w)),
                    in1=sq.ropeS.v(cs=cs), op=ALU.mult),
                    reads=[psB.buf, sq.ropeS.buf], writes=[tmp1.buf])
                P.op("vector", lambda v, m=m, cs=cs, w=w: v.tensor_tensor(
                    out=tmp2.v(cs=slice(0, w)), in0=raw_tvs[m].v(cs=cs),
                    in1=sq.ropeC.v(cs=cs), op=ALU.mult),
                    reads=[raw_tvs[m].buf, sq.ropeC.buf], writes=[tmp2.buf])
                P.op("vector", lambda v, m=m, cs=cs, w=w: v.tensor_tensor(
                    out=out_tvs[m].v(cs=cs), in0=tmp1.v(cs=slice(0, w)),
                    in1=tmp2.v(cs=slice(0, w)), op=ALU.add),
                    reads=[tmp1.buf, tmp2.buf], writes=[out_tvs[m].buf], excl=False)
        ba.free_tv(tmp1, tmp2)

    otm = ba.alloc(nt * 1024, "otm")
    Kh = ba.alloc(Tn, "Kh")
    Qh = ba.alloc(Tn, "Qh")
    Pb = ba.alloc(Tn, "Pb")
    PTs = ba.alloc(512, "PTs")
    for g in range(2):
        gcs = slice(g * 512, g * 512 + 512)
        kr = ba.alloc_list(2, Tn, "kr")
        rope(kraw[2 * g:2 * g + 2], kr)
        wqr = loadw(C, "W_qrope", 512, 512, name="wqr", cs=slice(g * 256, g * 256 + 256))
        qraw = ba.alloc_list(2, Tn, "qraw")
        for m in range(2):
            for w0 in range(0, Tn, 512):
                w = min(512, Tn - w0)
                for f in range(4):
                    P.op("tensor", lambda t, m=m, f=f, w0=w0, w=w, wqr=wqr: t.matmul(
                        psS.v(cs=slice(0, w)),
                        wqr[f].v(cs=slice(m * 128, m * 128 + 128)),
                        qc[f].v(cs=slice(w0, w0 + w)),
                        start=(f == 0), stop=(f == 3)),
                        reads=[wqr[f].buf, qc[f].buf], writes=[C.psSb[0]], excl=(f == 0))
                P.op("scalar", lambda s, m=m, w0=w0, w=w, qraw=qraw: s.copy(
                    out=qraw[m].v(cs=slice(w0, w0 + w)), in_=psS.v(cs=slice(0, w))),
                    reads=[C.psSb[0]], writes=[qraw[m].buf], excl=False)
        ba.free_tv(wqr)
        qr = ba.alloc_list(2, Tn, "qr")
        rope(qraw, qr)
        ba.free_tv(qraw)
        wqd = loadw(C, "W_qdec", 512, 1024, name="wqd", cs=gcs)
        if _dbg_on("WQD") and sq.name == "s0" and g == 1:
            for ii in range(4):
                P.dma("sync", C.dram["WQD"].ap()[ii], wqd[ii].v(),
                      reads=[wqd[ii].buf], writes=[_db(C, "WQD")], excl=False)
        qcont = ba.alloc_list(4, Tn, "qcont")
        for m in range(4):
            for w0 in range(0, Tn, 512):
                w = min(512, Tn - w0)
                for f in range(4):
                    P.op("tensor", lambda t, m=m, f=f, w0=w0, w=w, wqd=wqd: t.matmul(
                        psS.v(cs=slice(0, w)),
                        wqd[f].v(cs=slice(m * 128, m * 128 + 128)),
                        qc[f].v(cs=slice(w0, w0 + w)),
                        start=(f == 0), stop=(f == 3)),
                        reads=[wqd[f].buf, qc[f].buf], writes=[C.psSb[0]], excl=(f == 0))
                if _dbg_on("PSD") and sq.name == "s0" and g == 1 and m == 0 and w0 == 0:
                    tapb = ba.alloc(1024, "tapb")
                    P.op("vector", lambda v, tapb=tapb: v.tensor_copy(
                        out=tapb.v(cs=slice(0, 512)), in_=psS.v(cs=slice(0, 512))),
                        reads=[C.psSb[0]], writes=[tapb.buf])
                    P.dma("gpsimd", C.dram["PSD"].ap()[:, 0:512],
                          tapb.v(cs=slice(0, 512)), reads=[tapb.buf],
                          writes=[_db(C, "PSD")], excl=False)
                P.op("scalar", lambda s, m=m, w0=w0, w=w, qcont=qcont: s.copy(
                    out=qcont[m].v(cs=slice(w0, w0 + w)), in_=psS.v(cs=slice(0, w))),
                    reads=[C.psSb[0]], writes=[qcont[m].buf], excl=False)
        ba.free_tv(wqd)
        wkd = loadw(C, "W_kdec", 256, 1024, name="wkd", cs=gcs)
        kc = ba.alloc_list(4, Tn, "kc")
        for m in range(4):
            for w0 in range(0, Tn, 512):
                w = min(512, Tn - w0)
                for r in range(2):
                    P.op("tensor", lambda t, m=m, r=r, w0=w0, w=w, wkd=wkd: t.matmul(
                        psS.v(cs=slice(0, w)),
                        wkd[r].v(cs=slice(m * 128, m * 128 + 128)),
                        kvn[r].v(cs=slice(w0, w0 + w)),
                        start=(r == 0), stop=(r == 1)),
                        reads=[wkd[r].buf, kvn[r].buf], writes=[C.psSb[0]], excl=(r == 0))
                P.op("scalar", lambda s, m=m, w0=w0, w=w, kc=kc: s.copy(
                    out=kc[m].v(cs=slice(w0, w0 + w)), in_=psS.v(cs=slice(0, w))),
                    reads=[C.psSb[0]], writes=[kc[m].buf], excl=False)
        ba.free_tv(wkd)
        wvd = loadw(C, "W_vdec", 256, 1024, name="wvd", cs=gcs)
        vtm = ba.alloc(nt * 512, "vtm")
        for tc in range(nt):
            for r in range(2):
                P.op("tensor", lambda t, tc=tc, r=r, wvd=wvd: t.matmul(
                    psA.v(cs=slice(0, 512)),
                    kvn[r].v(cs=slice(tc * 128, tc * 128 + 128)),
                    wvd[r].v(), start=(r == 0), stop=(r == 1)),
                    reads=[kvn[r].buf, wvd[r].buf], writes=[psA.buf], excl=(r == 0))
            P.op("scalar", lambda s, tc=tc, vtm=vtm: s.copy(
                out=vtm.v(cs=slice(tc * 512, tc * 512 + 512)), in_=psA.v(cs=slice(0, 512))),
                reads=[psA.buf], writes=[vtm.buf], excl=False)
        ba.free_tv(wvd)

        if _dbg_on("QCD") and sq.name == "s0" and g == 1:
            for ii, tvv in enumerate(qcont + qr + kc[:1] + kr[:1]):
                P.dma("sync", C.dram["QCD"].ap()[ii], tvv.v(),
                      reads=[tvv.buf], writes=[_db(C, "QCD")], excl=False)
        for hl in range(8 if ATTN_STAGE >= 3 else 0):
            h = g * 8 + hl
            for dst, cont, rope_ in ((Kh, kc, kr), (Qh, qcont, qr)):
                P.op("vector", lambda v, dst=dst, cont=cont, hl=hl: v.tensor_copy(
                    out=dst.v(rs=slice(0, 64)),
                    in_=cont[hl // 2].v(rs=slice((hl % 2) * 64, (hl % 2) * 64 + 64))),
                    reads=[cont[hl // 2].buf], writes=[dst.buf])
                P.op("vector", lambda v, dst=dst, rope_=rope_, hl=hl: v.tensor_copy(
                    out=dst.v(rs=slice(64, 96)),
                    in_=rope_[hl // 4].v(rs=slice((hl % 4) * 32, (hl % 4) * 32 + 32))),
                    reads=[rope_[hl // 4].buf], writes=[dst.buf], excl=False)
            for qi in range(nt):
                klist = sq.kmap(qi)
                ktot = len(klist) * 128
                j0 = klist[0][0]
                for a0 in range(0, ktot, 512):
                    w = min(512, ktot - a0)
                    P.op("tensor", lambda t, a0=a0, w=w, qi=qi, j0=j0: t.matmul(
                        psS.v(cs=slice(a0, a0 + w)),
                        Qh.v(rs=slice(0, 96), cs=slice(qi * 128, qi * 128 + 128)),
                        Kh.v(rs=slice(0, 96), cs=slice(j0 * 128 + a0, j0 * 128 + a0 + w)),
                        start=True, stop=True),
                        reads=[Qh.buf, Kh.buf], writes=[C.psSb[a0 // 512]],
                        excl=True)
                for idx, (j, mtv) in enumerate(klist):
                    if mtv is not None:
                        P.op("vector", lambda v, idx=idx, mtv=mtv: v.tensor_tensor(
                            out=psS.v(cs=slice(idx * 128, idx * 128 + 128)),
                            in0=psS.v(cs=slice(idx * 128, idx * 128 + 128)),
                            in1=mtv.v(), op=ALU.add),
                            reads=[C.psSb[idx * 128 // 512], mtv.buf],
                            writes=[C.psSb[idx * 128 // 512]], excl=False)
                sbufs = [C.psSb[a // 512] for a in range(0, ktot, 512)]
                P.op("vector", lambda v, ktot=ktot: v.reduce_max(
                    out=st.v(cs=slice(8, 9)), in_=psS.v(cs=slice(0, ktot)), axis=AX.X),
                    reads=sbufs, writes=[st.buf], excl=False)
                P.op("vector", lambda v: v.tensor_single_scalar(
                    out=st.v(cs=slice(9, 10)), in_=st.v(cs=slice(8, 9)),
                    scalar=-float(SCALE_MLA), op=ALU.mult),
                    reads=[st.buf], writes=[st.buf], excl=False)
                P.op("scalar", lambda s, ktot=ktot: s.activation(
                    out=Pb.v(cs=slice(0, ktot)), in_=psS.v(cs=slice(0, ktot)),
                    func=AF.Exp, bias=st.v(cs=slice(9, 10)),
                    scale=float(SCALE_MLA), accum_out=st.v(cs=slice(10, 11))),
                    reads=sbufs + [st.buf], writes=[Pb.buf, st.buf])
                P.op("vector", lambda v: v.reciprocal(
                    out=st.v(cs=slice(11, 12)), in_=st.v(cs=slice(10, 11))),
                    reads=[st.buf], writes=[st.buf], excl=False)
                nk = len(klist)
                for b0 in range(0, nk, 4):
                    bw = min(4, nk - b0)
                    for u in range(bw):
                        P.op("tensor", lambda t, b0=b0, u=u: t.transpose(
                            psT.v(cs=slice(u * 128, u * 128 + 128)),
                            Pb.v(cs=slice((b0 + u) * 128, (b0 + u) * 128 + 128)),
                            C.ident.v()),
                            reads=[Pb.buf, C.ident.buf], writes=[psT.buf], excl=(u == 0))
                    eng = "vector" if (b0 // 4) % 2 == 0 else "scalar"
                    if eng == "vector":
                        fn = lambda e, bw=bw: e.tensor_copy(
                            out=PTs.v(cs=slice(0, bw * 128)),
                            in_=psT.v(cs=slice(0, bw * 128)))
                    else:
                        fn = lambda e, bw=bw: e.copy(
                            out=PTs.v(cs=slice(0, bw * 128)),
                            in_=psT.v(cs=slice(0, bw * 128)))
                    P.op(eng, fn, reads=[psT.buf], writes=[PTs.buf])
                    for u in range(bw):
                        j = klist[b0 + u][0]
                        P.op("tensor", lambda t, u=u, j=j, hl=hl, vtm=vtm,
                             first=(b0 == 0 and u == 0), last=(b0 + u == nk - 1): t.matmul(
                            psA.v(cs=slice(512, 576)),
                            PTs.v(cs=slice(u * 128, u * 128 + 128)),
                            vtm.v(cs=slice(j * 512 + hl * 64, j * 512 + hl * 64 + 64)),
                            start=first, stop=last),
                            reads=[PTs.buf, vtm.buf], writes=[psA.buf],
                            excl=(b0 == 0 and u == 0))
                P.op("vector", lambda v, qi=qi, h=h: v.tensor_single_scalar(
                    out=otm.v(cs=slice(qi * 1024 + h * 64, qi * 1024 + h * 64 + 64)),
                    in_=psA.v(cs=slice(512, 576)), scalar=st.v(cs=slice(11, 12)),
                    op=ALU.mult),
                    reads=[psA.buf, st.buf], writes=[otm.buf], excl=False)
                if _dbg_on("STA") and sq.name == "s0":
                    P.dma("sync", C.dram["STA"].ap()[g, hl, qi],
                          st.v(cs=slice(0, 16)), reads=[st.buf],
                          writes=[_db(C, "STA")], excl=False)
                    if g == 0 and hl == 0 and qi == 1:
                        P.dma("sync", C.dram["PBD"].ap()[:, 0:ktot],
                              Pb.v(cs=slice(0, ktot)), reads=[Pb.buf],
                              writes=[_db(C, "PBD")], excl=False)
        if _dbg_on("QCD2") and sq.name == "s0" and g == 1:
            P.dma("sync", C.dram["QCD2"].ap(), qcont[0].v(cs=slice(0, 512)),
                  reads=[qcont[0].buf], writes=[_db(C, "QCD2")], excl=False)
        ba.free_tv(kr, qr, qcont, kc, vtm)
    ba.free_tv(Kh, Qh, Pb, PTs, qc, kvn, kraw)
    if ATTN_STAGE < 4:
        return

    if _dbg_on("OTMD") and sq.name == "s0":
        for tc in range(nt):
            P.dma("sync", C.dram["OTMD"].ap()[tc],
                  otm.v(cs=slice(tc * 1024, tc * 1024 + 1024)),
                  reads=[otm.buf], writes=[_db(C, "OTMD")], excl=False)
    wo = loadw(C, "W_o", 1024, 1024)
    ofst = ba.alloc(1024, "ofstage")
    aost = [fa.alloc(1024, "aostage") for _ in range(2)]
    aod = C.dram[sq.ao_name]
    for tc in range(nt):
        for c in range(8):
            P.op("tensor", lambda t, tc=tc, c=c: t.transpose(
                psT.v(cs=slice(c * 128, c * 128 + 128)),
                otm.v(cs=slice(tc * 1024 + c * 128, tc * 1024 + c * 128 + 128)),
                C.ident.v()),
                reads=[otm.buf, C.ident.buf], writes=[psT.buf], excl=(c == 0))
        P.op("scalar", lambda s: s.copy(out=ofst.v(), in_=psT.v()),
             reads=[psT.buf], writes=[ofst.buf])
        for c in range(8):
            for half in range(2):
                P.op("tensor", lambda t, c=c, half=half: t.matmul(
                    psA.v(cs=slice(half * 512, half * 512 + 512)),
                    ofst.v(cs=slice(c * 128, c * 128 + 128)),
                    wo[c].v(cs=slice(half * 512, half * 512 + 512)),
                    start=(c == 0), stop=(c == 7)),
                    reads=[ofst.buf, wo[c].buf], writes=[psA.buf],
                    excl=(c == 0 and half == 0))
        ast = aost[tc % 2]
        P.op("vector", lambda v, ast=ast: v.tensor_copy(out=ast.v(), in_=psA.v()),
             reads=[psA.buf], writes=[ast.buf])
        P.dma("sync", aod.ap()[sq.ao_row0 + tc * 128: sq.ao_row0 + (tc + 1) * 128, :],
              ast.v(), reads=[ast.buf], writes=[_db(C, sq.ao_name)], excl=False)
    ba.free_tv(wo, ofst, otm, junk)
    fa.free_tv(aost, kvc, st)


def emit_l0_ffn(C, sq, l):
    P, ba, fa = C.P, C.ba, C.fa
    nt, Tn = sq.nt, sq.Tn
    W5 = min(512, Tn)
    psS, psA, psB, psT = C.psS, C.psA, C.psB, C.psT
    yd = C.dram[sq.y_name]
    ydb = _db(C, sq.y_name)

    yb = ba.alloc_list(8, Tn + 16, "yb")
    if l == 0:
        xnd = C.dram[sq.xn_name]
        for f in range(8):
            P.op("gpsimd", lambda g, f=f: g.memset(yb[f].v(cs=slice(0, 1)), 0.0),
                 writes=[yb[f].buf])
            src = xnd.ap()[sq.tc0:sq.tc0 + nt, :, f, :].transpose([1, 0, 2])
            P.dma("gpsimd", yb[f].v(cs=slice(1, Tn + 1)).rearrange(
                "p (a j) -> p a j", j=128), src,
                reads=[_db(C, sq.xn_name)], writes=[yb[f].buf], excl=False)
    else:
        for f in range(8):
            P.op("gpsimd", lambda g, f=f: g.memset(yb[f].v(cs=slice(0, 1)), 0.0),
                 writes=[yb[f].buf])
            P.dma("gpsimd", yb[f].v(cs=slice(1, Tn + 1)),
                  yd.ap()[sq.y_idx, f, :, 1:Tn + 1],
                  reads=[ydb], writes=[yb[f].buf], excl=False)

    if sq.shift_blocks is None:
        ysh = None
        def sh_ap(f, t0, w):
            base = yb[f] if f < 4 else yb[f]
            off = 1 if f < 4 else 0
            return base.v(cs=slice(t0 + off, t0 + off + w)), base.buf
    else:
        ysh = ba.alloc_list(4, Tn + 16, "ysh")
        nb = Tn // PT_
        for f in range(4, 8):
            sv = ysh[f - 4]
            P.op("gpsimd", lambda g, sv=sv: g.memset(sv.v(cs=slice(0, Tn + 1)), 0.0),
                 writes=[sv.buf])
            pitchy = yb[f].t.shape[1]
            src = AP(yb[f].t[:].tensor, yb[f].c0 + 1, [[pitchy, 128], [PT_, nb], [1, PT_ - 1]])
            dst = AP(sv.t[:].tensor, sv.c0 + 2, [[sv.t.shape[1], 128], [PT_, nb], [1, PT_ - 1]])
            P.op("vector", lambda v, dst=dst, src=src: v.tensor_copy(out=dst, in_=src),
                 reads=[yb[f].buf], writes=[sv.buf], excl=False)
        def sh_ap(f, t0, w):
            base = yb[f] if f < 4 else ysh[f - 4]
            return base.v(cs=slice(t0 + 1, t0 + 1 + w)), base.buf

    wqkv = loadw(C, f"wqkv{l}", 1024, 1536, name="wqkv")
    qfm = ba.alloc_list(4, Tn, "qfm")
    kfm = ba.alloc_list(4, Tn, "kfm")
    vloc = ba.alloc(nt * 512, "vloc")
    qkf = fa.alloc(1024, "qkf")
    qksq = fa.alloc(1024, "qksq")
    st16 = fa.alloc(64, "ffst")
    qkn = ba.alloc(1024, "qkn")
    q1 = ba.alloc(1024, "q1")
    qsw = ba.alloc(512, "qsw")
    q2 = ba.alloc(512, "q2")
    k2t = ba.alloc(512, "k2t")
    xt = {k: ba.alloc(512, "xt" + k) for k in ("cs", "ss", "ci", "si")}
    qsrb = load_const(C, f"qs_rb{l}", 512, "fa")
    ksrb = load_const(C, f"ks_rb{l}", 512, "fa")
    for tc in range(nt):
        t0 = tc * 128
        for k in ("cs", "ss", "ci", "si"):
            nmk = ("x" + k) + ("p" if sq.prefix else "")
            P.dma("sync", xt[k].v(), C.dram[nmk].ap()[t0:t0 + 128, :],
                  reads=[_db(C, nmk)], writes=[xt[k].buf])
        for n3 in range(3):
            for f in range(8):
                ap_, bf_ = sh_ap(f, t0, 128)
                P.op("tensor", lambda t, n3=n3, f=f, ap_=ap_: t.matmul(
                    psS.v(cs=slice(n3 * 512, n3 * 512 + 512)), ap_,
                    wqkv[f].v(cs=slice(n3 * 512, n3 * 512 + 512)),
                    start=(f == 0), stop=(f == 7)),
                    reads=[bf_, wqkv[f].buf], writes=[C.psSb[n3]], excl=(f == 0))
        for half in range(2):
            P.op("vector", lambda v, half=half: v.tensor_copy(
                out=qkf.v(cs=slice(half * 512, half * 512 + 512)),
                in_=psS.v(cs=slice(half * 512, half * 512 + 512))),
                reads=[C.psSb[half]], writes=[qkf.buf], excl=(half == 0))
        P.op("scalar", lambda s, tc=tc: s.copy(
            out=vloc.v(cs=slice(tc * 512, tc * 512 + 512)),
            in_=psS.v(cs=slice(1024, 1536))),
            reads=[C.psSb[2]], writes=[vloc.buf], excl=False)
        P.op("scalar", lambda s: s.square(out=qksq.v(), in_=qkf.v()),
             reads=[qkf.buf], writes=[qksq.buf])
        for hseg in range(16):
            P.op("vector", lambda v, hseg=hseg: v.reduce_sum(
                out=st16.v(cs=slice(hseg, hseg + 1)),
                in_=qksq.v(cs=slice(hseg * 64, hseg * 64 + 64)), axis=AX.X),
                reads=[qksq.buf], writes=[st16.buf], excl=False)
        P.op("scalar", lambda s: s.activation(
            out=st16.v(cs=slice(16, 32)), in_=st16.v(cs=slice(0, 16)),
            func=AF.Sqrt), reads=[st16.buf], writes=[st16.buf], excl=False)
        P.op("vector", lambda v: v.tensor_single_scalar(
            out=st16.v(cs=slice(32, 48)), in_=st16.v(cs=slice(16, 32)),
            scalar=1e-12, op=ALU.max), reads=[st16.buf], writes=[st16.buf], excl=False)
        P.op("vector", lambda v: v.reciprocal(
            out=st16.v(cs=slice(48, 64)), in_=st16.v(cs=slice(32, 48))),
            reads=[st16.buf], writes=[st16.buf], excl=False)
        P.op("vector", lambda v: v.tensor_tensor(
            out=qkn.v().rearrange("p (a b) -> p a b", a=16),
            in0=qkf.v().rearrange("p (a b) -> p a b", a=16),
            in1=AP(st16.t[:].tensor, st16.c0 + 48,
                   [[st16.t.shape[1], 128], [1, 16], [0, 64]]),
            op=ALU.mult),
            reads=[qkf.buf, st16.buf], writes=[qkn.buf])
        P.op("vector", lambda v: v.tensor_tensor(
            out=q1.v(cs=slice(0, 512)), in0=qkn.v(cs=slice(0, 512)),
            in1=qsrb.v(), op=ALU.mult),
            reads=[qkn.buf, qsrb.buf], writes=[q1.buf])
        P.op("vector", lambda v: v.tensor_tensor(
            out=q1.v(cs=slice(512, 1024)), in0=qkn.v(cs=slice(512, 1024)),
            in1=ksrb.v(), op=ALU.mult),
            reads=[qkn.buf, ksrb.buf], writes=[q1.buf], excl=False)
        for side, (tc_a, tc_b), dst in (("q", ("cs", "ss"), q2), ("k", ("ci", "si"), k2t)):
            off = 0 if side == "q" else 512
            src_sw = AP(q1.t[:].tensor, q1.c0 + off + 32,
                        [[q1.t.shape[1], 128], [64, 8], [1, 32]])
            dst_sw = AP(qsw.t[:].tensor, qsw.c0,
                        [[qsw.t.shape[1], 128], [64, 8], [1, 32]])
            P.op("vector", lambda v, src_sw=src_sw, dst_sw=dst_sw: v.tensor_copy(
                out=dst_sw, in_=src_sw), reads=[q1.buf], writes=[qsw.buf])
            src_sw2 = AP(q1.t[:].tensor, q1.c0 + off,
                         [[q1.t.shape[1], 128], [64, 8], [1, 32]])
            dst_sw2 = AP(qsw.t[:].tensor, qsw.c0 + 32,
                         [[qsw.t.shape[1], 128], [64, 8], [1, 32]])
            P.op("vector", lambda v, src_sw2=src_sw2, dst_sw2=dst_sw2: v.tensor_copy(
                out=dst_sw2, in_=src_sw2), reads=[q1.buf], writes=[qsw.buf], excl=False)
            P.op("vector", lambda v, tcb=tc_b: v.tensor_tensor(
                out=qsw.v(), in0=qsw.v(), in1=xt[tcb].v(),
                op=ALU.mult), reads=[qsw.buf, xt[tc_b].buf], writes=[qsw.buf])
            P.op("vector", lambda v, off=off, dst=dst, tc_a=tc_a: v.scalar_tensor_tensor(
                out=dst.v(), in0=q1.v(cs=slice(off, off + 512)), scalar=1.0,
                in1=xt[tc_a].v(), op0=ALU.mult, op1=ALU.mult),
                reads=[q1.buf, xt[tc_a].buf], writes=[dst.buf])
            P.op("vector", lambda v, dst=dst: v.tensor_tensor(
                out=dst.v(), in0=dst.v(), in1=qsw.v(), op=ALU.add),
                reads=[dst.buf, qsw.buf], writes=[dst.buf], excl=False)
        for src, fml in ((q2, qfm), (k2t, kfm)):
            for c in range(4):
                P.op("tensor", lambda t, src=src, c=c: t.transpose(
                    psT.v(cs=slice(c * 128, c * 128 + 128)),
                    src.v(cs=slice(c * 128, c * 128 + 128)), C.ident.v()),
                    reads=[src.buf, C.ident.buf], writes=[psT.buf], excl=(c == 0))
            for c in range(4):
                P.op("scalar", lambda s, fml=fml, c=c, t0=t0: s.copy(
                    out=fml[c].v(cs=slice(t0, t0 + 128)),
                    in_=psT.v(cs=slice(c * 128, c * 128 + 128))),
                    reads=[psT.buf], writes=[fml[c].buf], excl=False)
    ba.free_tv(wqkv, qkn, q1, qsw, q2, k2t, list(xt.values()))
    fa.free_tv(qkf, qksq, qsrb, ksrb)

    Pb = ba.alloc(768, "ffPb")
    PTs = ba.alloc(512, "ffPTs")
    otm = ba.alloc(nt * 512, "ffotm")
    for h in range(8):
        ch, rs_ = h // 2, slice((h % 2) * 64, (h % 2) * 64 + 64)
        for qi in range(nt):
            klist = sq.ff_kmap(qi)
            ktot = len(klist) * 128
            j0 = klist[0][0]
            for a0 in range(0, ktot, 512):
                w = min(512, ktot - a0)
                P.op("tensor", lambda t, a0=a0, w=w, qi=qi, j0=j0, ch=ch, rs_=rs_: t.matmul(
                    psS.v(cs=slice(a0, a0 + w)),
                    qfm[ch].v(rs=rs_, cs=slice(qi * 128, qi * 128 + 128)),
                    kfm[ch].v(rs=rs_, cs=slice(j0 * 128 + a0, j0 * 128 + a0 + w)),
                    start=True, stop=True),
                    reads=[qfm[ch].buf, kfm[ch].buf], writes=[C.psSb[a0 // 512]])
            for idx, (j, mtv) in enumerate(klist):
                if mtv is not None:
                    P.op("vector", lambda v, idx=idx, mtv=mtv: v.tensor_tensor(
                        out=psS.v(cs=slice(idx * 128, idx * 128 + 128)),
                        in0=psS.v(cs=slice(idx * 128, idx * 128 + 128)),
                        in1=mtv.v(), op=ALU.add),
                        reads=[C.psSb[idx * 128 // 512], mtv.buf],
                        writes=[C.psSb[idx * 128 // 512]], excl=False)
            sbufs = [C.psSb[a // 512] for a in range(0, ktot, 512)]
            P.op("vector", lambda v, ktot=ktot: v.reduce_max(
                out=st16.v(cs=slice(8, 9)), in_=psS.v(cs=slice(0, ktot)), axis=AX.X),
                reads=sbufs, writes=[st16.buf], excl=False)
            P.op("vector", lambda v: v.tensor_single_scalar(
                out=st16.v(cs=slice(9, 10)), in_=st16.v(cs=slice(8, 9)),
                scalar=-float(SCALE_LOC), op=ALU.mult),
                reads=[st16.buf], writes=[st16.buf], excl=False)
            P.op("scalar", lambda s, ktot=ktot: s.activation(
                out=Pb.v(cs=slice(0, ktot)), in_=psS.v(cs=slice(0, ktot)),
                func=AF.Exp, bias=st16.v(cs=slice(9, 10)),
                scale=float(SCALE_LOC), accum_out=st16.v(cs=slice(10, 11))),
                reads=sbufs + [st16.buf], writes=[Pb.buf, st16.buf])
            P.op("vector", lambda v: v.reciprocal(
                out=st16.v(cs=slice(11, 12)), in_=st16.v(cs=slice(10, 11))),
                reads=[st16.buf], writes=[st16.buf], excl=False)
            nk = len(klist)
            for b0 in range(0, nk, 4):
                bw = min(4, nk - b0)
                for u in range(bw):
                    P.op("tensor", lambda t, b0=b0, u=u: t.transpose(
                        psT.v(cs=slice(u * 128, u * 128 + 128)),
                        Pb.v(cs=slice((b0 + u) * 128, (b0 + u) * 128 + 128)),
                        C.ident.v()),
                        reads=[Pb.buf, C.ident.buf], writes=[psT.buf], excl=(u == 0))
                eng = "vector" if (b0 // 4) % 2 == 0 else "scalar"
                if eng == "vector":
                    fn = lambda e, bw=bw: e.tensor_copy(
                        out=PTs.v(cs=slice(0, bw * 128)), in_=psT.v(cs=slice(0, bw * 128)))
                else:
                    fn = lambda e, bw=bw: e.copy(
                        out=PTs.v(cs=slice(0, bw * 128)), in_=psT.v(cs=slice(0, bw * 128)))
                P.op(eng, fn, reads=[psT.buf], writes=[PTs.buf])
                for u in range(bw):
                    j = klist[b0 + u][0]
                    P.op("tensor", lambda t, u=u, j=j, h=h,
                         first=(b0 == 0 and u == 0), last=(b0 + u == nk - 1): t.matmul(
                        psA.v(cs=slice(0, 64)),
                        PTs.v(cs=slice(u * 128, u * 128 + 128)),
                        vloc.v(cs=slice(j * 512 + h * 64, j * 512 + h * 64 + 64)),
                        start=first, stop=last),
                        reads=[PTs.buf, vloc.buf], writes=[psA.buf],
                        excl=(b0 == 0 and u == 0))
            P.op("vector", lambda v, qi=qi, h=h: v.tensor_single_scalar(
                out=otm.v(cs=slice(qi * 512 + h * 64, qi * 512 + h * 64 + 64)),
                in_=psA.v(cs=slice(0, 64)), scalar=st16.v(cs=slice(11, 12)),
                op=ALU.mult),
                reads=[psA.buf, st16.buf], writes=[otm.buf], excl=False)
    ba.free_tv(Pb, PTs, qfm, kfm, vloc)

    ofm = ba.alloc_list(4, Tn, "ofm")
    for tc in range(nt):
        for c in range(4):
            P.op("tensor", lambda t, tc=tc, c=c: t.transpose(
                psT.v(cs=slice(c * 128, c * 128 + 128)),
                otm.v(cs=slice(tc * 512 + c * 128, tc * 512 + c * 128 + 128)),
                C.ident.v()),
                reads=[otm.buf, C.ident.buf], writes=[psT.buf], excl=(c == 0))
        for c in range(4):
            P.op("scalar", lambda s, c=c, tc=tc: s.copy(
                out=ofm[c].v(cs=slice(tc * 128, tc * 128 + 128)),
                in_=psT.v(cs=slice(c * 128, c * 128 + 128))),
                reads=[psT.buf], writes=[ofm[c].buf], excl=False)
    ba.free_tv(otm)
    wo = loadw(C, f"wo{l}", 512, 1024, name="wo")
    ybm = ba.alloc_list(8, Tn + 16, "ybm")
    yst = [fa.alloc(512, "yst") for _ in range(2)]
    yin = [fa.alloc(512, "yin") for _ in range(2)]
    for m in range(8):
        P.op("gpsimd", lambda g, m=m: g.memset(ybm[m].v(cs=slice(0, 1)), 0.0),
             writes=[ybm[m].buf], excl=False)
        for n in range(Tn // W5):
            n0 = n * W5
            for c in range(4):
                P.op("tensor", lambda t, m=m, c=c, n0=n0: t.matmul(
                    psB.v(cs=slice(0, W5)),
                    wo[c].v(cs=slice(m * 128, m * 128 + 128)),
                    ofm[c].v(cs=slice(n0, n0 + W5)),
                    start=(c == 0), stop=(c == 3)),
                    reads=[wo[c].buf, ofm[c].buf], writes=[psB.buf], excl=(c == 0))
            ys = yst[n % 2]
            if l == 0:
                P.op("vector", lambda v, m=m, n0=n0, ys=ys: v.tensor_tensor(
                    out=ys.v(cs=slice(0, W5)), in0=psB.v(cs=slice(0, W5)),
                    in1=yb[m].v(cs=slice(n0 + 1, n0 + 1 + W5)), op=ALU.add),
                    reads=[psB.buf, yb[m].buf], writes=[ys.buf])
            else:
                yi = yin[n % 2]
                P.dma("gpsimd", yi.v(cs=slice(0, W5)),
                      yd.ap()[sq.y_idx, m, :, n0 + 1:n0 + 1 + W5],
                      reads=[ydb], writes=[yi.buf])
                P.op("vector", lambda v, yi=yi, ys=ys: v.tensor_tensor(
                    out=ys.v(cs=slice(0, W5)), in0=psB.v(cs=slice(0, W5)),
                    in1=yi.v(cs=slice(0, W5)), op=ALU.add),
                    reads=[psB.buf, yi.buf], writes=[ys.buf])
            P.dma("sync", yd.ap()[sq.y_idx, m, :, n0 + 1:n0 + 1 + W5],
                  ys.v(cs=slice(0, W5)), reads=[ys.buf], writes=[ydb], excl=False)
            P.op("vector", lambda v, m=m, n0=n0, ys=ys: v.tensor_copy(
                out=ybm[m].v(cs=slice(n0 + 1, n0 + 1 + W5)), in_=ys.v(cs=slice(0, W5))),
                reads=[ys.buf], writes=[ybm[m].buf], excl=False)
    ba.free_tv(wo, ofm, yb)
    if ysh is not None:
        ba.free_tv(ysh)
    fa.free_tv(yst, yin)

    if sq.shift_blocks is None:
        ysh2 = None
        def sh2_ap(f, t0, w):
            off = 1 if f < 4 else 0
            return ybm[f].v(cs=slice(t0 + off, t0 + off + w)), ybm[f].buf
    else:
        ysh2 = ba.alloc_list(4, Tn + 16, "ysh2")
        nb = Tn // PT_
        for f in range(4, 8):
            sv = ysh2[f - 4]
            P.op("gpsimd", lambda g, sv=sv: g.memset(sv.v(cs=slice(0, Tn + 1)), 0.0),
                 writes=[sv.buf])
            src = AP(ybm[f].t[:].tensor, ybm[f].c0 + 1,
                     [[ybm[f].t.shape[1], 128], [PT_, nb], [1, PT_ - 1]])
            dst = AP(sv.t[:].tensor, sv.c0 + 2,
                     [[sv.t.shape[1], 128], [PT_, nb], [1, PT_ - 1]])
            P.op("vector", lambda v, dst=dst, src=src, f=f: v.tensor_copy(
                out=dst, in_=src),
                reads=[ybm[f].buf], writes=[sv.buf], excl=False)
        def sh2_ap(f, t0, w):
            base = ybm[f] if f < 4 else ysh2[f - 4]
            return base.v(cs=slice(t0 + 1, t0 + 1 + w)), base.buf

    w2res = loadw(C, f"w2_{l}", HID, 1024, name="w2res")
    b1cp = load_const(C, f"b1cp{l}", len(GLU_BLOCKS), "fa")
    b2cp = load_const(C, f"b2cp{l}", 8, "fa")
    lgcp = load_const(C, f"lngcp{l}", 8, "fa")
    lbcp = load_const(C, f"lnbcp{l}", 8, "fa")
    nchunks = Tn // W5
    xw1d = [ba.alloc_list(8, W5, "xw1n") for _ in range(min(2, nchunks))]
    hn = [ba.alloc(W5, f"hn{j}") for j in range(len(H_CHUNKS))]
    sqt = [ba.alloc(W5, "sqt") for _ in range(2)]
    srow = fa.alloc(W5, "srow")
    sqrow = fa.alloc(W5, "sqrow")
    t1f = fa.alloc(W5, "t1f")
    w2out = fa.alloc(W5, "w2out")
    yin2 = fa.alloc(W5, "yin2")
    w1t = [ba.alloc(1024, "w1t") for _ in range(4)]
    gl = ba.alloc(W5, "gelu")
    w1c = C.dram[f"W1C{l}"]
    w1p = C.dram[f"w1pt{l}"]

    def stats_ln_thunks(n):
        n0 = n * W5
        xw1n = xw1d[n % 2]
        th = []
        for f in range(8):
            ap_, bf_ = sh2_ap(f, n0, W5)
            th.append(lambda ap_=ap_, bf_=bf_, f=f: P.op(
                "tensor", lambda t: t.matmul(
                    psB.v(rs=slice(0, 1), cs=slice(0, W5)), C.onesk.v(), ap_,
                    start=(f == 0), stop=(f == 7)),
                reads=[bf_, C.onesk.buf], writes=[psB.buf], excl=(f == 0)))
        for f in range(8):
            ap_, bf_ = sh2_ap(f, n0, W5)
            sq_ = sqt[f % 2]
            th.append(lambda ap_=ap_, bf_=bf_, sq_=sq_: P.op(
                "scalar", lambda s: s.square(out=sq_.v(cs=slice(0, W5)), in_=ap_),
                reads=[bf_], writes=[sq_.buf]))
            th.append(lambda sq_=sq_, f=f: P.op(
                "tensor", lambda t: t.matmul(
                    psB.v(rs=slice(32, 33), cs=slice(0, W5)), C.onesk.v(),
                    sq_.v(cs=slice(0, W5)), start=(f == 0), stop=(f == 7)),
                reads=[sq_.buf, C.onesk.buf], writes=[psB.buf], excl=False))
        th.append(lambda: P.op("vector", lambda v: v.tensor_single_scalar(
            out=srow.v(rs=slice(0, 1)), in_=psB.v(rs=slice(0, 1), cs=slice(0, W5)),
            scalar=1.0 / 1024.0, op=ALU.mult),
            reads=[psB.buf], writes=[srow.buf]))
        th.append(lambda: P.op("vector", lambda v: v.tensor_single_scalar(
            out=sqrow.v(rs=slice(0, 1)), in_=psB.v(rs=slice(32, 33), cs=slice(0, W5)),
            scalar=1.0 / 1024.0, op=ALU.mult),
            reads=[psB.buf], writes=[sqrow.buf]))
        th.append(lambda: P.op("vector", lambda v: v.tensor_tensor(
            out=t1f.v(rs=slice(0, 1)), in0=srow.v(rs=slice(0, 1)),
            in1=srow.v(rs=slice(0, 1)), op=ALU.mult),
            reads=[srow.buf], writes=[t1f.buf]))
        th.append(lambda: P.op("vector", lambda v: v.tensor_tensor(
            out=sqrow.v(rs=slice(0, 1)), in0=sqrow.v(rs=slice(0, 1)),
            in1=t1f.v(rs=slice(0, 1)), op=ALU.subtract),
            reads=[sqrow.buf, t1f.buf], writes=[sqrow.buf]))
        th.append(lambda: P.op("vector", lambda v: v.tensor_single_scalar(
            out=sqrow.v(rs=slice(0, 1)), in_=sqrow.v(rs=slice(0, 1)),
            scalar=1e-5, op=ALU.add), reads=[sqrow.buf], writes=[sqrow.buf]))
        th.append(lambda: P.op("scalar", lambda s: s.activation(
            out=sqrow.v(rs=slice(0, 1)), in_=sqrow.v(rs=slice(0, 1)),
            func=AF.Sqrt),
            reads=[sqrow.buf], writes=[sqrow.buf]))
        th.append(lambda: P.op("vector", lambda v: v.reciprocal(
            out=sqrow.v(rs=slice(0, 1)), in_=sqrow.v(rs=slice(0, 1))),
            reads=[sqrow.buf], writes=[sqrow.buf]))
        th.append(lambda: P.op("tensor", lambda t: t.matmul(
            psA.v(cs=slice(0, W5)), C.onesmf.v(rs=slice(0, 1)),
            srow.v(rs=slice(0, 1)), start=True, stop=True),
            reads=[C.onesmf.buf, srow.buf], writes=[psA.buf]))
        th.append(lambda: P.op("tensor", lambda t: t.matmul(
            psA.v(cs=slice(512, 512 + W5)), C.onesmf.v(rs=slice(0, 1)),
            sqrow.v(rs=slice(0, 1)), start=True, stop=True),
            reads=[C.onesmf.buf, sqrow.buf], writes=[psA.buf], excl=False))
        for f in range(8):
            ap_, bf_ = sh2_ap(f, n0, W5)
            th.append(lambda ap_=ap_, bf_=bf_: P.op(
                "vector", lambda v: v.tensor_tensor(
                    out=t1f.v(cs=slice(0, W5)), in0=ap_,
                    in1=psA.v(cs=slice(0, W5)), op=ALU.subtract),
                reads=[bf_, psA.buf], writes=[t1f.buf]))
            th.append(lambda: P.op("vector", lambda v: v.tensor_tensor(
                out=t1f.v(cs=slice(0, W5)), in0=t1f.v(cs=slice(0, W5)),
                in1=psA.v(cs=slice(512, 512 + W5)), op=ALU.mult),
                reads=[t1f.buf, psA.buf], writes=[t1f.buf]))
            th.append(lambda f=f, xw1n=xw1n: P.op(
                "vector", lambda v: v.tensor_scalar(
                    out=xw1n[f].v(cs=slice(0, W5)), in0=t1f.v(cs=slice(0, W5)),
                    scalar1=lgcp.v(cs=slice(f, f + 1)), scalar2=lbcp.v(cs=slice(f, f + 1)),
                    op0=ALU.mult, op1=ALU.add),
                reads=[t1f.buf, lgcp.buf, lbcp.buf], writes=[xw1n[f].buf]))
        return th

    for t_ in stats_ln_thunks(0):
        t_()
    for n in range(nchunks):
        n0 = n * W5
        xw1n = xw1d[n % 2]
        pending = stats_ln_thunks(n + 1) if n + 1 < nchunks else []
        for jb, (kind, lo, hi) in enumerate(GLU_BLOCKS):
            blk = hi - lo
            wt = w1t[jb % 4]
            if C.w1_cached[l]:
                P.dma("gpsimd", wt.v(), w1c.ap()[jb], reads=[_db(C, f"W1C{l}")],
                      writes=[wt.buf])
            else:
                P.dma("gpsimd", wt.v(), w1p.ap()[jb],
                      reads=[_db(C, f"w1pt{l}")], writes=[wt.buf])
                P.dma("sync", w1c.ap()[jb], wt.v(), reads=[wt.buf],
                      writes=[_db(C, f"W1C{l}")], excl=False)
            bank = 2 * ((jb // 2) % 2) + (0 if kind == "a" else 1)
            for f in range(8):
                P.op("tensor", lambda t, wt=wt, f=f, blk=blk, bank=bank, xw1n=xw1n: t.matmul(
                    psS.v(rs=slice(0, blk), cs=slice(bank * 512, bank * 512 + W5)),
                    wt.v(cs=slice(f * 128, f * 128 + blk)),
                    xw1n[f].v(cs=slice(0, W5)),
                    start=(f == 0), stop=(f == 7)),
                    reads=[wt.buf, xw1n[f].buf], writes=[C.psSb[bank]], excl=(f == 0))
            for _ in range(2):
                if pending:
                    pending.pop(0)()
            if kind == "g":
                jj = jb // 2
                banka, bankg = 2 * (jj % 2), 2 * (jj % 2) + 1
                P.op("scalar", lambda s, blk=blk, bankg=bankg, jb=jb: s.activation(
                    out=gl.v(rs=slice(0, blk), cs=slice(0, W5)),
                    in_=psS.v(rs=slice(0, blk), cs=slice(bankg * 512, bankg * 512 + W5)),
                    func=AF.Gelu, bias=b1cp.v(rs=slice(0, blk), cs=slice(jb, jb + 1))),
                    reads=[C.psSb[bankg], b1cp.buf], writes=[gl.buf])
                P.op("vector", lambda v, blk=blk, banka=banka, jj=jj, jb=jb: v.scalar_tensor_tensor(
                    out=hn[jj].v(rs=slice(0, blk), cs=slice(0, W5)),
                    in0=psS.v(rs=slice(0, blk), cs=slice(banka * 512, banka * 512 + W5)),
                    scalar=b1cp.v(rs=slice(0, blk), cs=slice(jb - 1, jb)),
                    in1=gl.v(rs=slice(0, blk), cs=slice(0, W5)),
                    op0=ALU.add, op1=ALU.mult),
                    reads=[C.psSb[banka], b1cp.buf, gl.buf], writes=[hn[jj].buf])
        for m in range(8):
            wb = m % 4
            for k2, (r0, rw) in enumerate(H_CHUNKS):
                P.op("tensor", lambda t, m=m, k2=k2, rw=rw, wb=wb: t.matmul(
                    psS.v(cs=slice(wb * 512, wb * 512 + W5)),
                    w2res[k2].v(rs=slice(0, rw), cs=slice(m * 128, m * 128 + 128)),
                    hn[k2].v(rs=slice(0, rw), cs=slice(0, W5)),
                    start=(k2 == 0), stop=(k2 == len(H_CHUNKS) - 1)),
                    reads=[w2res[k2].buf, hn[k2].buf], writes=[C.psSb[wb]],
                    excl=(k2 == 0))
            P.dma("gpsimd", yin2.v(cs=slice(0, W5)),
                  yd.ap()[sq.y_idx, m, :, n0 + 1:n0 + 1 + W5],
                  reads=[ydb], writes=[yin2.buf])
            P.op("vector", lambda v, m=m, wb=wb: v.scalar_tensor_tensor(
                out=w2out.v(cs=slice(0, W5)), in0=psS.v(cs=slice(wb * 512, wb * 512 + W5)),
                scalar=b2cp.v(cs=slice(m, m + 1)), in1=yin2.v(cs=slice(0, W5)),
                op0=ALU.add, op1=ALU.add),
                reads=[C.psSb[wb], b2cp.buf, yin2.buf], writes=[w2out.buf])
            P.dma("sync", yd.ap()[sq.y_idx, m, :, n0 + 1:n0 + 1 + W5],
                  w2out.v(cs=slice(0, W5)), reads=[w2out.buf], writes=[ydb], excl=False)
            for _ in range(2):
                if pending:
                    pending.pop(0)()
        while pending:
            pending.pop(0)()
    C.w1_cached[l] = True
    ba.free_tv(ybm, xw1d, hn, sqt, w1t, gl, w2res)
    if ysh2 is not None:
        ba.free_tv(ysh2)
    fa.free_tv(srow, sqrow, t1f, w2out, yin2, b1cp, b2cp, lgcp, lbcp)
    fa.free_tv(st16)


def emit_x1(C, sq):
    P, ba, fa = C.P, C.ba, C.fa
    yd = C.dram[sq.y_name]
    ytmp = fa.alloc(1024, "x1y")
    xin = fa.alloc(1024, "x1x")
    aoin = fa.alloc(1024, "x1ao")
    x1st = fa.alloc(1024, "x1st")
    for tc in range(sq.nt):
        t0 = tc * 128
        for f in range(8):
            P.dma("gpsimd", ytmp.v(cs=slice(f * 128, f * 128 + 128)),
                  yd.ap()[sq.y_idx, f, :, t0 + 1:t0 + 129],
                  reads=[_db(C, sq.y_name)], writes=[ytmp.buf], excl=(f == 0))
        P.dma("sync", xin.v(), C.dram[sq.x_src].ap()[sq.x_row0 + t0:sq.x_row0 + t0 + 128, :],
              reads=[_db(C, sq.x_src)], writes=[xin.buf])
        P.dma("sync", aoin.v(), C.dram[sq.ao_name].ap()[sq.ao_row0 + t0:sq.ao_row0 + t0 + 128, :],
              reads=[_db(C, sq.ao_name)], writes=[aoin.buf])
        for f in range(8):
            P.op("tensor", lambda t, f=f: t.transpose(
                C.psA.v(cs=slice(f * 128, f * 128 + 128)),
                ytmp.v(cs=slice(f * 128, f * 128 + 128)), C.identf.v()),
                reads=[ytmp.buf, C.identf.buf], writes=[C.psA.buf], excl=(f == 0))
        P.op("vector", lambda v: v.tensor_tensor(
            out=x1st.v(), in0=C.psA.v(), in1=xin.v(), op=ALU.add),
            reads=[C.psA.buf, xin.buf], writes=[x1st.buf])
        P.op("vector", lambda v: v.tensor_tensor(
            out=x1st.v(), in0=x1st.v(), in1=aoin.v(), op=ALU.add),
            reads=[x1st.buf, aoin.buf], writes=[x1st.buf], excl=False)
        P.dma("sync", C.dram[sq.x1_name].ap()[sq.ao_row0 + t0:sq.ao_row0 + t0 + 128, :],
              x1st.v(), reads=[x1st.buf], writes=[_db(C, sq.x1_name)], excl=False)
    fa.free_tv(ytmp, xin, aoin, x1st)


def _tm_ln(C, st, src_v, src_bufs, width, eps, junk, out_v, out_bufs,
           out_excl=True):
    P = C.P
    P.op("vector", lambda v: v.reduce_sum(out=st.v(cs=slice(0, 1)), in_=src_v,
         axis=AX.X), reads=src_bufs, writes=[st.buf], excl=False)
    P.op("scalar", lambda s: s.activation(out=junk.v(cs=slice(0, width)), in_=src_v,
         func=AF.Square, accum_out=st.v(cs=slice(1, 2))),
         reads=src_bufs, writes=[st.buf, junk.buf], excl=False)
    ln_stats(C, st, 0, 1, width, eps, 2, 3)
    P.op("vector", lambda v: v.tensor_scalar(
        out=out_v, in0=src_v, scalar1=st.v(cs=slice(2, 3)),
        scalar2=st.v(cs=slice(3, 4)), op0=ALU.subtract, op1=ALU.mult),
        reads=src_bufs + [st.buf], writes=out_bufs, excl=out_excl)


def emit_prefix_l1(C):
    P, ba, fa = C.P, C.ba, C.fa
    psA, psB, psT = C.psA, C.psB, C.psT
    st = fa.alloc(32, "p1st")
    junk = ba.alloc(1024, "p1junk")
    x1t = fa.alloc(1024, "p1x")
    xn1 = [fa.alloc(1024, "p1xn") for _ in range(2)]
    za = ba.alloc_list(2, 1024, "p1za")
    zb = ba.alloc_list(2, 1024, "p1zb")
    for tt in range(2):
        P.dma("gpsimd", x1t.v(), C.dram["PX1"].ap()[tt * 128:tt * 128 + 128, :],
              reads=[_db(C, "PX1")], writes=[x1t.buf])
        _tm_ln(C, st, x1t.v(), [x1t.buf], 1024, 1e-5, junk,
               xn1[tt].v(), [xn1[tt].buf])
    lna = load_const(C, "lna_rb", 1024, "fa")
    lnb = load_const(C, "lnb_rb", 1024, "fa")
    tmpf = fa.alloc(1024, "p1tmp")
    for tt in range(2):
        _tm_ln(C, st, xn1[tt].v(), [xn1[tt].buf], 1024, 1e-5, junk,
               tmpf.v(), [tmpf.buf])
        P.op("vector", lambda v, tt=tt: v.tensor_tensor(
            out=za[tt].v(), in0=tmpf.v(), in1=lna.v(), op=ALU.mult),
            reads=[tmpf.buf, lna.buf], writes=[za[tt].buf])
        P.op("vector", lambda v, tt=tt: v.tensor_tensor(
            out=zb[tt].v(), in0=tmpf.v(), in1=lnb.v(), op=ALU.mult),
            reads=[tmpf.buf, lnb.buf], writes=[zb[tt].buf])
    fa.free_tv(lna, lnb, x1t)
    fa.free_tv(xn1)

    zaf = ba.alloc_list(8, 256, "p1zaf")
    zbf = ba.alloc_list(8, 256, "p1zbf")
    for src, fm in ((za, zaf), (zb, zbf)):
        for tt in range(2):
            for c in range(8):
                P.op("tensor", lambda t, src=src, tt=tt, c=c: t.transpose(
                    psT.v(cs=slice(c * 128, c * 128 + 128)),
                    src[tt].v(cs=slice(c * 128, c * 128 + 128)), C.ident.v()),
                    reads=[src[tt].buf, C.ident.buf], writes=[psT.buf], excl=(c == 0))
            for c in range(8):
                P.op("scalar", lambda s, fm=fm, tt=tt, c=c: s.copy(
                    out=fm[c].v(cs=slice(tt * 128, tt * 128 + 128)),
                    in_=psT.v(cs=slice(c * 128, c * 128 + 128))),
                    reads=[psT.buf], writes=[fm[c].buf], excl=False)
    ba.free_tv(za, zb)

    query = [fa.alloc(1024, "p1q") for _ in range(2)]
    kb16 = ba.alloc_list(2, 1024, "p1k")
    vb16 = ba.alloc_list(2, 1024, "p1v")
    for wname, brch, out_tiles, bias in (
            ("qwT", zaf, query, "qb_rb"), ("kwT", zbf, kb16, None),
            ("vwT", zbf, vb16, "vb_rb")):
        wt = loadw(C, wname, 1024, 1024, name=wname)
        btv = load_const(C, bias, 1024, "fa") if bias else None
        for tt in range(2):
            for half in range(2):
                for f in range(8):
                    P.op("tensor", lambda t, brch=brch, f=f, tt=tt, half=half, wt=wt: t.matmul(
                        psA.v(cs=slice(half * 512, half * 512 + 512)),
                        brch[f].v(cs=slice(tt * 128, tt * 128 + 128)),
                        wt[f].v(cs=slice(half * 512, half * 512 + 512)),
                        start=(f == 0), stop=(f == 7)),
                        reads=[brch[f].buf, wt[f].buf], writes=[psA.buf],
                        excl=(f == 0 and half == 0))
            ot = out_tiles[tt]
            if btv is not None:
                P.op("vector", lambda v, ot=ot, btv=btv: v.scalar_tensor_tensor(
                    out=ot.v(), in0=psA.v(), scalar=1.0, in1=btv.v(),
                    op0=ALU.mult, op1=ALU.add),
                    reads=[psA.buf, btv.buf], writes=[ot.buf])
            else:
                P.op("vector", lambda v, ot=ot: v.tensor_copy(out=ot.v(), in_=psA.v()),
                     reads=[psA.buf], writes=[ot.buf])
        ba.free_tv(wt)
        if btv is not None:
            fa.free_tv(btv)
    ba.free_tv(zaf, zbf)

    lkw = load_const(C, "lkwT", 64, rows=64)
    lvw = load_const(C, "lvwT", 64, rows=64)
    lqw = load_const(C, "lqwT", 64, rows=64)
    lkb = load_const(C, "lkb_rb", 64, "fa")
    lvb = load_const(C, "lvb_rb", 64, "fa")
    lqb = load_const(C, "lqb_rb", 64, "fa")
    lnc = load_const(C, "lnc_rb", 64, "fa")
    lnd = load_const(C, "lnd_rb", 64, "fa")
    kn = ba.alloc_list(2, 1024, "p1kn")
    vi = ba.alloc_list(2, 1024, "p1vi")
    kpf = fa.alloc(64, "p1kpf")
    htf = ba.alloc(128, "p1htf")

    def head_proj(src_tile, h, wtv, btv, out_ap, out_bufs, ln_gamma):
        P.op("tensor", lambda t, src_tile=src_tile, h=h: t.transpose(
            psT.v(rs=slice(0, 64), cs=slice(0, 128)),
            src_tile.v(cs=slice(h * 64, h * 64 + 64)), C.ident.v()),
            reads=[src_tile.buf, C.ident.buf], writes=[psT.buf])
        P.op("scalar", lambda s: s.copy(out=htf.v(rs=slice(0, 64), cs=slice(0, 128)),
             in_=psT.v(rs=slice(0, 64), cs=slice(0, 128))),
             reads=[psT.buf], writes=[htf.buf])
        P.op("tensor", lambda t, wtv=wtv: t.matmul(
            psB.v(cs=slice(0, 64)), htf.v(rs=slice(0, 64), cs=slice(0, 128)),
            wtv.v(rs=slice(0, 64)), start=True, stop=True),
            reads=[htf.buf, wtv.buf], writes=[psB.buf])
        P.op("vector", lambda v, btv=btv: v.scalar_tensor_tensor(
            out=kpf.v(cs=slice(0, 64)), in0=psB.v(cs=slice(0, 64)), scalar=1.0,
            in1=btv.v(), op0=ALU.mult, op1=ALU.add),
            reads=[psB.buf, btv.buf], writes=[kpf.buf])
        if ln_gamma is None:
            P.op("vector", lambda v: v.tensor_copy(out=out_ap, in_=kpf.v(cs=slice(0, 64))),
                 reads=[kpf.buf], writes=out_bufs, excl=False)
        else:
            _tm_ln(C, st, kpf.v(cs=slice(0, 64)), [kpf.buf], 64, 1e-5, junk,
                   tmpf.v(cs=slice(0, 64)), [tmpf.buf])
            P.op("vector", lambda v, ln_gamma=ln_gamma: v.tensor_tensor(
                out=out_ap, in0=tmpf.v(cs=slice(0, 64)), in1=ln_gamma.v(),
                op=ALU.mult), reads=[tmpf.buf, ln_gamma.buf], writes=out_bufs,
                excl=False)

    for tt in range(2):
        for h in range(16):
            head_proj(kb16[tt], h, lkw, lkb,
                      kn[tt].v(cs=slice(h * 64, h * 64 + 64)), [kn[tt].buf], lnd)
            head_proj(vb16[tt], h, lvw, lvb,
                      vi[tt].v(cs=slice(h * 64, h * 64 + 64)), [vi[tt].buf], None)
    ba.free_tv(kb16, vb16)
    knf = ba.alloc_list(16, 256, "p1knf")
    for h in range(16):
        for tt in range(2):
            P.op("tensor", lambda t, h=h, tt=tt: t.transpose(
                psT.v(rs=slice(0, 64), cs=slice(tt * 128, tt * 128 + 128)),
                kn[tt].v(cs=slice(h * 64, h * 64 + 64)), C.ident.v()),
                reads=[kn[tt].buf, C.ident.buf], writes=[psT.buf], excl=(tt == 0))
        P.op("scalar", lambda s, h=h: s.copy(
            out=knf[h].v(rs=slice(0, 64)), in_=psT.v(rs=slice(0, 64), cs=slice(0, 256))),
            reads=[psT.buf], writes=[knf[h].buf])
    ba.free_tv(kn)

    qn = ba.alloc(1024, "p1qn")
    qnf = ba.alloc(128, "p1qnf")
    Pp = ba.alloc(256, "p1Pp")
    PTs = ba.alloc(256, "p1PTs")
    itcur = [fa.alloc(1024, "p1it") for _ in range(2)]
    itsv = [fa.alloc(1024, "p1isv") for _ in range(2)]
    cb = fa.alloc(16, "p1cb")
    dcol = fa.alloc(16, "p1dcol")
    dbf = ba.alloc(16, "p1dbf")
    mpf = C.mPF
    qbb = ba.alloc_list(2, 1024, "p1qbb")
    for it in range(3):
        for tt in range(2):
            P.op("vector", lambda v, tt=tt: v.tensor_copy(
                out=qbb[tt].v(), in_=query[tt].v()),
                reads=[query[tt].buf], writes=[qbb[tt].buf])
            for h in range(16):
                head_proj(qbb[tt], h, lqw, lqb,
                          qn.v(cs=slice(h * 64, h * 64 + 64)), [qn.buf], lnc)
            for h in range(16):
                P.op("tensor", lambda t, h=h: t.transpose(
                    psT.v(rs=slice(0, 64), cs=slice(0, 128)),
                    qn.v(cs=slice(h * 64, h * 64 + 64)),
                    C.ident.v()),
                    reads=[qn.buf, C.ident.buf], writes=[psT.buf])
                P.op("scalar", lambda s: s.copy(
                    out=qnf.v(rs=slice(0, 64)), in_=psT.v(rs=slice(0, 64), cs=slice(0, 128))),
                    reads=[psT.buf], writes=[qnf.buf])
                P.op("tensor", lambda t, h=h, tt=tt: t.matmul(
                    psB.v(cs=slice(0, 128)), qnf.v(rs=slice(0, 64), cs=slice(0, 128)),
                    knf[h].v(rs=slice(0, 64), cs=slice(tt * 128, tt * 128 + 128)),
                    start=True, stop=True),
                    reads=[qnf.buf, knf[h].buf], writes=[psB.buf])
                P.op("vector", lambda v: v.tensor_tensor(
                    out=psB.v(cs=slice(0, 128)),
                    in0=psB.v(cs=slice(0, 128)),
                    in1=mpf.v(), op=ALU.add),
                    reads=[psB.buf, mpf.buf], writes=[psB.buf], excl=False)
                P.op("vector", lambda v: v.reduce_max(
                    out=st.v(cs=slice(8, 9)), in_=psB.v(cs=slice(0, 128)), axis=AX.X),
                    reads=[psB.buf], writes=[st.buf], excl=False)
                P.op("vector", lambda v: v.tensor_single_scalar(
                    out=st.v(cs=slice(9, 10)), in_=st.v(cs=slice(8, 9)),
                    scalar=-float(SCALE_L1), op=ALU.mult),
                    reads=[st.buf], writes=[st.buf], excl=False)
                P.op("scalar", lambda s: s.activation(
                    out=Pp.v(cs=slice(0, 128)), in_=psB.v(cs=slice(0, 128)),
                    func=AF.Exp, bias=st.v(cs=slice(9, 10)), scale=float(SCALE_L1),
                    accum_out=st.v(cs=slice(10, 11))),
                    reads=[psB.buf, st.buf], writes=[Pp.buf, st.buf])
                P.op("vector", lambda v: v.reciprocal(
                    out=st.v(cs=slice(11, 12)), in_=st.v(cs=slice(10, 11))),
                    reads=[st.buf], writes=[st.buf], excl=False)
                P.op("tensor", lambda t: t.transpose(
                    psT.v(cs=slice(0, 128)),
                    Pp.v(cs=slice(0, 128)), C.ident.v()),
                    reads=[Pp.buf, C.ident.buf], writes=[psT.buf])
                P.op("scalar", lambda s: s.copy(
                    out=PTs.v(cs=slice(0, 128)), in_=psT.v(cs=slice(0, 128))),
                    reads=[psT.buf], writes=[PTs.buf])
                P.op("tensor", lambda t, tt=tt, h=h: t.matmul(
                    psB.v(cs=slice(256, 320)),
                    PTs.v(cs=slice(0, 128)),
                    vi[tt].v(cs=slice(h * 64, h * 64 + 64)),
                    start=True, stop=True),
                    reads=[PTs.buf, vi[tt].buf], writes=[psB.buf])
                P.op("vector", lambda v, tt=tt, h=h: v.tensor_single_scalar(
                    out=itcur[tt].v(cs=slice(h * 64, h * 64 + 64)),
                    in_=psB.v(cs=slice(256, 320)), scalar=st.v(cs=slice(11, 12)),
                    op=ALU.mult),
                    reads=[psB.buf, st.buf], writes=[itcur[tt].buf], excl=False)
        if it == 0:
            for tt in range(2):
                P.op("vector", lambda v, tt=tt: v.tensor_copy(
                    out=itsv[tt].v(), in_=itcur[tt].v()),
                    reads=[itcur[tt].buf], writes=[itsv[tt].buf])
                P.op("vector", lambda v, tt=tt: v.tensor_tensor(
                    out=query[tt].v(), in0=query[tt].v(), in1=itcur[tt].v(),
                    op=ALU.add),
                    reads=[query[tt].buf, itcur[tt].buf], writes=[query[tt].buf],
                    excl=False)
        elif it == 1:
            for tt in range(2):
                P.op("vector", lambda v, tt=tt: v.tensor_tensor(
                    out=itsv[tt].v(), in0=itcur[tt].v(), in1=itsv[tt].v(),
                    op=ALU.subtract),
                    reads=[itcur[tt].buf, itsv[tt].buf], writes=[itsv[tt].buf])
                P.op("vector", lambda v, tt=tt: v.tensor_reduce(
                    out=dcol.v(cs=slice(tt, tt + 1)), in_=itsv[tt].v(),
                    axis=AX.X, op=ALU.add, apply_absolute_value=True),
                    reads=[itsv[tt].buf], writes=[dcol.buf], excl=False)
            P.op("vector", lambda v: v.tensor_tensor(
                out=dcol.v(cs=slice(2, 3)), in0=dcol.v(cs=slice(0, 1)),
                in1=dcol.v(cs=slice(1, 2)), op=ALU.add),
                reads=[dcol.buf], writes=[dcol.buf], excl=False)
            P.op("vector", lambda v: v.tensor_copy(
                out=dbf.v(cs=slice(0, 1)), in_=dcol.v(cs=slice(2, 3))),
                reads=[dcol.buf], writes=[dbf.buf])
            P.op("tensor", lambda t: t.matmul(
                psB.v(rs=slice(0, 1), cs=slice(0, 1)), C.onesk.v(),
                dbf.v(cs=slice(0, 1)), start=True, stop=True),
                reads=[dbf.buf, C.onesk.buf], writes=[psB.buf])
            NTOT = float(B_ALL * 16 * T * 64)
            P.op("vector", lambda v: v.tensor_single_scalar(
                out=dcol.v(rs=slice(0, 1), cs=slice(4, 5)),
                in_=psB.v(rs=slice(0, 1), cs=slice(0, 1)),
                scalar=0.9 / NTOT, op=ALU.mult),
                reads=[psB.buf], writes=[dcol.buf])
            P.op("vector", lambda v: v.tensor_single_scalar(
                out=dcol.v(rs=slice(0, 1), cs=slice(5, 6)),
                in_=dcol.v(rs=slice(0, 1), cs=slice(4, 5)),
                scalar=0.01, op=ALU.is_lt),
                reads=[dcol.buf], writes=[dcol.buf], excl=False)
            P.op("tensor", lambda t: t.matmul(
                psB.v(cs=slice(16, 17)), C.onesmf.v(rs=slice(0, 1)),
                dcol.v(rs=slice(0, 1), cs=slice(5, 6)), start=True, stop=True),
                reads=[dcol.buf, C.onesmf.buf], writes=[psB.buf])
            P.op("vector", lambda v: v.tensor_copy(
                out=cb.v(cs=slice(0, 1)), in_=psB.v(cs=slice(16, 17))),
                reads=[psB.buf], writes=[cb.buf])
            P.op("vector", lambda v: v.tensor_scalar(
                out=cb.v(cs=slice(1, 2)), in0=cb.v(cs=slice(0, 1)),
                scalar1=-1.0, scalar2=1.0, op0=ALU.mult, op1=ALU.add),
                reads=[cb.buf], writes=[cb.buf], excl=False)
            for tt in range(2):
                P.op("vector", lambda v, tt=tt: v.tensor_copy(
                    out=itsv[tt].v(), in_=itcur[tt].v()),
                    reads=[itcur[tt].buf], writes=[itsv[tt].buf])
                P.op("vector", lambda v, tt=tt: v.scalar_tensor_tensor(
                    out=query[tt].v(), in0=itcur[tt].v(), scalar=cb.v(cs=slice(1, 2)),
                    in1=query[tt].v(), op0=ALU.mult, op1=ALU.add),
                    reads=[itcur[tt].buf, cb.buf, query[tt].buf],
                    writes=[query[tt].buf])
    fa.free_tv(query)
    ba.free_tv(qbb)
    fin = [fa.alloc(1024, "p1fin") for _ in range(2)]
    for tt in range(2):
        P.op("vector", lambda v, tt=tt: v.tensor_single_scalar(
            out=fin[tt].v(), in_=itcur[tt].v(), scalar=cb.v(cs=slice(1, 2)),
            op=ALU.mult), reads=[itcur[tt].buf, cb.buf], writes=[fin[tt].buf])
        P.op("vector", lambda v, tt=tt: v.scalar_tensor_tensor(
            out=fin[tt].v(), in0=itsv[tt].v(), scalar=cb.v(cs=slice(0, 1)),
            in1=fin[tt].v(), op0=ALU.mult, op1=ALU.add),
            reads=[itsv[tt].buf, cb.buf, fin[tt].buf], writes=[fin[tt].buf])
    fa.free_tv(itcur, itsv)
    ba.free_tv(qn, qnf, Pp, PTs, knf, vi)
    finb = ba.alloc(1024, "p1finb")
    P.op("vector", lambda v: v.tensor_copy(out=finb.v(), in_=fin[0].v()),
         reads=[fin[0].buf], writes=[finb.buf])
    ff = ba.alloc_list(8, 128, "p1ff")
    for c in range(8):
        P.op("tensor", lambda t, c=c: t.transpose(
            psT.v(cs=slice(c * 128, c * 128 + 128)),
            finb.v(cs=slice(c * 128, c * 128 + 128)), C.ident.v()),
            reads=[finb.buf, C.ident.buf], writes=[psT.buf], excl=(c == 0))
    for c in range(8):
        P.op("scalar", lambda s, c=c: s.copy(
            out=ff[c].v(), in_=psT.v(cs=slice(c * 128, c * 128 + 128))),
            reads=[psT.buf], writes=[ff[c].buf], excl=False)
    owt = loadw(C, "owT", 1024, 1024, name="owT")
    o16 = fa.alloc(1024, "o16")
    for half in range(2):
        for c in range(8):
            P.op("tensor", lambda t, c=c, half=half: t.matmul(
                psA.v(cs=slice(half * 512, half * 512 + 512)),
                ff[c].v(), owt[c].v(cs=slice(half * 512, half * 512 + 512)),
                start=(c == 0), stop=(c == 7)),
                reads=[ff[c].buf, owt[c].buf], writes=[psA.buf],
                excl=(c == 0 and half == 0))
    P.op("vector", lambda v: v.tensor_copy(out=o16.v(), in_=psA.v()),
         reads=[psA.buf], writes=[o16.buf])
    o16b = fa.alloc(1024, "o16b")
    P.op("vector", lambda v: v.tensor_copy(
        out=o16b.v(rs=slice(0, 16)), in_=o16.v(rs=slice(32, 48))),
        reads=[o16.buf], writes=[o16b.buf])
    C.o16b = o16b
    ba.free_tv(owt, ff, finb, junk, dbf, htf, lkw, lvw, lqw)
    fa.free_tv(fin, cb, dcol, kpf, tmpf, st, lkb, lvb, lqb, lnc, lnd)
    C.o16 = o16


def emit_wquant(C, src_name, K, M):
    P, ba, fa = C.P, C.ba, C.fa
    nk = (K + 127) // 128
    wq = ba.alloc_list(nk, M, "wq_" + src_name)
    st = fa.alloc(16, "wqst")
    acc = fa.alloc(16, "wqacc")
    tmps = [fa.alloc(1024, "wqtmp") for _ in range(2)]
    d = C.dram[src_name]
    P.op("vector", lambda v: v.memset(acc.v(cs=slice(0, 1)), 0.0), writes=[acc.buf])
    si = 0
    for k in range(nk):
        for m0 in range(0, M, 1024):
            w = min(1024, M - m0)
            tmp = tmps[si % 2]
            sc_ = 4 + si % 2
            si += 1
            P.dma("sync", tmp.v(cs=slice(0, w)), d.ap()[k * 128:(k + 1) * 128, m0:m0 + w],
                  reads=[_db(C, src_name)], writes=[tmp.buf])
            P.op("vector", lambda v, w=w, tmp=tmp, sc_=sc_: v.tensor_reduce(
                out=st.v(cs=slice(sc_, sc_ + 1)), in_=tmp.v(cs=slice(0, w)), axis=AX.X,
                op=ALU.add, apply_absolute_value=True),
                reads=[tmp.buf], writes=[st.buf])
            P.op("vector", lambda v, sc_=sc_: v.tensor_tensor(
                out=acc.v(cs=slice(0, 1)), in0=acc.v(cs=slice(0, 1)),
                in1=st.v(cs=slice(sc_, sc_ + 1)), op=ALU.add),
                reads=[st.buf, acc.buf], writes=[acc.buf])
    P.op("tensor", lambda t: t.transpose(
        C.psA.v(rs=slice(0, 1), cs=slice(0, 128)), acc.v(cs=slice(0, 1)),
        C.identf.v()),
        reads=[acc.buf, C.identf.buf], writes=[C.psA.buf])
    for i in range(4):
        P.op("vector", lambda v, i=i: v.reduce_sum(
            out=st.v(rs=slice(0, 1), cs=slice(4 + i, 5 + i)),
            in_=C.psA.v(rs=slice(0, 1), cs=slice(32 * i, 32 * i + 32)), axis=AX.X),
            reads=[C.psA.buf], writes=[st.buf], excl=(i == 0))
    P.op("vector", lambda v: v.reduce_sum(
        out=st.v(rs=slice(0, 1), cs=slice(0, 1)),
        in_=st.v(rs=slice(0, 1), cs=slice(4, 8)), axis=AX.X),
        reads=[st.buf], writes=[st.buf], excl=False)
    P.op("vector", lambda v: v.tensor_single_scalar(
        out=st.v(rs=slice(0, 1), cs=slice(1, 2)),
        in_=st.v(rs=slice(0, 1), cs=slice(0, 1)),
        scalar=1.0 / float(K * M), op=ALU.mult),
        reads=[st.buf], writes=[st.buf], excl=False)
    P.op("vector", lambda v: v.tensor_single_scalar(
        out=st.v(rs=slice(0, 1), cs=slice(2, 3)), in_=st.v(rs=slice(0, 1), cs=slice(1, 2)),
        scalar=1e-5, op=ALU.max), reads=[st.buf], writes=[st.buf], excl=False)
    P.op("vector", lambda v: v.reciprocal(
        out=st.v(rs=slice(0, 1), cs=slice(3, 4)), in_=st.v(rs=slice(0, 1), cs=slice(2, 3))),
        reads=[st.buf], writes=[st.buf], excl=False)
    swb = fa.alloc(16, "swb")
    for i, srcc in enumerate((3, 2)):
        P.op("tensor", lambda t, i=i, srcc=srcc: t.matmul(
            C.psB.v(cs=slice(16 + i, 17 + i)), C.onesmf.v(rs=slice(0, 1)),
            st.v(rs=slice(0, 1), cs=slice(srcc, srcc + 1)), start=True, stop=True),
            reads=[st.buf, C.onesmf.buf], writes=[C.psB.buf], excl=(i == 0))
    P.op("vector", lambda v: v.tensor_copy(
        out=swb.v(cs=slice(0, 2)), in_=C.psB.v(cs=slice(16, 18))),
        reads=[C.psB.buf], writes=[swb.buf])
    si = 0
    for k in range(nk):
        r = min(128, K - k * 128)
        for m0 in range(0, M, 1024):
            w = min(1024, M - m0)
            tmp = tmps[si % 2]
            si += 1
            P.dma("sync", tmp.v(rs=slice(0, r), cs=slice(0, w)),
                  d.ap()[k * 128:k * 128 + r, m0:m0 + w],
                  reads=[_db(C, src_name)], writes=[tmp.buf])
            P.op("vector", lambda v, w=w, tmp=tmp: v.tensor_scalar(
                out=tmp.v(cs=slice(0, w)), in0=tmp.v(cs=slice(0, w)),
                scalar1=swb.v(cs=slice(0, 1)), scalar2=float(RND_C),
                op0=ALU.mult, op1=ALU.add),
                reads=[tmp.buf, swb.buf], writes=[tmp.buf], excl=False)
            P.op("vector", lambda v, w=w, tmp=tmp: v.tensor_scalar(
                out=tmp.v(cs=slice(0, w)), in0=tmp.v(cs=slice(0, w)),
                scalar1=float(RND_C), scalar2=-1.0,
                op0=ALU.subtract, op1=ALU.max),
                reads=[tmp.buf], writes=[tmp.buf], excl=False)
            P.op("vector", lambda v, k=k, m0=m0, w=w, tmp=tmp: v.tensor_scalar(
                out=wq[k].v(cs=slice(m0, m0 + w)), in0=tmp.v(cs=slice(0, w)),
                scalar1=1.0, scalar2=None, op0=ALU.min),
                reads=[tmp.buf], writes=[wq[k].buf])
    fa.free_tv(st, acc, tmps)
    return wq, swb


def emit_l1_main(C):
    P, ba, fa = C.P, C.ba, C.fa
    psS, psA, psB, psT = C.psS, C.psA, C.psB, C.psT
    nt = RMAIN // 128

    w1q, sw1b = emit_wquant(C, "w1fT", 1024, 4096)
    ln1g = load_const(C, "ln1g_rb", 1024)
    ln1b = load_const(C, "ln1b_rb", 1024)
    b1f = load_const(C, "b1f_rb", 4096)
    ln2g = load_const(C, "ln2g_rb", 4096)
    ln2b = load_const(C, "ln2b_rb", 4096)
    junk = ba.alloc(4096, "l1junk")
    h1 = fa.alloc(4096, "l1h1")
    x1c = fa.alloc(1024, "l1x")
    xn1c = fa.alloc(1024, "l1xn")
    hc = fa.alloc(1024, "l1h")
    st = fa.alloc(32, "l1st")
    st2 = fa.alloc(32, "l1st2")
    xqb = ba.alloc(1024, "l1xq")
    xqfd = [ba.alloc(1024, "l1xqf") for _ in range(2)]
    xq2b = ba.alloc(4096, "l1xq2")
    scales2 = fa.alloc(48, "l1sc2")
    gacc = fa.alloc(16, "l1gacc")
    xq2d = C.dram["XQ2"]

    def frontend(tc):
        xqf = xqfd[tc % 2]
        P.dma("gpsimd", x1c.v(), C.dram["X1"].ap()[tc * 128:(tc + 1) * 128, :],
              reads=[_db(C, "X1")], writes=[x1c.buf])
        _tm_ln(C, st2, x1c.v(), [x1c.buf], 1024, 1e-5, junk, xn1c.v(), [xn1c.buf])
        _tm_ln(C, st2, xn1c.v(), [xn1c.buf], 1024, 1e-5, junk, hc.v(), [hc.buf])
        P.op("vector", lambda v: v.tensor_tensor(
            out=hc.v(), in0=hc.v(), in1=ln1g.v(), op=ALU.mult),
            reads=[hc.buf, ln1g.buf], writes=[hc.buf], excl=False)
        P.op("vector", lambda v: v.tensor_tensor(
            out=hc.v(), in0=hc.v(), in1=ln1b.v(), op=ALU.add),
            reads=[hc.buf, ln1b.buf], writes=[hc.buf], excl=False)
        P.op("vector", lambda v: v.tensor_reduce(
            out=st2.v(cs=slice(16, 17)), in_=hc.v(), axis=AX.X, op=ALU.max,
            apply_absolute_value=True), reads=[hc.buf], writes=[st2.buf], excl=False)
        P.op("vector", lambda v: v.tensor_single_scalar(
            out=st2.v(cs=slice(17, 18)), in_=st2.v(cs=slice(16, 17)),
            scalar=1e-5, op=ALU.max), reads=[st2.buf], writes=[st2.buf], excl=False)
        P.op("vector", lambda v: v.reciprocal(
            out=st2.v(cs=slice(18, 19)), in_=st2.v(cs=slice(17, 18))),
            reads=[st2.buf], writes=[st2.buf], excl=False)
        P.op("vector", lambda v: v.tensor_single_scalar(
            out=st2.v(cs=slice(19, 20)), in_=st2.v(cs=slice(18, 19)),
            scalar=127.0, op=ALU.mult), reads=[st2.buf], writes=[st2.buf], excl=False)
        P.op("vector", lambda v: v.tensor_single_scalar(
            out=st2.v(cs=slice(20, 21)), in_=st2.v(cs=slice(17, 18)),
            scalar=1.0 / 127.0, op=ALU.mult), reads=[st2.buf], writes=[st2.buf], excl=False)
        P.op("vector", lambda v: v.tensor_tensor(
            out=st2.v(cs=slice(21, 22)), in0=st2.v(cs=slice(20, 21)),
            in1=sw1b.v(cs=slice(1, 2)), op=ALU.mult),
            reads=[st2.buf, sw1b.buf], writes=[st2.buf], excl=False)
        P.op("vector", lambda v: v.tensor_scalar(
            out=hc.v(), in0=hc.v(), scalar1=st2.v(cs=slice(19, 20)),
            scalar2=float(RND_C), op0=ALU.mult, op1=ALU.add),
            reads=[hc.buf, st2.buf], writes=[hc.buf], excl=False)
        P.op("vector", lambda v: v.tensor_single_scalar(
            out=xqb.v(), in_=hc.v(), scalar=float(RND_C), op=ALU.subtract),
            reads=[hc.buf], writes=[xqb.buf])
        for c in range(8):
            P.op("tensor", lambda t, c=c: t.transpose(
                psT.v(cs=slice(c * 128, c * 128 + 128)),
                xqb.v(cs=slice(c * 128, c * 128 + 128)), C.ident.v()),
                reads=[xqb.buf, C.ident.buf], writes=[psT.buf], excl=(c == 0))
        P.op("scalar", lambda s, xqf=xqf: s.copy(out=xqf.v(), in_=psT.v()),
             reads=[psT.buf], writes=[xqf.buf])

    frontend(0)
    for tc in range(nt):
        xqf = xqfd[tc % 2]
        P.op("vector", lambda v: v.memset(gacc.v(cs=slice(0, 8)), 0.0),
             writes=[gacc.buf])
        for nb in range(8):
            bank = nb % 4
            for f in range(8):
                P.op("tensor", lambda t, nb=nb, f=f, bank=bank, xqf=xqf: t.matmul(
                    psS.v(cs=slice(bank * 512, bank * 512 + 512)),
                    xqf.v(cs=slice(f * 128, f * 128 + 128)),
                    w1q[f].v(cs=slice(nb * 512, nb * 512 + 512)),
                    start=(f == 0), stop=(f == 7)),
                    reads=[xqf.buf, w1q[f].buf], writes=[C.psSb[bank]], excl=(f == 0))
            P.op("vector", lambda v, nb=nb, bank=bank: v.scalar_tensor_tensor(
                out=h1.v(cs=slice(nb * 512, nb * 512 + 512)),
                in0=psS.v(cs=slice(bank * 512, bank * 512 + 512)),
                scalar=st2.v(cs=slice(21, 22)),
                in1=b1f.v(cs=slice(nb * 512, nb * 512 + 512)),
                op0=ALU.mult, op1=ALU.add),
                reads=[C.psSb[bank], st2.buf, b1f.buf], writes=[h1.buf],
                excl=(nb == 0))
            P.op("scalar", lambda s, nb=nb: s.activation(
                out=h1.v(cs=slice(nb * 512, nb * 512 + 512)),
                in_=h1.v(cs=slice(nb * 512, nb * 512 + 512)),
                func=AF.Gelu, accum_out=gacc.v(cs=slice(nb, nb + 1))),
                reads=[h1.buf], writes=[h1.buf, gacc.buf], excl=False)
        if tc + 1 < nt:
            frontend(tc + 1)
        P.op("vector", lambda v: v.reduce_sum(
            out=st.v(cs=slice(0, 1)), in_=gacc.v(cs=slice(0, 8)), axis=AX.X),
            reads=[gacc.buf], writes=[st.buf], excl=False)
        P.op("scalar", lambda s: s.activation(
            out=junk.v(), in_=h1.v(), func=AF.Square, accum_out=st.v(cs=slice(1, 2))),
            reads=[h1.buf], writes=[st.buf, junk.buf], excl=False)
        ln_stats(C, st, 0, 1, 4096, 1e-5, 2, 3)
        P.op("vector", lambda v: v.tensor_scalar(
            out=h1.v(), in0=h1.v(), scalar1=st.v(cs=slice(2, 3)),
            scalar2=st.v(cs=slice(3, 4)), op0=ALU.subtract, op1=ALU.mult),
            reads=[h1.buf, st.buf], writes=[h1.buf], excl=False)
        P.op("vector", lambda v: v.tensor_tensor(
            out=h1.v(), in0=h1.v(), in1=ln2g.v(), op=ALU.mult),
            reads=[h1.buf, ln2g.buf], writes=[h1.buf], excl=False)
        P.op("vector", lambda v: v.tensor_tensor(
            out=h1.v(), in0=h1.v(), in1=ln2b.v(), op=ALU.add),
            reads=[h1.buf, ln2b.buf], writes=[h1.buf], excl=False)
        P.op("vector", lambda v: v.tensor_reduce(
            out=st.v(cs=slice(16, 17)), in_=h1.v(), axis=AX.X, op=ALU.max,
            apply_absolute_value=True), reads=[h1.buf], writes=[st.buf], excl=False)
        P.op("vector", lambda v: v.tensor_single_scalar(
            out=st.v(cs=slice(17, 18)), in_=st.v(cs=slice(16, 17)),
            scalar=1e-5, op=ALU.max), reads=[st.buf], writes=[st.buf], excl=False)
        P.op("vector", lambda v: v.reciprocal(
            out=st.v(cs=slice(18, 19)), in_=st.v(cs=slice(17, 18))),
            reads=[st.buf], writes=[st.buf], excl=False)
        P.op("vector", lambda v: v.tensor_single_scalar(
            out=st.v(cs=slice(19, 20)), in_=st.v(cs=slice(18, 19)),
            scalar=127.0, op=ALU.mult), reads=[st.buf], writes=[st.buf], excl=False)
        P.op("vector", lambda v, tc=tc: v.tensor_single_scalar(
            out=scales2.v(cs=slice(tc, tc + 1)), in_=st.v(cs=slice(17, 18)),
            scalar=1.0 / 127.0, op=ALU.mult),
            reads=[st.buf], writes=[scales2.buf], excl=False)
        P.op("vector", lambda v: v.tensor_scalar(
            out=h1.v(), in0=h1.v(), scalar1=st.v(cs=slice(19, 20)),
            scalar2=float(RND_C), op0=ALU.mult, op1=ALU.add),
            reads=[h1.buf, st.buf], writes=[h1.buf], excl=False)
        P.op("vector", lambda v: v.tensor_single_scalar(
            out=xq2b.v(), in_=h1.v(), scalar=float(RND_C), op=ALU.subtract),
            reads=[h1.buf], writes=[xq2b.buf])
        P.dma("sync", xq2d.ap()[tc], xq2b.v(), reads=[xq2b.buf],
              writes=[_db(C, "XQ2")], excl=False)
    ba.free_tv(w1q, b1f, ln2g, ln2b, xqb, xqfd, ln1g, ln1b)
    fa.free_tv(sw1b, x1c, xn1c, hc, h1, gacc, st2)

    w2q, sw2b = emit_wquant(C, "w2fT", 4096, 1024)
    b2f = load_const(C, "b2f_rb", 1024, "fa")
    obr = load_const(C, "ob_rb", 1024, "fa")
    combo = fa.alloc(1024, "l1combo")
    P.op("vector", lambda v: v.tensor_tensor(
        out=combo.v(), in0=b2f.v(), in1=obr.v(), op=ALU.add),
        reads=[b2f.buf, obr.buf], writes=[combo.buf])
    fa.free_tv(b2f, obr)
    x1c = fa.alloc(1024, "l2x")
    t1 = fa.alloc(1024, "l2t1")
    sc2c = fa.alloc(16, "l2sc")
    xq2bd = [xq2b, ba.alloc(4096, "l2xq2b")]
    xq2fd2 = [ba.alloc(4096, "l2xq2f") for _ in range(2)]
    outd = C.dram["out"]

    def frontend2(tc):
        xb, xf = xq2bd[tc % 2], xq2fd2[tc % 2]
        P.dma("gpsimd", xb.v(), xq2d.ap()[tc], reads=[_db(C, "XQ2")],
              writes=[xb.buf])
        for b4 in range(4):
            for c in range(8):
                P.op("tensor", lambda t, b4=b4, c=c, xb=xb: t.transpose(
                    psT.v(cs=slice(c * 128, c * 128 + 128)),
                    xb.v(cs=slice(b4 * 1024 + c * 128, b4 * 1024 + c * 128 + 128)),
                    C.ident.v()),
                    reads=[xb.buf, C.ident.buf], writes=[psT.buf], excl=(c == 0))
            P.op("scalar", lambda s, b4=b4, xf=xf: s.copy(
                out=xf.v(cs=slice(b4 * 1024, b4 * 1024 + 1024)), in_=psT.v()),
                reads=[psT.buf], writes=[xf.buf], excl=(b4 == 0))

    frontend2(0)
    for tc in range(nt):
        xq2f = xq2fd2[tc % 2]
        P.op("vector", lambda v, tc=tc: v.tensor_tensor(
            out=sc2c.v(cs=slice(0, 1)), in0=scales2.v(cs=slice(tc, tc + 1)),
            in1=sw2b.v(cs=slice(1, 2)), op=ALU.mult),
            reads=[scales2.buf, sw2b.buf], writes=[sc2c.buf])
        for half in range(2):
            for k2 in range(32):
                P.op("tensor", lambda t, half=half, k2=k2, xq2f=xq2f: t.matmul(
                    psA.v(cs=slice(half * 512, half * 512 + 512)),
                    xq2f.v(cs=slice(k2 * 128, k2 * 128 + 128)),
                    w2q[k2].v(cs=slice(half * 512, half * 512 + 512)),
                    start=(k2 == 0), stop=(k2 == 31)),
                    reads=[xq2f.buf, w2q[k2].buf], writes=[psA.buf],
                    excl=(half == 0 and k2 == 0))
        P.dma("sync", x1c.v(), C.dram["X1"].ap()[tc * 128:(tc + 1) * 128, :],
              reads=[_db(C, "X1")], writes=[x1c.buf])
        if tc + 1 < nt:
            frontend2(tc + 1)
        P.op("vector", lambda v: v.scalar_tensor_tensor(
            out=t1.v(), in0=psA.v(), scalar=sc2c.v(cs=slice(0, 1)),
            in1=combo.v(), op0=ALU.mult, op1=ALU.add),
            reads=[psA.buf, sc2c.buf, combo.buf], writes=[t1.buf])
        P.op("vector", lambda v: v.tensor_tensor(
            out=t1.v(), in0=t1.v(), in1=x1c.v(), op=ALU.add),
            reads=[t1.buf, x1c.buf], writes=[t1.buf], excl=False)
        if tc == 0:
            P.op("vector", lambda v: v.tensor_tensor(
                out=t1.v(rs=slice(0, 16)), in0=t1.v(rs=slice(0, 16)),
                in1=C.o16.v(rs=slice(0, 16)), op=ALU.add),
                reads=[t1.buf, C.o16.buf], writes=[t1.buf], excl=False)
        if tc == 16:
            P.op("vector", lambda v: v.tensor_tensor(
                out=t1.v(rs=slice(0, 16)), in0=t1.v(rs=slice(0, 16)),
                in1=C.o16b.v(rs=slice(0, 16)), op=ALU.add),
                reads=[t1.buf, C.o16b.buf], writes=[t1.buf], excl=False)
        P.dma("sync", outd.ap()[tc * 128:(tc + 1) * 128, :], t1.v(),
              reads=[t1.buf], writes=[_db(C, "out")], excl=False)
    ba.free_tv(w2q, xq2bd, xq2fd2, junk)
    fa.free_tv(combo, t1, x1c, st, scales2, sw2b, sc2c)


def build_program():
    nc = bass.Bass("TRN2", target_bir_lowering=False, detect_race_conditions=False)
    C = Ctx()
    C.nc = nc
    C.dram = {}
    C.dbuf = {}
    _mk_dram(C)
    C.P = Prog(nc)
    C.ba = Arena(nc, "AB", 84992, BF16)
    C.fa = Arena(nc, "FA", 10240, F32)
    C.psS = TV(nc.alloc_psum_tensor("psS", [128, 2048], F32), 0, 2048, "psS")
    C.psSb = [Buf(f"psS{i}", serial=True) for i in range(4)]
    C.psS.buf.serial = True
    C.psA = TV(nc.alloc_psum_tensor("psA", [128, 1024], F32), 0, 1024, "psA")
    C.psB = TV(nc.alloc_psum_tensor("psB", [128, 512], F32), 0, 512, "psB")
    C.psT = TV(nc.alloc_psum_tensor("psT", [128, 1024], BF16), 0, 1024, "psT")
    for b_ in (C.psA.buf, C.psB.buf, C.psT.buf):
        b_.serial = True
    C.ident = load_const(C, "ident", 128)
    C.identf = load_const(C, "identf", 128, "fa")
    C.rrot = load_const(C, "Rrot", 128)
    C.onesk = load_const(C, "ones_k", 1)
    C.onesmf = load_const(C, "ones_mf", 128, "fa", rows=1)
    C.mLT = load_const(C, "maskLT", 128, "fa")
    C.mUT = load_const(C, "maskUT", 128, "fa")
    C.mPD = load_const(C, "maskPD", 128, "fa")
    C.mPF = load_const(C, "maskPF", 128, "fa")
    C.gkv = load_const(C, "g_kv_rb", 256, "fa")
    C.w1_cached = [False, False]

    ph = [0]

    def phase():
        ph[0] += 1
        return ph[0] <= MAX_PHASE

    if phase():
        emit_ln0(C, "x", RMAIN, "XN")
    if phase():
        emit_ln0(C, "px", RPREF, "PXN")

    def main_kmap(qi):
        return [(j, (C.mLT if j == qi else None)) for j in range(qi + 1)]

    def main_ffkmap(qi):
        lo = max(0, qi - 4)
        out = []
        for j in range(lo, qi + 1):
            m = None
            if qi >= 4 and j == qi - 4:
                m = C.mUT
            if j == qi:
                m = C.mLT
            out.append((j, m))
        return out

    def pref_kmap(qi):
        return [(qi, C.mPD)]

    seqs = []
    for s in range(SEQ_PER_CORE):
        seqs.append(SeqCtx(
            name=f"s{s}", Tn=T, tc0=s * (T // 128), prefix=False,
            xn_name="XN", ao_name="AO", ao_row0=s * T,
            ropeC_name="ropeCos", ropeS_name="ropeSin",
            kmap=main_kmap, ff_kmap=main_ffkmap,
            y_name="Y", y_idx=s, shift_blocks=None,
            x_src="x", x_row0=s * T, x1_name="X1"))
    seqs.append(SeqCtx(
        name="pref", Tn=RPREF, tc0=0, prefix=True,
        xn_name="PXN", ao_name="PAO", ao_row0=0,
        ropeC_name="ropeCosP", ropeS_name="ropeSinP",
        kmap=pref_kmap, ff_kmap=pref_kmap,
        y_name="PY", y_idx=0, shift_blocks=PT_,
        x_src="px", x_row0=0, x1_name="PX1"))

    for sq in seqs:
        if phase():
            sq.ropeC = load_const(C, sq.ropeC_name, sq.Tn)
            sq.ropeS = load_const(C, sq.ropeS_name, sq.Tn)
            emit_l0_attn(C, sq)
            C.ba.free_tv(sq.ropeC, sq.ropeS)
        for l in range(2):
            if phase():
                emit_l0_ffn(C, sq, l)
        if phase():
            emit_x1(C, sq)

    if phase():
        C.fa.free_tv(C.gkv, C.mLT, C.mUT, C.mPD)
        emit_prefix_l1(C)
        C.fa.free_tv(C.mPF)
    if phase():
        emit_l1_main(C)
    C.P.wait_all("sync")
    C.P.finalize()
    print(f"[build] n_inst={nc.n_instructions()} ba_peak={C.ba.peak} "
          f"fa_peak={C.fa.peak} selfwaits={C.P.n_selfwaits} sems={C.P.count}")
    return nc, C.input_names
